# version 04
# baseline (speedup 1.0000x reference)
import numpy as np
from contextlib import ExitStack

import concourse.bass as bass
import concourse.mybir as mybir

F32 = mybir.dt.float32
BF16 = mybir.dt.bfloat16
AF = mybir.ActivationFunctionType
ALU = mybir.AluOpType
AX = mybir.AxisListType.X

P = 128
D = 2048
NCH = 16
NH = 16
HD = 128
B = 2
S = 8192
TOK = B * S
TBW = 512
NTB = TOK // TBW
OWN = 2048
NOB = 4
NJT = 64
EPS = 1e-5
SCALE = float(HD) ** -0.5
NBH = B * NH


class Ct:
    def __init__(self, stack, nc, name):
        self.h = stack.enter_context(nc.semaphore(name))
        self.n = 0

    def inc(self, instr, k=1):
        instr.then_inc(self.h, k)
        self.n += k
        return self.n

    def pre(self, k=1):
        self.n += k
        return self.n


def build(nphase=5, debug=False):
    nc = bass.Bass("TRN2", target_bir_lowering=False)

    xt_own = nc.dram_tensor("xt_own", [D, OWN], F32, kind="ExternalInput").ap()
    wq_d = nc.dram_tensor("wq", [D, D], F32, kind="ExternalInput").ap()
    wk_d = nc.dram_tensor("wk", [D, HD], F32, kind="ExternalInput").ap()
    wv_d = nc.dram_tensor("wv", [D, HD], F32, kind="ExternalInput").ap()
    brow_d = nc.dram_tensor("brow", [1, D + 2 * HD], F32, kind="ExternalInput").ap()
    wproj_d = nc.dram_tensor("wproj", [D, D], F32, kind="ExternalInput").ap()
    bprow_d = nc.dram_tensor("bprow", [1, D], F32, kind="ExternalInput").ap()
    wffn_d = nc.dram_tensor("wffn", [D, D], F32, kind="ExternalInput").ap()
    wa_d = nc.dram_tensor("wa", [D, 128], F32, kind="ExternalInput").ap()
    wb_d = nc.dram_tensor("wb", [128, D], F32, kind="ExternalInput").ap()
    wr_d = nc.dram_tensor("wr", [D, 8], F32, kind="ExternalInput").ap()
    dlt_d = nc.dram_tensor("dlt", [P, 8], F32, kind="ExternalInput").ap()
    e8_d = nc.dram_tensor("e8", [8, 128], F32, kind="ExternalInput").ap()
    idn_d = nc.dram_tensor("idn", [P, P], F32, kind="ExternalInput").ap()

    out_d = nc.dram_tensor("out", [P, NCH, OWN], F32, kind="ExternalOutput").ap()
    qt_scr = nc.dram_tensor("qt_scr", [NH * P, OWN], BF16, kind="Internal").ap()
    kag_in = nc.dram_tensor("kag_in", [P, OWN], BF16, kind="Internal").ap()
    vag_in = nc.dram_tensor("vag_in", [OWN, P], BF16, kind="Internal").ap()
    kag_out = nc.dram_tensor("kag_out", [8 * P, OWN], BF16, kind="Internal").ap()
    vag_out = nc.dram_tensor("vag_out", [8 * OWN, P], BF16, kind="Internal").ap()
    x1_scr = nc.dram_tensor("x1_scr", [P, NCH, OWN], F32, kind="Internal").ap()

    dbg = {}
    if debug:
        dbg["kt"] = nc.dram_tensor("dbg_kt", [P, TOK], BF16, kind="ExternalOutput").ap()
        dbg["va"] = nc.dram_tensor("dbg_va", [P, B * NJT * 129], BF16, kind="ExternalOutput").ap()
        dbg["rows"] = nc.dram_tensor("dbg_rows", [2, TOK], BF16, kind="ExternalOutput").ap()
        if nphase >= 2:
            dbg["qt"] = nc.dram_tensor("dbg_qt", [NH * P, OWN], BF16, kind="ExternalOutput").ap()
            dbg["orows"] = nc.dram_tensor("dbg_orows", [2, OWN], BF16, kind="ExternalOutput").ap()
        if nphase >= 3:
            dbg["oall"] = nc.dram_tensor("dbg_oall", [P, NH, OWN], BF16, kind="ExternalOutput").ap()
        if nphase >= 4:
            dbg["x1"] = nc.dram_tensor("dbg_x1", [P, NCH, OWN], F32, kind="ExternalOutput").ap()

    st = ExitStack()
    with st:
        ec = st.enter_context

        def sbl(name, shape, dt):
            return ec(nc.sbuf_tensor(name, shape, dt, side="left"))

        ones_red = sbl("ones_red", [P, P], BF16)
        onesrow = sbl("onesrow", [1, TBW], BF16)
        negcol = sbl("negcol", [P, 1], BF16)
        oneB = sbl("oneB", [1, P], BF16)
        idn_f = sbl("idn_f", [P, P], F32)
        idn_bf = sbl("idn_bf", [P, P], BF16)
        dD = sbl("dD", [P, P], F32)
        dlt_sb = sbl("dlt_sb", [P, 8], F32)
        tmask = sbl("tmask", [P, 8, P], BF16)
        robf = sbl("robf", [1, OWN], BF16)
        mrobf = sbl("mrobf", [1, OWN], BF16)
        brow_sb = sbl("brow_sb", [1, D + 2 * HD], BF16)
        mu_t = sbl("mu_t", [1, 2, TBW], F32)
        msq_t = sbl("msq_t", [1, 2, TBW], F32)
        rf_t = sbl("rf_t", [1, 2, TBW], F32)
        scrap = sbl("scrap", [1, 8], F32)
        epscol = sbl("epscol", [P, 1], F32)

        sB = Ct(st, nc, "sB")
        sC = Ct(st, nc, "sC")
        sCs = Ct(st, nc, "sCs")
        sDBG = Ct(st, nc, "sDBG")
        sXB = [Ct(st, nc, f"sXB{i}") for i in range(2)]
        sXO = Ct(st, nc, "sXO")
        sWQ = [Ct(st, nc, f"sWQ{i}") for i in range(2)]
        sQO = [Ct(st, nc, f"sQO{i}") for i in range(2)]
        sQI = [Ct(st, nc, f"sQI{i}") for i in range(2)]
        sWP = [Ct(st, nc, f"sWP{i}") for i in range(2)]
        sWF = [Ct(st, nc, f"sWF{i}") for i in range(2)]
        sXOF = [Ct(st, nc, f"sXOF{i}") for i in range(2)]
        sAGB = Ct(st, nc, "sAGB")
        sCC = Ct(st, nc, "sCC")
        sKV = Ct(st, nc, "sKV")
        sX1O = [Ct(st, nc, f"sX1O{i}") for i in range(2)]
        sX1I = Ct(st, nc, "sX1I")
        sX2O = [Ct(st, nc, f"sX2O{i}") for i in range(2)]
        pe1 = Ct(st, nc, "pe1")
        pe2 = Ct(st, nc, "pe2")
        act1 = Ct(st, nc, "act1")
        act2 = Ct(st, nc, "act2")
        dve1 = Ct(st, nc, "dve1")
        dve2 = Ct(st, nc, "dve2")
        dveE = Ct(st, nc, "dveE")
        allct = ([sC, sCs, sDBG, sXO, sAGB, sCC, sKV, sX1I] + sXOF + sX2O +
                 sWQ + sQO + sQI +
                 sWP + sWF + sX1O + [pe1, pe2, act1, act2, dve1, dve2, dveE])

        def ed(e, ct, instr):
            ct.inc(instr)
            e.wait_ge(ct.h, ct.n)

        def dma(e, ct, out, in_):
            v = ct.inc(e.dma_start(out, in_), 16)
            return (ct, v)

        def wt(e, evt):
            e.wait_ge(evt[0].h, evt[1])

        block = ec(nc.Block())
        T = block.tensor
        V = block.vector
        A = block.scalar
        G = block.gpsimd
        SY = block.sync

        def entry(e):
            e.wait_ge(sB.h, sB.n)

        def barrier():
            vals = [(c, c.n) for c in allct]

            def bt(e):
                for c, v in vals:
                    e.wait_ge(c.h, v)
                sB.inc(e.ldweights(ones_red[:, 0:1]))

            def bv(e):
                for c, v in vals:
                    e.wait_ge(c.h, v)
                sB.inc(e.tensor_copy(scrap[0:1, 0:2], scrap[0:1, 2:4]))

            def ba(e):
                for c, v in vals:
                    e.wait_ge(c.h, v)
                sB.inc(e.activation(scrap[0:1, 4:6], scrap[0:1, 6:8], AF.Copy,
                                    bias=0.0, scale=1.0))
            T(bt)
            V(bv)
            A(ba)

        ev = {}

        def stats_math(ps, slot, rrow, murrow, ev_st, prev_mur, evout, W=TBW):
            def m_a1(e):
                e.wait_ge(pe1.h, ev_st)
                if prev_mur is not None:
                    e.wait_ge(dve2.h, prev_mur)
                e.activation(mu_t[0:1, slot, 0:W], ps[0:1, 0:W],
                             AF.Copy, bias=0.0, scale=1.0 / D)
                cp = e.activation(msq_t[0:1, slot, 0:W], ps[0:1, W:2 * W],
                                  AF.Copy, bias=0.0, scale=1.0 / D)
                evout["mu"] = act1.inc(cp)

            def m_v1(e):
                e.wait_ge(act1.h, evout["mu"])
                ed(e, dve2, e.tensor_mul(rf_t[0:1, slot, 0:W],
                                         mu_t[0:1, slot, 0:W],
                                         mu_t[0:1, slot, 0:W]))
                tt = e.tensor_sub(rf_t[0:1, slot, 0:W], msq_t[0:1, slot, 0:W],
                                  rf_t[0:1, slot, 0:W])
                evout["var"] = dve2.inc(tt)

            def m_a2(e):
                e.wait_ge(dve2.h, evout["var"])
                ed(e, act2, e.activation(rf_t[0:1, slot, 0:W],
                                         rf_t[0:1, slot, 0:W],
                                         AF.Ln, bias=epscol[0:1, 0:1], scale=1.0))
                ed(e, act2, e.activation(rf_t[0:1, slot, 0:W],
                                         rf_t[0:1, slot, 0:W],
                                         AF.Exp, bias=0.0, scale=-0.5))
                cp = e.activation(rrow, rf_t[0:1, slot, 0:W],
                                  AF.Copy, bias=0.0, scale=1.0)
                evout["r"] = act1.inc(cp)

            def m_v2(e):
                e.wait_ge(act1.h, evout["r"])
                tt = e.tensor_mul(murrow, mu_t[0:1, slot, 0:W],
                                  rf_t[0:1, slot, 0:W])
                evout["mur"] = dve2.inc(tt)
            A(m_a1)
            V(m_v1)
            A(m_a2)
            V(m_v2)

        def s_sy(e):
            dma(e, sCs, dlt_sb[:], dlt_d)
            dma(e, sCs, idn_f[:], idn_d)

        def s_g(e):
            dma(e, sC, brow_sb[:], brow_d)
            dma(e, sC, idn_bf[:], idn_d)
            dve2.inc(e.iota(dD[:], [[1, P]], channel_multiplier=-1,
                            allow_small_or_imprecise_dtypes=True))

        def s_v(e):
            e.memset(ones_red[:], 1.0)
            e.memset(onesrow[:], 1.0)
            e.memset(negcol[:], -1.0)
            e.memset(oneB[:], 1.0)
            e.memset(scrap[:], 1.0)
            e.memset(epscol[:], EPS)
            e.wait_ge(dve2.h, dve2.n)
            e.wait_ge(sCs.h, sCs.n)
            for r in range(8):
                dve1.inc(e.tensor_scalar(tmask[:, r, :], dD[:],
                                         dlt_sb[:, r:r + 1], 0.0,
                                         ALU.add, ALU.is_ge))
        SY(s_sy)
        G(s_g)
        V(s_v)
        barrier()

        es_o = ExitStack()
        if True:
            with ExitStack() as _es:
                kt_all = _es.enter_context(nc.sbuf_tensor("kt_all", [P, TOK], BF16, side="right"))
                v_all = _es.enter_context(nc.sbuf_tensor("v_all", [P, B * NJT, 129], BF16, side="right"))
                if nphase >= 2:
                    with ExitStack() as _es:
                        xo0 = _es.enter_context(nc.sbuf_tensor("xo0", [P, NCH, TBW], BF16, side="right"))
                        xo1 = _es.enter_context(nc.sbuf_tensor("xo1", [P, NCH, TBW], BF16, side="right"))
                        xo2 = _es.enter_context(nc.sbuf_tensor("xo2", [P, NCH, TBW], BF16, side="right"))
                        xo3 = _es.enter_context(nc.sbuf_tensor("xo3", [P, NCH, TBW], BF16, side="right"))
                        xsq = _es.enter_context(nc.sbuf_tensor("xsq", [P, NCH, TBW], BF16, side="right"))
                        wqc0 = _es.enter_context(nc.sbuf_tensor("wqc0", [P, NCH, P], BF16, side="right"))
                        wqc1 = _es.enter_context(nc.sbuf_tensor("wqc1", [P, NCH, P], BF16, side="right"))
                        rrown = _es.enter_context(nc.sbuf_tensor("rrown", [P, NOB, TBW], BF16, side="right"))
                        wnq = _es.enter_context(nc.sbuf_tensor("wnq", [1, P], BF16, side="right"))
                        qstage = _es.enter_context(nc.sbuf_tensor("qstage", [P, 2, TBW], BF16, side="right"))
                        wk_bf = _es.enter_context(nc.sbuf_tensor("wk_bf", [P, NCH, HD], BF16, side="right"))
                        wv_bf = _es.enter_context(nc.sbuf_tensor("wv_bf", [P, NCH, HD], BF16, side="right"))
                        wnkv = _es.enter_context(nc.sbuf_tensor("wnkv", [1, 2 * P], BF16, side="right"))
                        kstage = _es.enter_context(nc.sbuf_tensor("kstage", [P, NOB, TBW], BF16, side="right"))
                        vstage = _es.enter_context(nc.sbuf_tensor("vstage", [P, NCH, P], BF16, side="right"))
                        vtmp1 = _es.enter_context(nc.sbuf_tensor("vtmp1", [P, TBW], BF16, side="right"))
                        ps_q = _es.enter_context(nc.psum_tensor("ps_q", [P, 2 * TBW], F32))
                        ps_st1 = _es.enter_context(nc.psum_tensor("ps_st1", [1, 2 * TBW], F32))
                        ps_rr1 = _es.enter_context(nc.psum_tensor("ps_rr1", [P, TBW], F32))
                        ps_kv1 = _es.enter_context(nc.psum_tensor("ps_kv1", [P, 2 * TBW], F32))
                        ps_tr1 = _es.enter_context(nc.psum_tensor("ps_tr1", [P, P], BF16))
                        ps_w = ps_rr1
                        xol = [xo0, xo1, xo2, xo3]
                        wqc = [wqc0, wqc1]

                        def p1a_g(e):
                            entry(e)
                            for ob in range(NOB):
                                t0 = ob * TBW
                                dma(e, sXO, xol[ob][:],
                                    xt_own[0:D, t0:t0 + TBW].rearrange(
                                        "(c p) t -> p c t", p=P))
                            for ob in range(NOB):
                                ev[("xo", ob)] = (sXO, sXO.n)
                            dma(e, sC, wk_bf[:],
                                wk_d.rearrange("(c p) o -> p c o", p=P))
                            ev["wkv"] = dma(e, sC, wv_bf[:],
                                            wv_d.rearrange("(c p) o -> p c o", p=P))
                            ev[("wq", 0)] = dma(
                                e, sWQ[0], wqc[0][:],
                                wq_d[0:D, 0:P].rearrange("(c p) o -> p c o", p=P))
                        G(p1a_g)

                        ev[("wn_ev", 0)] = act2.pre()
                        ev[("wn_ev", 1)] = act2.pre()

                        def p01_wn_t(e):
                            entry(e)
                            wt(e, ev["wkv"])
                            for half in range(2):
                                if half == 1:
                                    e.wait_ge(act2.h, ev[("wn_ev", 0)])
                                w = wk_bf if half == 0 else wv_bf
                                for c in range(NCH):
                                    mm = e.matmul(ps_w[0:1, 0:P], negcol[:],
                                                  w[:, c, :], start=(c == 0),
                                                  stop=(c == NCH - 1))
                                ev[("wn", half)] = pe1.inc(mm)

                        def p01_wn_a(e):
                            entry(e)
                            for half in range(2):
                                e.wait_ge(pe1.h, ev[("wn", half)])
                                cp = e.activation(wnkv[0:1, half * P:(half + 1) * P],
                                                  ps_w[0:1, 0:P], AF.Copy,
                                                  bias=0.0, scale=1.0)
                                cp.then_inc(act2.h, 1)

                        def p01_ones_v(e):
                            entry(e)
                            dve2.inc(e.memset(v_all[:, :, 128:129], 1.0))
                        T(p01_wn_t)
                        A(p01_wn_a)
                        V(p01_ones_v)

                        for ob in range(NOB):
                            slot = ob % 2

                            def p1a_v1(e, ob=ob):
                                wt(e, ev[("xo", ob)])
                                if ob - 1 >= 0:
                                    e.wait_ge(pe1.h, ev[("ost", ob - 1)])
                                for c in range(NCH):
                                    tt = e.tensor_mul(xsq[:, c, :],
                                                      xol[ob][:, c, :],
                                                      xol[ob][:, c, :])
                                ev[("osq", ob)] = dve1.inc(tt)

                            def p1a_t1(e, ob=ob):
                                wt(e, ev[("xo", ob)])
                                if ob - 1 >= 0:
                                    e.wait_ge(act1.h,
                                              ev[("omath", ob - 1)]["mu"])
                                for c in range(NCH):
                                    e.matmul(ps_st1[0:1, 0:TBW], ones_red[:, 0:1],
                                             xol[ob][:, c, :], start=(c == 0),
                                             stop=(c == NCH - 1))
                                e.wait_ge(dve1.h, ev[("osq", ob)])
                                for c in range(NCH):
                                    mm = e.matmul(ps_st1[0:1, TBW:2 * TBW],
                                                  ones_red[:, 0:1], xsq[:, c, :],
                                                  start=(c == 0),
                                                  stop=(c == NCH - 1))
                                ev[("ost", ob)] = pe1.inc(mm)

                            V(p1a_v1)
                            T(p1a_t1)

                            ev[("omath", ob)] = {}
                            pm = ev[("omath", ob - 2)]["mur"] if ob >= 2 else None
                            t0ob = ob * TBW
                            stats_math(ps_st1, slot, robf[0:1, t0ob:t0ob + TBW],
                                       mrobf[0:1, t0ob:t0ob + TBW],
                                       ev[("ost", ob)], pm, ev[("omath", ob)])

                            def p1a_t2(e, ob=ob):
                                e.wait_ge(act1.h, ev[("omath", ob)]["r"])
                                if ob - 1 >= 0:
                                    e.wait_ge(act2.h, ev[("rrevo", ob - 1)])
                                else:
                                    e.wait_ge(act2.h, ev[("wn_ev", 1)])
                                t0 = ob * TBW
                                mm = e.matmul(ps_rr1[:, 0:TBW], oneB[0:1, :],
                                              robf[0:1, t0:t0 + TBW], start=True,
                                              stop=True)
                                ev[("rrmmo", ob)] = pe1.inc(mm)

                            def p1a_a2(e, ob=ob):
                                e.wait_ge(pe1.h, ev[("rrmmo", ob)])
                                cp = e.activation(rrown[:, ob, :], ps_rr1[:, 0:TBW],
                                                  AF.Copy, bias=0.0, scale=1.0)
                                ev[("rrevo", ob)] = act2.inc(cp)

                            T(p1a_t2)
                            A(p1a_a2)

                        for ob in range(NOB):
                            def p01_kv_t(e, ob=ob):
                                e.wait_ge(act2.h, ev[("rrevo", ob)])
                                if ob > 0:
                                    e.wait_ge(dve1.h, ev[("kvfix", ob - 1)])
                                t0 = ob * TBW
                                for half in range(2):
                                    w = wk_bf if half == 0 else wv_bf
                                    lo = half * TBW
                                    for c in range(NCH):
                                        e.matmul(ps_kv1[:, lo:lo + TBW],
                                                 w[:, c, :], xol[ob][:, c, :],
                                                 start=(c == 0), stop=False)
                                    boff = D + half * HD
                                    e.matmul(ps_kv1[:, lo:lo + TBW],
                                             brow_sb[0:1, boff:boff + P],
                                             onesrow[:], start=False, stop=False)
                                    mm = e.matmul(
                                        ps_kv1[:, lo:lo + TBW],
                                        wnkv[0:1, half * P:(half + 1) * P],
                                        mrobf[0:1, t0:t0 + TBW],
                                        start=False, stop=True)
                                ev[("kvmm", ob)] = pe2.inc(mm)

                            def p01_kv_v(e, ob=ob):
                                e.wait_ge(pe2.h, ev[("kvmm", ob)])
                                if ob > 0:
                                    e.wait_ge(pe2.h, ev[("vtr", ob - 1, 3)])
                                e.tensor_mul(kstage[:, ob, :],
                                             ps_kv1[:, 0:TBW], rrown[:, ob, :])
                                tt = e.tensor_mul(vtmp1[:],
                                                  ps_kv1[:, TBW:2 * TBW],
                                                  rrown[:, ob, :])
                                ev[("kvfix", ob)] = dve1.inc(tt)

                            for i in range(4):
                                ev[("vtrev", ob, i)] = act2.pre()

                            def p01_tr_t(e, ob=ob):
                                e.wait_ge(dve1.h, ev[("kvfix", ob)])
                                for i in range(4):
                                    prev = (ob, i - 1) if i > 0 else (ob - 1, 3)
                                    if ("vtrev",) + prev in ev:
                                        e.wait_ge(act2.h, ev[("vtrev",) + prev])
                                    tr = e.transpose(ps_tr1[:],
                                                     vtmp1[:, i * P:(i + 1) * P],
                                                     idn_bf[:])
                                    ev[("vtr", ob, i)] = pe2.inc(tr)

                            def p01_tr_a(e, ob=ob):
                                for i in range(4):
                                    e.wait_ge(pe2.h, ev[("vtr", ob, i)])
                                    cp = e.activation(
                                        vstage[:, ob * 4 + i, :], ps_tr1[:],
                                        AF.Copy, bias=0.0, scale=1.0)
                                    cp.then_inc(act2.h, 1)

                            T(p01_kv_t)
                            V(p01_kv_v)
                            T(p01_tr_t)
                            A(p01_tr_a)

                        def p01_ag_g(e):
                            e.wait_ge(dve1.h, ev[("kvfix", NOB - 1)])
                            dma(e, sAGB, kag_in,
                                kstage[:].rearrange("p a b -> p (a b)"))
                            e.wait_ge(act2.h, ev[("vtrev", NOB - 1, 3)])
                            dma(e, sAGB, vag_in.rearrange("(t p) c -> p t c", p=P),
                                vstage[:])
                            e.wait_ge(sAGB.h, sAGB.n)
                            cc = e.collective_compute(
                                "AllGather", mybir.AluOpType.bypass,
                                replica_groups=[list(range(8))],
                                ins=[kag_in.opt()], outs=[kag_out.opt()])
                            sCC.inc(cc)
                            cc = e.collective_compute(
                                "AllGather", mybir.AluOpType.bypass,
                                replica_groups=[list(range(8))],
                                ins=[vag_in.opt()], outs=[vag_out.opt()])
                            sCC.inc(cc)
                        G(p01_ag_g)

                        def p01_sc_sy(e):
                            e.wait_ge(sCC.h, sCC.n)
                            ktv = kt_all[:].rearrange(
                                "p (b t2 rr c) -> p b t2 rr c", b=B, t2=8, rr=8,
                                c=P)
                            vav = v_all[:].rearrange(
                                "p (b t2 rr) c -> p b t2 rr c", b=B, t2=8, rr=8)
                            for r in range(8):
                                src = kag_out[r * P:(r + 1) * P, :].rearrange(
                                    "p (b t c) -> p b t c", b=B, t=8, c=TBW // 4)
                                dma(e, sKV, ktv[:, :, :, r, :], src)
                                vsrc = vag_out[r * OWN:(r + 1) * OWN, :].rearrange(
                                    "(b t k) c -> k b t c", b=B, t=8, k=P)
                                dma(e, sKV, vav[:, :, :, r, 0:P], vsrc)
                        SY(p01_sc_sy)

                        for h in range(NH):
                            parh = h % 2

                            def p1b_g1(e, h=h):
                                if h + 1 < NH:
                                    if h - 1 >= 0:
                                        e.wait_ge(pe2.h,
                                                  ev[("qmm", h - 1, NOB - 1)])
                                    o0 = (h + 1) * P
                                    ev[("wq", h + 1)] = dma(
                                        e, sWQ[(h + 1) % 2], wqc[(h + 1) % 2][:],
                                        wq_d[0:D, o0:o0 + P].rearrange(
                                            "(c p) o -> p c o", p=P))

                            def p1b_t1(e, h=h, parh=parh):
                                wt(e, ev[("wq", h)])
                                if h == 0:
                                    e.wait_ge(dve2.h,
                                              ev[("omath", NOB - 1)]["mur"])
                                    e.wait_ge(act2.h, ev[("rrevo", NOB - 1)])
                                    e.wait_ge(pe2.h, ev[("kvmm", NOB - 1)])
                                if h > 0:
                                    e.wait_ge(act2.h, ev[("wnqev", h - 1)])
                                for c in range(NCH):
                                    mm = e.matmul(ps_w[0:1, 0:P], negcol[:],
                                                  wqc[parh][:, c, :],
                                                  start=(c == 0),
                                                  stop=(c == NCH - 1))
                                ev[("wnq", h)] = pe1.inc(mm)

                            def p1b_a1(e, h=h):
                                e.wait_ge(pe1.h, ev[("wnq", h)])
                                cp = e.activation(wnq[0:1, :], ps_w[0:1, 0:P],
                                                  AF.Copy, bias=0.0, scale=1.0)
                                ev[("wnqev", h)] = act2.inc(cp)

                            def p1b_t2(e, h=h, parh=parh):
                                e.wait_ge(act2.h, ev[("wnqev", h)])
                                for ob in range(NOB):
                                    lo = (ob % 2) * TBW
                                    pob = (h, ob - 1) if ob > 0 else (h - 1, NOB - 1)
                                    if ("qev",) + pob in ev:
                                        e.wait_ge(dve1.h, ev[("qev",) + pob])
                                    for c in range(NCH):
                                        e.matmul(ps_q[:, lo:lo + TBW],
                                                 wqc[parh][:, c, :],
                                                 xol[ob][:, c, :],
                                                 start=(c == 0), stop=False)
                                    boff = h * P
                                    e.matmul(ps_q[:, lo:lo + TBW],
                                             brow_sb[0:1, boff:boff + P],
                                             onesrow[:], start=False, stop=False)
                                    t0 = ob * TBW
                                    mm = e.matmul(ps_q[:, lo:lo + TBW], wnq[0:1, :],
                                                  mrobf[0:1, t0:t0 + TBW],
                                                  start=False, stop=True)
                                    ev[("qmm", h, ob)] = pe2.inc(mm)

                            def p1b_v1(e, h=h):
                                for ob in range(NOB):
                                    e.wait_ge(pe2.h, ev[("qmm", h, ob)])
                                    e.wait_ge(act2.h, ev[("rrevo", ob)])
                                    qp = ob % 2
                                    pob = (h, ob - 2) if ob >= 2 else (h - 1, ob + 2)
                                    if ("qdma",) + pob in ev:
                                        wt(e, ev[("qdma",) + pob])
                                    lo = qp * TBW
                                    tt = e.tensor_mul(qstage[:, qp, :],
                                                      ps_q[:, lo:lo + TBW],
                                                      rrown[:, ob, :])
                                    tt.then_inc(dve1.h, 1)

                            def p1b_sy(e, h=h):
                                for ob in range(NOB):
                                    e.wait_ge(dve1.h, ev[("qev", h, ob)])
                                    qp = ob % 2
                                    t0 = ob * TBW
                                    e.dma_start(
                                        qt_scr[h * P:(h + 1) * P, t0:t0 + TBW],
                                        qstage[:, qp, :]).then_inc(sQO[qp].h, 16)

                            G(p1b_g1)
                            T(p1b_t1)
                            A(p1b_a1)
                            for ob in range(NOB):
                                ev[("qev", h, ob)] = dve1.pre()
                                ct = sQO[ob % 2]
                                ev[("qdma", h, ob)] = (ct, ct.pre(16))
                            T(p1b_t2)
                            V(p1b_v1)
                            SY(p1b_sy)

                        if debug:
                            def p1_dbg(e):
                                for c in allct:
                                    e.wait_ge(c.h, c.n)
                                dma(e, sDBG, dbg["kt"], kt_all[:])
                                dma(e, sDBG, dbg["va"],
                                    v_all[:].rearrange("p a b -> p (a b)"))
                                dma(e, sDBG, dbg["qt"], qt_scr)
                                dma(e, sDBG, dbg["orows"][0:1, :], robf[:])
                                dma(e, sDBG, dbg["orows"][1:2, :], mrobf[:])
                            SY(p1_dbg)
                        barrier()

                if nphase >= 3:
                    o_all = es_o.enter_context(
                        nc.sbuf_tensor("o_all", [P, NH, OWN], BF16, side="left"))
                    with ExitStack() as _es:
                        qbuf = _es.enter_context(nc.sbuf_tensor("qbuf", [P, 2, 1024], BF16, side="right"))
                        pbuf = _es.enter_context(nc.sbuf_tensor("pbuf", [P, 4, 512], BF16, side="right"))
                        rl_sb = _es.enter_context(nc.sbuf_tensor("rl_sb", [P, 1024], F32, side="right"))
                        ps_sA0 = _es.enter_context(nc.psum_tensor("ps_sA0", [P, 512], F32))
                        ps_sA1 = _es.enter_context(nc.psum_tensor("ps_sA1", [P, 512], F32))
                        ps_sB0 = _es.enter_context(nc.psum_tensor("ps_sB0", [P, 512], F32))
                        ps_sB1 = _es.enter_context(nc.psum_tensor("ps_sB1", [P, 512], F32))
                        ps_o = _es.enter_context(nc.psum_tensor("ps_o", [P, 1024], F32))
                        ps_l = _es.enter_context(nc.psum_tensor("ps_l", [P, 1024], F32))
                        ps_sc = [[ps_sA0, ps_sA1], [ps_sB0, ps_sB1]]
                        peS0, peL0 = pe1.n, pe2.n
                        actX0, dveT0, dveE0 = act1.n, dve1.n, dveE.n
                        sQI0 = [sQI[0].n, sQI[1].n]
                        units = []
                        for j in range(NJT):
                            units.append((1, j))
                            if j < 32:
                                units.append((0, j))
                        nU = len(units)
                        cseq = {}
                        ctxu = [[], []]
                        for u, (cx, j) in enumerate(units):
                            cseq[u] = len(ctxu[cx])
                            ctxu[cx].append(u)

                        def owner(j):
                            return 0 if j < 32 else 1

                        def loc0(cx, j):
                            m = j // 8
                            return 128 * m if cx == 0 else 128 * max(0, m - 4)

                        evS = lambda g: peS0 + g + 1
                        evL = lambda g: peL0 + g + 1
                        evX = lambda g: actX0 + g + 1
                        evT = lambda bh, j: dveT0 + bh * NJT + j + 1
                        evE = lambda bh: dveE0 + bh + 1
                        evQ = lambda bh: (sQI[bh % 2],
                                          sQI0[bh % 2] + 16 * (bh // 2 + 1))

                        def p2_sy(e):
                            entry(e)
                            for bh in range(NBH):
                                b, h = bh // NH, bh % NH
                                if bh >= 2:
                                    e.wait_ge(pe1.h, evS((bh - 1) * nU - 1))
                                t0 = b * 1024
                                evt = dma(e, sQI[bh % 2], qbuf[:, bh % 2, :],
                                          qt_scr[h * P:(h + 1) * P, t0:t0 + 1024])
                                assert evt == evQ(bh)
                        SY(p2_sy)

                        for bh in range(NBH):
                            b, h = bh // NH, bh % NH
                            g0 = bh * nU

                            def p2_t(e, bh=bh, b=b, h=h, g0=g0):
                                if bh == 0:
                                    entry(e)
                                qb = qbuf[:, bh % 2, :]

                                def scores(u):
                                    cx, j = units[u]
                                    c = cseq[u]
                                    l0 = loc0(cx, j)
                                    if u == 0:
                                        wt(e, evQ(bh))
                                    if c >= 2:
                                        e.wait_ge(act1.h,
                                                  evX(g0 + ctxu[cx][c - 2]))
                                    elif bh > 0:
                                        pu = ctxu[cx][len(ctxu[cx]) - 2 + c]
                                        e.wait_ge(act1.h, evX(g0 - nU + pu))
                                    k0 = b * S + j * P
                                    mm = e.matmul(ps_sc[cx][c % 2][:, l0:512],
                                                  kt_all[:, k0:k0 + P],
                                                  qb[:, 512 * cx + l0:
                                                     512 * cx + 512],
                                                  start=True, stop=True)
                                    assert pe1.inc(mm) == evS(g0 + u)

                                def pvl(u):
                                    cx, j = units[u]
                                    c = cseq[u]
                                    l0 = loc0(cx, j)
                                    gq0 = 512 * cx + l0
                                    if owner(j) == cx:
                                        e.wait_ge(dve1.h, evT(bh, j))
                                    else:
                                        e.wait_ge(act1.h, evX(g0 + u))
                                    if j == 0 and bh > 0:
                                        e.wait_ge(dveE.h, evE(bh - 1))
                                    pb = pbuf[:, u % 4, :]
                                    stp = (j == 31) if cx == 0 else (j == NJT - 1)
                                    e.matmul(ps_o[:, gq0:512 * cx + 512],
                                             v_all[:, b * NJT + j, 0:P],
                                             pb[:, l0:512], start=(j == 0),
                                             stop=stp)
                                    mm = e.matmul(ps_l[:, gq0:512 * cx + 512],
                                                  ones_red[:], pb[:, l0:512],
                                                  start=(j == 0), stop=stp)
                                    assert pe2.inc(mm) == evL(g0 + u)

                                scores(0)
                                scores(1)
                                scores(2)
                                for u in range(nU):
                                    pvl(u)
                                    if u + 3 < nU:
                                        scores(u + 3)

                            def p2_a(e, bh=bh, g0=g0):
                                if bh == 0:
                                    entry(e)
                                for u in range(nU):
                                    cx, j = units[u]
                                    c = cseq[u]
                                    l0 = loc0(cx, j)
                                    e.wait_ge(pe1.h, evS(g0 + u))
                                    if g0 + u >= 4:
                                        e.wait_ge(pe2.h, evL(g0 + u - 4))
                                    x = e.activation(pbuf[:, u % 4, l0:512],
                                                     ps_sc[cx][c % 2][:, l0:512],
                                                     AF.Exp, bias=0.0, scale=SCALE)
                                    assert act1.inc(x) == evX(g0 + u)

                            def p2_v(e, bh=bh, b=b, h=h, g0=g0):
                                if bh == 0:
                                    entry(e)
                                for j in range(NJT):
                                    cx = owner(j)
                                    u = ctxu[cx][cseq_of_owner[j]]
                                    l0 = loc0(cx, j)
                                    e.wait_ge(act1.h, evX(g0 + u))
                                    tt = e.tensor_mul(
                                        pbuf[:, u % 4, l0:l0 + P],
                                        pbuf[:, u % 4, l0:l0 + P],
                                        tmask[:, j % 8, :])
                                    assert dve1.inc(tt) == evT(bh, j)
                                e.wait_ge(pe2.h, evL(g0 + nU - 1))
                                dve2.inc(e.reciprocal(rl_sb[:], ps_l[:]))
                                e.wait_ge(dve2.h, dve2.n)
                                t0 = b * 1024
                                e.tensor_mul(o_all[:, h, t0:t0 + 512],
                                             ps_o[:, 0:512], rl_sb[:, 0:512])
                                tt = e.tensor_mul(o_all[:, h, t0 + 512:t0 + 1024],
                                                  ps_o[:, 512:1024],
                                                  rl_sb[:, 512:1024])
                                assert dveE.inc(tt) == evE(bh)

                            cseq_of_owner = {}
                            for j in range(NJT):
                                cx = owner(j)
                                for uu in ctxu[cx]:
                                    if units[uu][1] == j:
                                        cseq_of_owner[j] = cseq[uu]
                                        break

                            T(p2_t)
                            A(p2_a)
                            V(p2_v)

                        if debug:
                            def p2_dbg(e):
                                for c in allct:
                                    e.wait_ge(c.h, c.n)
                                dma(e, sDBG, dbg["oall"],
                                    o_all[:].rearrange("p a b -> p a b"))
                            SY(p2_dbg)
                        barrier()
            if nphase >= 4:
                with ExitStack() as _es:
                    z_ob = _es.enter_context(nc.sbuf_tensor("z_ob", [P, 2, NCH, TBW], BF16, side="right"))
                    bprow_sb = _es.enter_context(nc.sbuf_tensor("bprow_sb", [1, D], BF16, side="right"))
                    zsq = _es.enter_context(nc.sbuf_tensor("zsq", [P, NCH, TBW], BF16, side="right"))
                    xof = _es.enter_context(nc.sbuf_tensor("xof", [P, 2, NCH, TBW], BF16, side="right"))
                    x1st = _es.enter_context(nc.sbuf_tensor("x1st", [P, 2, TBW], F32, side="right"))
                    wpc0 = _es.enter_context(nc.sbuf_tensor("wpc0", [P, NCH, P], BF16, side="right"))
                    wpc1 = _es.enter_context(nc.sbuf_tensor("wpc1", [P, NCH, P], BF16, side="right"))
                    rr3 = _es.enter_context(nc.sbuf_tensor("rr3", [P, 2, TBW], BF16, side="right"))
                    rm3 = _es.enter_context(nc.sbuf_tensor("rm3", [P, 2, TBW], BF16, side="right"))
                    r3row = _es.enter_context(nc.sbuf_tensor("r3row", [1, 2, TBW], BF16, side="right"))
                    m3row = _es.enter_context(nc.sbuf_tensor("m3row", [1, 2, TBW], BF16, side="right"))
                    ttmp = _es.enter_context(nc.sbuf_tensor("ttmp", [P, TBW], F32, side="right"))
                    ps_z = _es.enter_context(nc.psum_tensor("ps_z", [P, 2 * TBW], F32))
                    ps_st3 = _es.enter_context(nc.psum_tensor("ps_st3", [1, 2 * TBW], F32))
                    ps_u3 = _es.enter_context(nc.psum_tensor("ps_u3", [P, TBW], F32))
                    wpc = [wpc0, wpc1]

                    def p3_head(ob):
                        par = ob % 2

                        def p3_sy1(e, ob=ob, par=par):
                            pass

                        def p3_g(e, ob=ob, par=par):
                            if ob == 0:
                                entry(e)
                                ev["bprow"] = dma(e, sC, bprow_sb[:], bprow_d)
                            if ob >= 2:
                                e.wait_ge(dve2.h, ev[("res3", ob - 2)])
                            t0 = ob * TBW
                            ev[("xof", ob)] = dma(
                                e, sXOF[par], xof[:, par, :, :],
                                xt_own[0:D, t0:t0 + TBW].rearrange(
                                    "(c p) t -> p c t", p=P))
                            for oc in range(NCH):
                                pc = (ob, oc - 2) if oc >= 2 else (ob - 1, oc + 14)
                                if ("zmm",) + pc in ev:
                                    e.wait_ge(pe1.h, ev[("zmm",) + pc])
                                o0 = oc * P
                                ev[("wp", ob, oc)] = dma(
                                    e, sWP[oc % 2], wpc[oc % 2][:],
                                    wproj_d[0:D, o0:o0 + P].rearrange(
                                        "(c p) o -> p c o", p=P))

                        def p3_t1(e, ob=ob, par=par):
                            if ob == 0:
                                entry(e)
                                wt(e, ev["bprow"])
                            for oc in range(NCH):
                                wt(e, ev[("wp", ob, oc)])
                                pc = (ob, oc - 2) if oc >= 2 else (ob - 1, oc + 14)
                                if ("zev",) + pc in ev:
                                    e.wait_ge(act1.h, ev[("zev",) + pc])
                                lo = (oc % 2) * TBW
                                for c in range(NCH):
                                    e.matmul(ps_z[:, lo:lo + TBW],
                                             wpc[oc % 2][:, c, :],
                                             o_all[:, c, ob * TBW:(ob + 1) * TBW],
                                             start=(c == 0), stop=False)
                                o0 = oc * P
                                mm = e.matmul(ps_z[:, lo:lo + TBW],
                                              bprow_sb[0:1, o0:o0 + P], onesrow[:],
                                              start=False, stop=True)
                                mm.then_inc(pe1.h, 1)

                        def p3_a1(e, ob=ob, par=par):
                            if ob == 0:
                                entry(e)
                            for oc in range(NCH):
                                e.wait_ge(pe1.h, ev[("zmm", ob, oc)])
                                if oc == 0 and ob >= 2:
                                    e.wait_ge(dve2.h, ev[("res3", ob - 2)])
                                    e.wait_ge(pe1.h, ev[("st3", ob - 2)])
                                lo = (oc % 2) * TBW
                                cp = e.activation(z_ob[:, par, oc, :],
                                                  ps_z[:, lo:lo + TBW], AF.Copy,
                                                  bias=0.0, scale=1.0)
                                cp.then_inc(act1.h, 1)

                        for oc in range(NCH):
                            ev[("zmm", ob, oc)] = pe1.pre()
                            ev[("zev", ob, oc)] = act1.pre()
                        for c in range(NCH):
                            ct = sX1O[c % 2]
                            ev[("x1dmac", ob, c)] = (ct, ct.pre(16))
                        G(p3_g)
                        T(p3_t1)
                        A(p3_a1)

                    def p3_tail(ob):
                        par = ob % 2

                        def p3_v1(e, ob=ob, par=par):
                            e.wait_ge(act1.h, ev[("zev", ob, NCH - 1)])
                            if ob > 0:
                                e.wait_ge(pe1.h, ev[("st3", ob - 1)])
                            for c in range(NCH):
                                tt = e.tensor_mul(zsq[:, c, :],
                                                  z_ob[:, par, c, :],
                                                  z_ob[:, par, c, :])
                            ev[("zsq", ob)] = dve1.inc(tt)

                        def p3_t2(e, ob=ob, par=par):
                            e.wait_ge(act1.h, ev[("zev", ob, NCH - 1)])
                            if ob > 0:
                                e.wait_ge(act1.h, ev[("math3", ob - 1)]["mu"])
                            for c in range(NCH):
                                e.matmul(ps_st3[0:1, 0:TBW], ones_red[:, 0:1],
                                         z_ob[:, par, c, :], start=(c == 0),
                                         stop=(c == NCH - 1))
                            e.wait_ge(dve1.h, ev[("zsq", ob)])
                            for c in range(NCH):
                                mm = e.matmul(ps_st3[0:1, TBW:2 * TBW],
                                              ones_red[:, 0:1], zsq[:, c, :],
                                              start=(c == 0), stop=(c == NCH - 1))
                            ev[("st3", ob)] = pe1.inc(mm)
                        V(p3_v1)
                        T(p3_t2)

                        ev[("math3", ob)] = {}
                        pm = ev[("math3", ob - 1)]["mur"] if ob >= 1 else None
                        stats_math(ps_st3, par, r3row[0:1, par, :],
                                   m3row[0:1, par, :], ev[("st3", ob)], pm,
                                   ev[("math3", ob)])
                        md = ev[("math3", ob)]

                        def p3_t3(e, ob=ob, par=par, md=md):
                            e.wait_ge(act1.h, md["r"])
                            if ob >= 2:
                                e.wait_ge(dve2.h, ev[("res3", ob - 2)])
                            mm = e.matmul(ps_u3[:, 0:TBW], oneB[0:1, :],
                                          r3row[0:1, par, :], start=True,
                                          stop=True)
                            ev[("rr3mm", ob)] = pe1.inc(mm)

                        def p3_a4(e, ob=ob, par=par):
                            e.wait_ge(pe1.h, ev[("rr3mm", ob)])
                            cp = e.activation(rr3[:, par, :], ps_u3[:, 0:TBW],
                                              AF.Copy, bias=0.0, scale=1.0)
                            ev[("rr3ev", ob)] = act2.inc(cp)

                        def p3_t4(e, ob=ob, par=par, md=md):
                            e.wait_ge(dve2.h, md["mur"])
                            e.wait_ge(act2.h, ev[("rr3ev", ob)])
                            mm = e.matmul(ps_u3[:, 0:TBW], oneB[0:1, :],
                                          m3row[0:1, par, :], start=True,
                                          stop=True)
                            ev[("rm3mm", ob)] = pe1.inc(mm)

                        def p3_a5(e, ob=ob, par=par):
                            e.wait_ge(pe1.h, ev[("rm3mm", ob)])
                            cp = e.activation(rm3[:, par, :], ps_u3[:, 0:TBW],
                                              AF.Copy, bias=0.0, scale=1.0)
                            ev[("rm3ev", ob)] = act2.inc(cp)

                        def p3_v4(e, ob=ob, par=par):
                            e.wait_ge(act2.h, ev[("rm3ev", ob)])
                            wt(e, ev[("xof", ob)])
                            for c in range(NCH):
                                pc = (ob, c - 2) if c >= 2 else (ob - 1, c + 14)
                                if ("x1dmac",) + pc in ev:
                                    wt(e, ev[("x1dmac",) + pc])
                                ed(e, dve2, e.tensor_mul(ttmp[:],
                                                         z_ob[:, par, c, :],
                                                         rr3[:, par, :]))
                                ed(e, dve2, e.scalar_tensor_tensor(
                                    ttmp[:], rm3[:, par, :], -1.0, ttmp[:],
                                    ALU.mult, ALU.add))
                                tt = e.tensor_add(x1st[:, c % 2, :], ttmp[:],
                                                  xof[:, par, c, :])
                                ev[("res3c", ob, c)] = dve2.inc(tt)
                                e.wait_ge(dve2.h, dve2.n)
                            ev[("res3", ob)] = ev[("res3c", ob, NCH - 1)]

                        def p3_sy2(e, ob=ob):
                            t0 = ob * TBW
                            for c in range(NCH):
                                e.wait_ge(dve2.h, ev[("res3c", ob, c)])
                                e.dma_start(
                                    x1_scr[:, c, t0:t0 + TBW],
                                    x1st[:, c % 2, :]).then_inc(sX1O[c % 2].h, 16)
                            ev[("x1dma", ob)] = ev[("x1dmac", ob, NCH - 1)]
                        T(p3_t3)
                        A(p3_a4)
                        T(p3_t4)
                        A(p3_a5)
                        V(p3_v4)
                        SY(p3_sy2)

                    for ob in range(NOB):
                        p3_head(ob)
                        if ob >= 1:
                            p3_tail(ob - 1)
                    p3_tail(NOB - 1)

                    if debug:
                        def p3_dbg(e):
                            for c in allct:
                                e.wait_ge(c.h, c.n)
                            dma(e, sDBG, dbg["x1"], x1_scr)
                        SY(p3_dbg)
                    barrier()
            if nphase >= 3:
                es_o.close()

            if nphase >= 5:
                with ExitStack() as _es:
                    x1b = _es.enter_context(nc.sbuf_tensor("x1b", [P, 2, NCH, TBW], BF16, side="right"))
                    sq4 = _es.enter_context(nc.sbuf_tensor("sq4", [P, NCH, TBW], BF16, side="right"))
                    sq4b = _es.enter_context(nc.sbuf_tensor("sq4b", [P, NCH, TBW], BF16, side="right"))
                    ln2 = _es.enter_context(nc.sbuf_tensor("ln2", [P, 2, NCH, TBW], BF16, side="right"))
                    z2 = _es.enter_context(nc.sbuf_tensor("z2", [P, 2, NCH, TBW], BF16, side="right"))
                    x2st = _es.enter_context(nc.sbuf_tensor("x2st", [P, 2, TBW], F32, side="right"))
                    wfc0 = _es.enter_context(nc.sbuf_tensor("wfc0", [P, NCH, P], BF16, side="right"))
                    wfc1 = _es.enter_context(nc.sbuf_tensor("wfc1", [P, NCH, P], BF16, side="right"))
                    wa_bf = _es.enter_context(nc.sbuf_tensor("wa_bf", [P, NCH, P], BF16, side="right"))
                    wb_bf = _es.enter_context(nc.sbuf_tensor("wb_bf", [P, D], BF16, side="right"))
                    wr_bf = _es.enter_context(nc.sbuf_tensor("wr_bf", [P, NCH, 8], BF16, side="right"))
                    e8_bf = _es.enter_context(nc.sbuf_tensor("e8_bf", [8, P], BF16, side="right"))
                    idn_f4 = _es.enter_context(nc.sbuf_tensor("idn_f4", [P, P], F32, side="right"))
                    ax_sb = _es.enter_context(nc.sbuf_tensor("ax_sb", [P, TBW], BF16, side="right"))
                    sc_sb = _es.enter_context(nc.sbuf_tensor("sc_sb", [P, TBW], BF16, side="right"))
                    we8 = _es.enter_context(nc.sbuf_tensor("we8", [P, 4 * P + 64], F32, side="right"))
                    w8t = _es.enter_context(nc.sbuf_tensor("w8t", [8, TBW], BF16, side="right"))
                    tcol = _es.enter_context(nc.sbuf_tensor("tcol", [P, 8], F32, side="right"))
                    lg_sb = _es.enter_context(nc.sbuf_tensor("lg_sb", [P, 32], F32, side="right"))
                    rr4 = _es.enter_context(nc.sbuf_tensor("rr4", [P, 2, TBW], BF16, side="right"))
                    rm4 = _es.enter_context(nc.sbuf_tensor("rm4", [P, 2, TBW], BF16, side="right"))
                    r4row = _es.enter_context(nc.sbuf_tensor("r4row", [1, 2, TBW], BF16, side="right"))
                    m4row = _es.enter_context(nc.sbuf_tensor("m4row", [1, 2, TBW], BF16, side="right"))
                    tt4 = _es.enter_context(nc.sbuf_tensor("tt4", [P, TBW], F32, side="right"))
                    ps_z4 = _es.enter_context(nc.psum_tensor("ps_z4", [P, 2 * TBW], F32))
                    ps_u4 = _es.enter_context(nc.psum_tensor("ps_u4", [P, 2 * TBW], F32))
                    ps_lg = _es.enter_context(nc.psum_tensor("ps_lg", [P, 32], F32))
                    ps_w4 = _es.enter_context(nc.psum_tensor("ps_w4", [P, TBW], F32))
                    ps_st4 = _es.enter_context(nc.psum_tensor("ps_st4", [1, 2 * TBW], F32))
                    wfc = [wfc0, wfc1]

                    def p4_g0(e):
                        entry(e)
                        dma(e, sC, wa_bf[:], wa_d.rearrange("(c p) o -> p c o", p=P))
                        dma(e, sC, wb_bf[:], wb_d)
                        dma(e, sC, wr_bf[:], wr_d.rearrange("(c p) o -> p c o", p=P))
                        dma(e, sC, e8_bf[:], e8_d)
                        for k in ("wa", "wb", "wr", "e8"):
                            ev[k] = (sC, sC.n)

                    def p4_sy0(e):
                        entry(e)
                        ev["idn4"] = dma(e, sCs, idn_f4[:], idn_d)

                    def p4_v_init(e):
                        entry(e)
                        dve2.inc(e.memset(we8[:], 0.0))
                    G(p4_g0)
                    SY(p4_sy0)
                    V(p4_v_init)

                    def p4_headmid(ob):
                        par = ob % 2

                        def p4_g1(e, ob=ob, par=par):
                            if ob >= 2:
                                e.wait_ge(dve2.h, ev[("res4", ob - 2)])
                                e.wait_ge(pe1.h, ev[("st4", ob - 2)])
                            t0 = ob * TBW
                            ev[("x1l", ob)] = dma(e, sX1I, x1b[:, par, :, :],
                                                  x1_scr[:, :, t0:t0 + TBW])
                            if ob == 0:
                                wt(e, ev["wr"])

                        def p4_v0(e, ob=ob, par=par):
                            if ob == 0:
                                entry(e)
                            wt(e, ev[("x1l", ob)])
                            if ob > 0:
                                e.wait_ge(pe1.h, ev[("st4", ob - 1)])
                            for c in range(NCH):
                                tt = e.tensor_mul(sq4[:, c, :], x1b[:, par, c, :],
                                                  x1b[:, par, c, :])
                            ev[("x1sq", ob)] = dve1.inc(tt)

                        def p4_t0(e, ob=ob, par=par):
                            if ob == 0:
                                entry(e)
                            wt(e, ev[("x1l", ob)])
                            if ob == 1:
                                e.wait_ge(act1.h, ev[("math4", 0)]["mu"])
                            elif ob >= 2:
                                e.wait_ge(act1.h, ev[("math4b", ob - 2)]["mu"])
                            for c in range(NCH):
                                e.matmul(ps_st4[0:1, 0:TBW], ones_red[:, 0:1],
                                         x1b[:, par, c, :], start=(c == 0),
                                         stop=(c == NCH - 1))
                            e.wait_ge(dve1.h, ev[("x1sq", ob)])
                            for c in range(NCH):
                                mm = e.matmul(ps_st4[0:1, TBW:2 * TBW],
                                              ones_red[:, 0:1], sq4[:, c, :],
                                              start=(c == 0), stop=(c == NCH - 1))
                            ev[("st4", ob)] = pe1.inc(mm)

                        G(p4_g1)
                        V(p4_v0)
                        T(p4_t0)

                        ev[("math4", ob)] = {}
                        pm = (ev[("math4b", ob - 2)]["mur"] if ob >= 2
                              else (ev[("math4", 0)]["mur"] if ob == 1 else None))
                        stats_math(ps_st4, par, r4row[0:1, par, :],
                                   m4row[0:1, par, :], ev[("st4", ob)], pm,
                                   ev[("math4", ob)])
                        md4 = ev[("math4", ob)]

                        def p4_t1(e, ob=ob, par=par, md=md4):
                            e.wait_ge(act1.h, md["r"])
                            if ob > 0:
                                e.wait_ge(act1.h, ev[("w8tev", ob - 1, 3)])
                            if ob >= 2:
                                e.wait_ge(dve2.h, ev[("res4", ob - 2)])
                            mm = e.matmul(ps_w4[:, 0:TBW], oneB[0:1, :],
                                          r4row[0:1, par, :], start=True,
                                          stop=True)
                            ev[("rr4mm", ob)] = pe1.inc(mm)

                        def p4_a3(e, ob=ob, par=par):
                            e.wait_ge(pe1.h, ev[("rr4mm", ob)])
                            cp = e.activation(rr4[:, par, :], ps_w4[:, 0:TBW],
                                              AF.Copy, bias=0.0, scale=1.0)
                            ev[("rr4ev", ob)] = act2.inc(cp)

                        def p4_t2(e, ob=ob, par=par, md=md4):
                            e.wait_ge(dve2.h, md["mur"])
                            e.wait_ge(act2.h, ev[("rr4ev", ob)])
                            mm = e.matmul(ps_w4[:, 0:TBW], oneB[0:1, :],
                                          m4row[0:1, par, :], start=True,
                                          stop=True)
                            ev[("rm4mm", ob)] = pe1.inc(mm)

                        def p4_a4(e, ob=ob, par=par):
                            e.wait_ge(pe1.h, ev[("rm4mm", ob)])
                            cp = e.activation(rm4[:, par, :], ps_w4[:, 0:TBW],
                                              AF.Copy, bias=0.0, scale=1.0)
                            ev[("rm4ev", ob)] = act2.inc(cp)

                        def p4_v3(e, ob=ob, par=par):
                            e.wait_ge(act2.h, ev[("rm4ev", ob)])
                            if ob >= 2:
                                e.wait_ge(pe1.h, ev[("z4mm", ob - 2, NCH - 1)])
                            for c in range(NCH):
                                ed(e, dve2, e.tensor_mul(ln2[:, par, c, :],
                                                         x1b[:, par, c, :],
                                                         rr4[:, par, :]))
                                tt = e.scalar_tensor_tensor(
                                    ln2[:, par, c, :], rm4[:, par, :], -1.0,
                                    ln2[:, par, c, :], ALU.mult, ALU.add)
                                if c < NCH - 1:
                                    ed(e, dve2, tt)
                            ev[("ln2", ob)] = dve1.inc(tt)
                        T(p4_t1)
                        A(p4_a3)
                        T(p4_t2)
                        A(p4_a4)
                        V(p4_v3)

                        for si in range(4):
                            ev[("lgev", ob, si)] = act2.pre()

                        def p4_t3(e, ob=ob, par=par):
                            e.wait_ge(dve1.h, ev[("ln2", ob)])
                            if ob == 0:
                                wt(e, ev["wa"])
                            else:
                                e.wait_ge(dve1.h, ev[("sc", ob - 1)])
                                e.wait_ge(dve2.h, ev[("top2", ob - 1, 3)])
                            for c in range(NCH):
                                mm = e.matmul(ps_u4[:, 0:TBW], wa_bf[:, c, :],
                                              ln2[:, par, c, :], start=(c == 0),
                                              stop=(c == NCH - 1))
                            ev[("axmm", ob)] = pe1.inc(mm)
                            for si in range(4):
                                if si > 0:
                                    e.wait_ge(act2.h, ev[("lgev", ob, si - 1)])
                                elif ob > 0:
                                    e.wait_ge(act2.h, ev[("lgev", ob - 1, 3)])
                                for c in range(NCH):
                                    mm = e.matmul(ps_lg[:, si * 8:(si + 1) * 8],
                                                  ln2[:, par, c,
                                                      si * P:(si + 1) * P],
                                                  wr_bf[:, c, :], start=(c == 0),
                                                  stop=(c == NCH - 1))
                                ev[("lg", ob, si)] = pe1.inc(mm)

                        def p4_a5b(e, ob=ob):
                            for si in range(4):
                                e.wait_ge(pe1.h, ev[("lg", ob, si)])
                                if si == 0 and ob > 0:
                                    e.wait_ge(dve2.h, ev[("lgdone", ob - 1, 3)])
                                cp = e.activation(lg_sb[:, si * 8:(si + 1) * 8],
                                                  ps_lg[:, si * 8:(si + 1) * 8],
                                                  AF.Copy, bias=0.0, scale=1.0)
                                cp.then_inc(act2.h, 1)

                        def p4_a5(e, ob=ob):
                            e.wait_ge(pe1.h, ev[("axmm", ob)])
                            if ob > 0:
                                e.wait_ge(dve1.h, ev[("sc", ob - 1)])
                            cp = e.activation(ax_sb[:], ps_u4[:, 0:TBW], AF.Copy,
                                              bias=0.0, scale=1.0)
                            ev[("axev", ob)] = act1.inc(cp)
                        T(p4_t3)
                        A(p4_a5b)
                        A(p4_a5)

                        for si in range(4):
                            ev[("eexp", ob, si)] = act2.pre()

                        def p4_v4(e, ob=ob):
                            for si in range(4):
                                e.wait_ge(act2.h, ev[("lgev", ob, si)])
                                lg = lg_sb[:, si * 8:(si + 1) * 8]
                                m1 = tcol[:, 0:1]
                                m2 = tcol[:, 1:2]
                                dm = tcol[:, 2:3]
                                w1 = tcol[:, 3:4]
                                w2 = tcol[:, 4:5]
                                eecol = tcol[:, 5:6]
                                eq1 = we8[:, 512:520]
                                lgm = we8[:, 520:528]
                                eq2 = we8[:, 528:536]
                                ed(e, dve2, e.reduce_max(m1, lg, axis=AX))
                                ed(e, dve2, e.tensor_scalar(eq1, lg, m1, None,
                                                            ALU.is_equal))
                                ed(e, dve2, e.scalar_tensor_tensor(
                                    lgm, eq1, -1e30, lg, ALU.mult, ALU.add))
                                ed(e, dve2, e.reduce_max(m2, lgm, axis=AX))
                                ed(e, dve2, e.tensor_scalar(eq2, lgm, m2, None,
                                                            ALU.is_equal))
                                tt = e.tensor_sub(dm, m2, m1)
                                ev[("dm", ob, si)] = dve2.inc(tt)
                                ev[("lgdone", ob, si)] = ev[("dm", ob, si)]
                                e.wait_ge(act2.h, ev[("eexp", ob, si)])
                                ed(e, dve2, e.tensor_scalar(w1, eecol, 1.0, None,
                                                            ALU.add))
                                ed(e, dve2, e.reciprocal(w1, w1))
                                ed(e, dve2, e.tensor_scalar(w1, w1, 2.0, None,
                                                            ALU.mult))
                                ed(e, dve2, e.tensor_scalar(w2, w1, -1.0, 2.0,
                                                            ALU.mult, ALU.add))
                                ed(e, dve2, e.tensor_scalar(
                                    we8[:, si * P:si * P + 8], eq1, w1, None,
                                    ALU.mult))
                                tt = e.scalar_tensor_tensor(
                                    we8[:, si * P:si * P + 8], eq2, w2,
                                    we8[:, si * P:si * P + 8], ALU.mult, ALU.add)
                                ev[("top2", ob, si)] = dve2.inc(tt)

                        def p4_a6(e, ob=ob):
                            for si in range(4):
                                e.wait_ge(dve2.h, ev[("dm", ob, si)])
                                if si > 0:
                                    e.wait_ge(dve2.h, ev[("top2", ob, si - 1)])
                                elif ob > 0:
                                    e.wait_ge(dve2.h, ev[("top2", ob - 1, 3)])
                                x = e.activation(tcol[:, 5:6], tcol[:, 2:3],
                                                 AF.Exp, bias=0.0, scale=1.0)
                                x.then_inc(act2.h, 1)
                        V(p4_v4)
                        A(p4_a6)

                        for si in range(4):
                            ev[("w8tev", ob, si)] = act1.pre()

                        def p4_t4(e, ob=ob):
                            for si in range(4):
                                e.wait_ge(dve2.h, ev[("top2", ob, si)])
                                if si == 0:
                                    e.wait_ge(act2.h, ev[("rm4ev", ob)])
                                    wt(e, ev["idn4"])
                                    if ob > 0:
                                        e.wait_ge(act1.h,
                                                  ev[("w8tev", ob - 1, 3)])
                                else:
                                    e.wait_ge(act1.h, ev[("w8tev", ob, si - 1)])
                                tr = e.transpose(ps_w4[:, 0:P],
                                                 we8[:, si * P:(si + 1) * P],
                                                 idn_f4[:])
                                ev[("w8tr", ob, si)] = pe2.inc(tr)

                        def p4_a7(e, ob=ob):
                            for si in range(4):
                                e.wait_ge(pe2.h, ev[("w8tr", ob, si)])
                                cp = e.activation(w8t[0:8, si * P:(si + 1) * P],
                                                  ps_w4[0:8, 0:P], AF.Copy,
                                                  bias=0.0, scale=1.0)
                                cp.then_inc(act1.h, 1)

                        def p4_t5(e, ob=ob):
                            e.wait_ge(act1.h, ev[("w8tev", ob, 3)])
                            if ob == 0:
                                wt(e, ev["e8"])
                            mm = e.matmul(ps_u4[:, TBW:2 * TBW], e8_bf[:], w8t[:],
                                          start=True, stop=True)
                            ev[("wxmm", ob)] = pe2.inc(mm)

                        def p4_v5(e, ob=ob):
                            e.wait_ge(pe2.h, ev[("wxmm", ob)])
                            e.wait_ge(act1.h, ev[("axev", ob)])
                            tt = e.tensor_mul(sc_sb[:], ax_sb[:],
                                              ps_u4[:, TBW:2 * TBW])
                            ev[("sc", ob)] = dve1.inc(tt)
                        T(p4_t4)
                        A(p4_a7)
                        T(p4_t5)
                        V(p4_v5)

                        for oc in range(NCH):
                            ev[("z4mm", ob, oc)] = pe1.pre()
                            ev[("z4ev", ob, oc)] = act1.pre()

                        def p4_g1b(e, ob=ob):
                            for oc in range(NCH):
                                pc = (ob, oc - 2) if oc >= 2 else (ob - 1, oc + 14)
                                if ("z4mm",) + pc in ev:
                                    e.wait_ge(pe1.h, ev[("z4mm",) + pc])
                                o0 = oc * P
                                ev[("wf", ob, oc)] = dma(
                                    e, sWF[oc % 2], wfc[oc % 2][:],
                                    wffn_d[0:D, o0:o0 + P].rearrange(
                                        "(c p) o -> p c o", p=P))
                        G(p4_g1b)

                        def p4_t6(e, ob=ob, par=par):
                            e.wait_ge(dve1.h, ev[("sc", ob)])
                            if ob == 0:
                                wt(e, ev["wb"])
                            for oc in range(NCH):
                                wt(e, ev[("wf", ob, oc)])
                                pc = (ob, oc - 2) if oc >= 2 else (ob - 1, oc + 14)
                                if ("z4ev",) + pc in ev:
                                    e.wait_ge(act1.h, ev[("z4ev",) + pc])
                                lo = (oc % 2) * TBW
                                for c in range(NCH):
                                    e.matmul(ps_z4[:, lo:lo + TBW],
                                             wfc[oc % 2][:, c, :],
                                             ln2[:, par, c, :],
                                             start=(c == 0), stop=False)
                                o0 = oc * P
                                mm = e.matmul(ps_z4[:, lo:lo + TBW],
                                              wb_bf[:, o0:o0 + P], sc_sb[:],
                                              start=False, stop=True)
                                mm.then_inc(pe1.h, 1)

                        def p4_a8(e, ob=ob, par=par):
                            for oc in range(NCH):
                                e.wait_ge(pe1.h, ev[("z4mm", ob, oc)])
                                if oc == 0 and ob >= 2:
                                    e.wait_ge(dve2.h, ev[("res4", ob - 2)])
                                    e.wait_ge(pe1.h, ev[("st4b", ob - 2)])
                                lo = (oc % 2) * TBW
                                cp = e.activation(z2[:, par, oc, :],
                                                  ps_z4[:, lo:lo + TBW], AF.Copy,
                                                  bias=0.0, scale=1.0)
                                cp.then_inc(act1.h, 1)
                        T(p4_t6)
                        A(p4_a8)

                    def p4_tail(ob):
                        par = ob % 2

                        def p4_v6(e, ob=ob, par=par):
                            e.wait_ge(act1.h, ev[("z4ev", ob, NCH - 1)])
                            if ob > 0:
                                e.wait_ge(pe1.h, ev[("st4b", ob - 1)])
                            for c in range(NCH):
                                tt = e.tensor_mul(sq4b[:, c, :], z2[:, par, c, :],
                                                  z2[:, par, c, :])
                            ev[("z4sq", ob)] = dve1.inc(tt)

                        def p4_t7(e, ob=ob, par=par):
                            e.wait_ge(act1.h, ev[("z4ev", ob, NCH - 1)])
                            if ob + 1 < NOB:
                                e.wait_ge(act1.h, ev[("math4", ob + 1)]["mu"])
                            elif ob > 0:
                                e.wait_ge(act1.h, ev[("math4b", ob - 1)]["mu"])
                            for c in range(NCH):
                                e.matmul(ps_st4[0:1, 0:TBW], ones_red[:, 0:1],
                                         z2[:, par, c, :], start=(c == 0),
                                         stop=(c == NCH - 1))
                            e.wait_ge(dve1.h, ev[("z4sq", ob)])
                            for c in range(NCH):
                                mm = e.matmul(ps_st4[0:1, TBW:2 * TBW],
                                              ones_red[:, 0:1], sq4b[:, c, :],
                                              start=(c == 0), stop=(c == NCH - 1))
                            ev[("st4b", ob)] = pe1.inc(mm)
                        V(p4_v6)
                        T(p4_t7)

                        ev[("math4b", ob)] = {}
                        pmb = (ev[("math4b", ob - 1)]["mur"] if ob >= 1
                               else ev[("math4", 0)]["mur"])
                        stats_math(ps_st4, par, r4row[0:1, par, :],
                                   m4row[0:1, par, :], ev[("st4b", ob)], pmb,
                                   ev[("math4b", ob)])
                        md4b = ev[("math4b", ob)]

                        def p4_t8(e, ob=ob, par=par, md=md4b):
                            e.wait_ge(act1.h, md["r"])
                            e.wait_ge(act1.h, ev[("w8tev", ob, 3)])
                            mm = e.matmul(ps_w4[:, 0:TBW], oneB[0:1, :],
                                          r4row[0:1, par, :], start=True,
                                          stop=True)
                            ev[("rr4bmm", ob)] = pe1.inc(mm)

                        def p4_a11(e, ob=ob, par=par):
                            e.wait_ge(pe1.h, ev[("rr4bmm", ob)])
                            cp = e.activation(rr4[:, par, :], ps_w4[:, 0:TBW],
                                              AF.Copy, bias=0.0, scale=1.0)
                            ev[("rr4bev", ob)] = act2.inc(cp)

                        def p4_t9(e, ob=ob, par=par, md=md4b):
                            e.wait_ge(dve2.h, md["mur"])
                            e.wait_ge(act2.h, ev[("rr4bev", ob)])
                            mm = e.matmul(ps_w4[:, 0:TBW], oneB[0:1, :],
                                          m4row[0:1, par, :], start=True,
                                          stop=True)
                            ev[("rm4bmm", ob)] = pe1.inc(mm)

                        def p4_a12(e, ob=ob, par=par):
                            e.wait_ge(pe1.h, ev[("rm4bmm", ob)])
                            cp = e.activation(rm4[:, par, :], ps_w4[:, 0:TBW],
                                              AF.Copy, bias=0.0, scale=1.0)
                            ev[("rm4bev", ob)] = act2.inc(cp)

                        def p4_v9(e, ob=ob, par=par):
                            e.wait_ge(act2.h, ev[("rm4bev", ob)])
                            for c in range(NCH):
                                pc = (ob, c - 2) if c >= 2 else (ob - 1, c + 14)
                                if ("x2dmac",) + pc in ev:
                                    wt(e, ev[("x2dmac",) + pc])
                                ed(e, dve2, e.tensor_mul(tt4[:], z2[:, par, c, :],
                                                         rr4[:, par, :]))
                                ed(e, dve2, e.scalar_tensor_tensor(
                                    tt4[:], rm4[:, par, :], -1.0, tt4[:],
                                    ALU.mult, ALU.add))
                                tt = e.tensor_add(x2st[:, c % 2, :], tt4[:],
                                                  x1b[:, par, c, :])
                                ev[("res4c", ob, c)] = dve2.inc(tt)
                                e.wait_ge(dve2.h, dve2.n)
                            ev[("res4", ob)] = ev[("res4c", ob, NCH - 1)]

                        def p4_sy2(e, ob=ob):
                            t0 = ob * TBW
                            for c in range(NCH):
                                e.wait_ge(dve2.h, ev[("res4c", ob, c)])
                                e.dma_start(
                                    out_d[:, c, t0:t0 + TBW],
                                    x2st[:, c % 2, :]).then_inc(sX2O[c % 2].h, 16)
                        T(p4_t8)
                        A(p4_a11)
                        T(p4_t9)
                        A(p4_a12)
                        V(p4_v9)
                        SY(p4_sy2)

                    for ob in range(NOB):
                        for c in range(NCH):
                            ct = sX2O[c % 2]
                            ev[("x2dmac", ob, c)] = (ct, ct.pre(16))
                        p4_headmid(ob)
                        if ob >= 1:
                            p4_tail(ob - 1)
                    p4_tail(NOB - 1)

        def fin(e):
            for c in allct:
                e.wait_ge(c.h, c.n)
        SY(fin)
    return nc


def own_cols(c):
    cols = []
    for b in range(B):
        for t in range(8):
            a = b * S + (c + 8 * t) * P
            cols.extend(range(a, a + P))
    return np.array(cols)


def host_prep_shared(inputs):
    f = np.float32
    x = np.asarray(inputs["L0_attn_x"], f)
    wqkv = np.asarray(inputs["L0_attn_W_qkv"], f)
    sh = {}
    sh["xt_full"] = np.ascontiguousarray(x.reshape(TOK, D).T)
    sh["wq"] = np.ascontiguousarray(wqkv[:, 0:D])
    sh["wk"] = np.ascontiguousarray(wqkv[:, D:D + HD])
    sh["wv"] = np.ascontiguousarray(wqkv[:, D + HD:D + 2 * HD])
    sh["brow"] = np.asarray(inputs["L0_attn_b_qkv"], f).reshape(1, -1)
    sh["wproj"] = np.ascontiguousarray(np.asarray(inputs["L0_attn_W_proj"], f))
    sh["bprow"] = np.asarray(inputs["L0_attn_b_proj"], f).reshape(1, -1)
    sh["wffn"] = np.ascontiguousarray(np.asarray(inputs["L0_ffn_weight"], f).T)
    sh["wa"] = np.ascontiguousarray(
        np.asarray(inputs["L0_ffn_lora_A"], f).reshape(128, D).T)
    sh["wb"] = np.ascontiguousarray(
        np.asarray(inputs["L0_ffn_lora_B"], f).transpose(0, 2, 1).reshape(128, D))
    sh["wr"] = np.ascontiguousarray(np.asarray(inputs["L0_ffn_router_w"], f).T)
    e8m = np.zeros((8, 128), f)
    for e in range(8):
        e8m[e, e * 16:(e + 1) * 16] = 1.0
    sh["e8"] = e8m
    sh["idn"] = np.eye(P, dtype=f)
    return sh


def host_prep_core(sh, c):
    m = {k: v for k, v in sh.items() if k != "xt_full"}
    cols = own_cols(c)
    m["xt_own"] = np.ascontiguousarray(sh["xt_full"][:, cols])
    dltv = 128.0 * (c - np.arange(8, dtype=np.float32))
    m["dlt"] = np.ascontiguousarray(
        np.broadcast_to(dltv[None, :], (P, 8)).astype(np.float32))
    return m


def host_gather(results):
    full = np.empty((TOK, D), np.float32)
    for c, r in enumerate(results):
        o = np.asarray(r["out"])
        xt = o.transpose(1, 0, 2).reshape(D, OWN)
        full[own_cols(c), :] = xt.T
    return full.reshape(B, S, D)


def kernel(**inputs):
    from concourse.bass_utils import run_bass_kernel_spmd
    sh = host_prep_shared(inputs)
    in_maps = [host_prep_core(sh, c) for c in range(8)]
    nc = build(nphase=5, debug=False)
    res = run_bass_kernel_spmd(nc, in_maps, core_ids=list(range(8)))
    return host_gather(res.results)


# version 06
# speedup vs baseline: 1.0174x; 1.0174x over previous
import numpy as np
from contextlib import ExitStack

import concourse.bass as bass
import concourse.mybir as mybir

F32 = mybir.dt.float32
BF16 = mybir.dt.bfloat16
AF = mybir.ActivationFunctionType
ALU = mybir.AluOpType
AX = mybir.AxisListType.X

P = 128
D = 2048
NCH = 16
NH = 16
HD = 128
B = 2
S = 8192
TOK = B * S
TBW = 512
NTB = TOK // TBW
OWN = 2048
NOB = 4
NJT = 64
EPS = 1e-5
SCALE = float(HD) ** -0.5
NBH = B * NH


class Ct:
    def __init__(self, stack, nc, name):
        self.h = stack.enter_context(nc.semaphore(name))
        self.n = 0

    def inc(self, instr, k=1):
        instr.then_inc(self.h, k)
        self.n += k
        return self.n

    def pre(self, k=1):
        self.n += k
        return self.n


def build(nphase=5, debug=False):
    nc = bass.Bass("TRN2", target_bir_lowering=False)

    xt_own = nc.dram_tensor("xt_own", [D, OWN], F32, kind="ExternalInput").ap()
    wq_d = nc.dram_tensor("wq", [D, D], F32, kind="ExternalInput").ap()
    wk_d = nc.dram_tensor("wk", [D, HD], F32, kind="ExternalInput").ap()
    wv_d = nc.dram_tensor("wv", [D, HD], F32, kind="ExternalInput").ap()
    brow_d = nc.dram_tensor("brow", [1, D + 2 * HD], F32, kind="ExternalInput").ap()
    wproj_d = nc.dram_tensor("wproj", [D, D], F32, kind="ExternalInput").ap()
    bprow_d = nc.dram_tensor("bprow", [1, D], F32, kind="ExternalInput").ap()
    wffn_d = nc.dram_tensor("wffn", [D, D], F32, kind="ExternalInput").ap()
    wa_d = nc.dram_tensor("wa", [D, 128], F32, kind="ExternalInput").ap()
    wb_d = nc.dram_tensor("wb", [128, D], F32, kind="ExternalInput").ap()
    wr_d = nc.dram_tensor("wr", [D, 8], F32, kind="ExternalInput").ap()
    dlt_d = nc.dram_tensor("dlt", [P, 8], F32, kind="ExternalInput").ap()
    e8_d = nc.dram_tensor("e8", [8, 128], F32, kind="ExternalInput").ap()
    idn_d = nc.dram_tensor("idn", [P, P], F32, kind="ExternalInput").ap()

    out_d = nc.dram_tensor("out", [P, NCH, OWN], F32, kind="ExternalOutput").ap()
    qt_scr = nc.dram_tensor("qt_scr", [NH * P, OWN], BF16, kind="Internal").ap()
    kag_in = nc.dram_tensor("kag_in", [P, OWN], BF16, kind="Internal").ap()
    wp_scr = nc.dram_tensor("wp_scr", [D, D], BF16, kind="Internal").ap()
    wf_scr = nc.dram_tensor("wf_scr", [D, D], BF16, kind="Internal").ap()
    vag_in = nc.dram_tensor("vag_in", [OWN, P], BF16, kind="Internal").ap()
    kag_out = nc.dram_tensor("kag_out", [8 * P, OWN], BF16, kind="Internal").ap()
    vag_out = nc.dram_tensor("vag_out", [8 * OWN, P], BF16, kind="Internal").ap()
    x1_scr = nc.dram_tensor("x1_scr", [P, NCH, OWN], F32, kind="Internal").ap()

    dbg = {}
    if debug:
        dbg["kt"] = nc.dram_tensor("dbg_kt", [P, TOK], BF16, kind="ExternalOutput").ap()
        dbg["va"] = nc.dram_tensor("dbg_va", [P, B * NJT * 129], BF16, kind="ExternalOutput").ap()
        dbg["rows"] = nc.dram_tensor("dbg_rows", [2, TOK], BF16, kind="ExternalOutput").ap()
        if nphase >= 2:
            dbg["qt"] = nc.dram_tensor("dbg_qt", [NH * P, OWN], BF16, kind="ExternalOutput").ap()
            dbg["orows"] = nc.dram_tensor("dbg_orows", [2, OWN], BF16, kind="ExternalOutput").ap()
        if nphase >= 3:
            dbg["oall"] = nc.dram_tensor("dbg_oall", [P, NH, OWN], BF16, kind="ExternalOutput").ap()
        if nphase >= 4:
            dbg["x1"] = nc.dram_tensor("dbg_x1", [P, NCH, OWN], F32, kind="ExternalOutput").ap()

    st = ExitStack()
    with st:
        ec = st.enter_context

        def sbl(name, shape, dt):
            return ec(nc.sbuf_tensor(name, shape, dt, side="left"))

        ones_red = sbl("ones_red", [P, P], BF16)
        onesrow = sbl("onesrow", [1, TBW], BF16)
        negcol = sbl("negcol", [P, 1], BF16)
        oneB = sbl("oneB", [1, P], BF16)
        idn_f = sbl("idn_f", [P, P], F32)
        idn_bf = sbl("idn_bf", [P, P], BF16)
        dD = sbl("dD", [P, P], F32)
        dlt_sb = sbl("dlt_sb", [P, 8], F32)
        tmask = sbl("tmask", [P, 8, P], BF16)
        robf = sbl("robf", [1, OWN], BF16)
        mrobf = sbl("mrobf", [1, OWN], BF16)
        brow_sb = sbl("brow_sb", [1, D + 2 * HD], BF16)
        mu_t = sbl("mu_t", [1, 2, TBW], F32)
        msq_t = sbl("msq_t", [1, 2, TBW], F32)
        rf_t = sbl("rf_t", [1, 2, TBW], F32)
        scrap = sbl("scrap", [1, 8], F32)
        epscol = sbl("epscol", [P, 1], F32)

        sB = Ct(st, nc, "sB")
        sC = Ct(st, nc, "sC")
        sCs = Ct(st, nc, "sCs")
        sDBG = Ct(st, nc, "sDBG")
        sXB = [Ct(st, nc, f"sXB{i}") for i in range(2)]
        sXO = Ct(st, nc, "sXO")
        sWQ = [Ct(st, nc, f"sWQ{i}") for i in range(2)]
        sQO = [Ct(st, nc, f"sQO{i}") for i in range(2)]
        sQI = [Ct(st, nc, f"sQI{i}") for i in range(2)]
        sWP = [Ct(st, nc, f"sWP{i}") for i in range(2)]
        sWF = [Ct(st, nc, f"sWF{i}") for i in range(2)]
        sXOF = [Ct(st, nc, f"sXOF{i}") for i in range(2)]
        sAGB = Ct(st, nc, "sAGB")
        sPWI = [Ct(st, nc, f"sPWI{i}") for i in range(2)]
        sPWO = [Ct(st, nc, f"sPWO{i}") for i in range(2)]
        sCC = Ct(st, nc, "sCC")
        sKV = Ct(st, nc, "sKV")
        sX1O = [Ct(st, nc, f"sX1O{i}") for i in range(2)]
        sX1I = Ct(st, nc, "sX1I")
        sX2O = [Ct(st, nc, f"sX2O{i}") for i in range(2)]
        pe1 = Ct(st, nc, "pe1")
        pe2 = Ct(st, nc, "pe2")
        act1 = Ct(st, nc, "act1")
        act2 = Ct(st, nc, "act2")
        dve1 = Ct(st, nc, "dve1")
        dve2 = Ct(st, nc, "dve2")
        dveE = Ct(st, nc, "dveE")
        allct = ([sC, sCs, sDBG, sXO, sAGB, sCC, sKV, sX1I] + sPWI + sPWO +
                 sXOF + sX2O +
                 sWQ + sQO + sQI +
                 sWP + sWF + sX1O + [pe1, pe2, act1, act2, dve1, dve2, dveE])

        def ed(e, ct, instr):
            ct.inc(instr)
            e.wait_ge(ct.h, ct.n)

        def dma(e, ct, out, in_):
            v = ct.inc(e.dma_start(out, in_), 16)
            return (ct, v)

        def wt(e, evt):
            e.wait_ge(evt[0].h, evt[1])

        block = ec(nc.Block())
        T = block.tensor
        V = block.vector
        A = block.scalar
        G = block.gpsimd
        SY = block.sync

        def entry(e):
            e.wait_ge(sB.h, sB.n)

        def barrier():
            vals = [(c, c.n) for c in allct]

            def bt(e):
                for c, v in vals:
                    e.wait_ge(c.h, v)
                sB.inc(e.ldweights(ones_red[:, 0:1]))

            def bv(e):
                for c, v in vals:
                    e.wait_ge(c.h, v)
                sB.inc(e.tensor_copy(scrap[0:1, 0:2], scrap[0:1, 2:4]))

            def ba(e):
                for c, v in vals:
                    e.wait_ge(c.h, v)
                sB.inc(e.activation(scrap[0:1, 4:6], scrap[0:1, 6:8], AF.Copy,
                                    bias=0.0, scale=1.0))
            T(bt)
            V(bv)
            A(ba)

        ev = {}

        def stats_math(ps, slot, rrow, murrow, ev_st, prev_mur, evout, W=TBW):
            def m_a1(e):
                e.wait_ge(pe1.h, ev_st)
                if prev_mur is not None:
                    e.wait_ge(dve2.h, prev_mur)
                e.activation(mu_t[0:1, slot, 0:W], ps[0:1, 0:W],
                             AF.Copy, bias=0.0, scale=1.0 / D)
                cp = e.activation(msq_t[0:1, slot, 0:W], ps[0:1, W:2 * W],
                                  AF.Copy, bias=0.0, scale=1.0 / D)
                evout["mu"] = act1.inc(cp)

            def m_v1(e):
                e.wait_ge(act1.h, evout["mu"])
                ed(e, dve2, e.tensor_mul(rf_t[0:1, slot, 0:W],
                                         mu_t[0:1, slot, 0:W],
                                         mu_t[0:1, slot, 0:W]))
                tt = e.tensor_sub(rf_t[0:1, slot, 0:W], msq_t[0:1, slot, 0:W],
                                  rf_t[0:1, slot, 0:W])
                evout["var"] = dve2.inc(tt)

            def m_a2(e):
                e.wait_ge(dve2.h, evout["var"])
                ed(e, act2, e.activation(rf_t[0:1, slot, 0:W],
                                         rf_t[0:1, slot, 0:W],
                                         AF.Ln, bias=epscol[0:1, 0:1], scale=1.0))
                ed(e, act2, e.activation(rf_t[0:1, slot, 0:W],
                                         rf_t[0:1, slot, 0:W],
                                         AF.Exp, bias=0.0, scale=-0.5))
                cp = e.activation(rrow, rf_t[0:1, slot, 0:W],
                                  AF.Copy, bias=0.0, scale=1.0)
                evout["r"] = act1.inc(cp)

            def m_v2(e):
                e.wait_ge(act1.h, evout["r"])
                tt = e.tensor_mul(murrow, mu_t[0:1, slot, 0:W],
                                  rf_t[0:1, slot, 0:W])
                evout["mur"] = dve2.inc(tt)
            A(m_a1)
            V(m_v1)
            A(m_a2)
            V(m_v2)

        def s_sy(e):
            dma(e, sCs, dlt_sb[:], dlt_d)
            dma(e, sCs, idn_f[:], idn_d)

        def s_g(e):
            dma(e, sC, brow_sb[:], brow_d)
            dma(e, sC, idn_bf[:], idn_d)
            dve2.inc(e.iota(dD[:], [[1, P]], channel_multiplier=-1,
                            allow_small_or_imprecise_dtypes=True))

        def s_v(e):
            e.memset(ones_red[:], 1.0)
            e.memset(onesrow[:], 1.0)
            e.memset(negcol[:], -1.0)
            e.memset(oneB[:], 1.0)
            e.memset(scrap[:], 1.0)
            e.memset(epscol[:], EPS)
            e.wait_ge(dve2.h, dve2.n)
            e.wait_ge(sCs.h, sCs.n)
            for r in range(8):
                dve1.inc(e.tensor_scalar(tmask[:, r, :], dD[:],
                                         dlt_sb[:, r:r + 1], 0.0,
                                         ALU.add, ALU.is_ge))
        SY(s_sy)
        G(s_g)
        V(s_v)
        barrier()

        es_o = ExitStack()
        if True:
            with ExitStack() as _es:
                kt_all = _es.enter_context(nc.sbuf_tensor("kt_all", [P, TOK], BF16, side="right"))
                v_all = _es.enter_context(nc.sbuf_tensor("v_all", [P, B * NJT, 129], BF16, side="right"))
                if nphase >= 2:
                    with ExitStack() as _es:
                        xo0 = _es.enter_context(nc.sbuf_tensor("xo0", [P, NCH, TBW], BF16, side="right"))
                        xo1 = _es.enter_context(nc.sbuf_tensor("xo1", [P, NCH, TBW], BF16, side="right"))
                        xo2 = _es.enter_context(nc.sbuf_tensor("xo2", [P, NCH, TBW], BF16, side="right"))
                        xo3 = _es.enter_context(nc.sbuf_tensor("xo3", [P, NCH, TBW], BF16, side="right"))
                        xsq = _es.enter_context(nc.sbuf_tensor("xsq", [P, NCH, TBW], BF16, side="right"))
                        wqc0 = _es.enter_context(nc.sbuf_tensor("wqc0", [P, NCH, P], BF16, side="right"))
                        wqc1 = _es.enter_context(nc.sbuf_tensor("wqc1", [P, NCH, P], BF16, side="right"))
                        rrown = _es.enter_context(nc.sbuf_tensor("rrown", [P, NOB, TBW], BF16, side="right"))
                        wnq = _es.enter_context(nc.sbuf_tensor("wnq", [1, P], BF16, side="right"))
                        qstage = _es.enter_context(nc.sbuf_tensor("qstage", [P, 2, TBW], BF16, side="right"))
                        wk_bf = _es.enter_context(nc.sbuf_tensor("wk_bf", [P, NCH, HD], BF16, side="right"))
                        wv_bf = _es.enter_context(nc.sbuf_tensor("wv_bf", [P, NCH, HD], BF16, side="right"))
                        wnkv = _es.enter_context(nc.sbuf_tensor("wnkv", [1, 2 * P], BF16, side="right"))
                        kstage = _es.enter_context(nc.sbuf_tensor("kstage", [P, NOB, TBW], BF16, side="right"))
                        vstage = _es.enter_context(nc.sbuf_tensor("vstage", [P, NCH, P], BF16, side="right"))
                        vtmp1 = _es.enter_context(nc.sbuf_tensor("vtmp1", [P, TBW], BF16, side="right"))
                        ps_q = _es.enter_context(nc.psum_tensor("ps_q", [P, 2 * TBW], F32))
                        ps_st1 = _es.enter_context(nc.psum_tensor("ps_st1", [1, 2 * TBW], F32))
                        ps_rr1 = _es.enter_context(nc.psum_tensor("ps_rr1", [P, TBW], F32))
                        ps_kv1 = _es.enter_context(nc.psum_tensor("ps_kv1", [P, 2 * TBW], F32))
                        ps_tr1 = _es.enter_context(nc.psum_tensor("ps_tr1", [P, P], BF16))
                        ps_w = ps_rr1
                        xol = [xo0, xo1, xo2, xo3]
                        wqc = [wqc0, wqc1]

                        def p1a_g(e):
                            entry(e)
                            for ob in range(NOB):
                                t0 = ob * TBW
                                dma(e, sXO, xol[ob][:],
                                    xt_own[0:D, t0:t0 + TBW].rearrange(
                                        "(c p) t -> p c t", p=P))
                            for ob in range(NOB):
                                ev[("xo", ob)] = (sXO, sXO.n)
                            dma(e, sC, wk_bf[:],
                                wk_d.rearrange("(c p) o -> p c o", p=P))
                            ev["wkv"] = dma(e, sC, wv_bf[:],
                                            wv_d.rearrange("(c p) o -> p c o", p=P))
                            ev[("wq", 0)] = dma(
                                e, sWQ[0], wqc[0][:],
                                wq_d[0:D, 0:P].rearrange("(c p) o -> p c o", p=P))
                        G(p1a_g)

                        ev[("wn_ev", 0)] = act2.pre()
                        ev[("wn_ev", 1)] = act2.pre()

                        def p01_wn_t(e):
                            entry(e)
                            wt(e, ev["wkv"])
                            for half in range(2):
                                if half == 1:
                                    e.wait_ge(act2.h, ev[("wn_ev", 0)])
                                w = wk_bf if half == 0 else wv_bf
                                for c in range(NCH):
                                    mm = e.matmul(ps_w[0:1, 0:P], negcol[:],
                                                  w[:, c, :], start=(c == 0),
                                                  stop=(c == NCH - 1))
                                ev[("wn", half)] = pe1.inc(mm)

                        def p01_wn_a(e):
                            entry(e)
                            for half in range(2):
                                e.wait_ge(pe1.h, ev[("wn", half)])
                                cp = e.activation(wnkv[0:1, half * P:(half + 1) * P],
                                                  ps_w[0:1, 0:P], AF.Copy,
                                                  bias=0.0, scale=1.0)
                                cp.then_inc(act2.h, 1)

                        def p01_ones_v(e):
                            entry(e)
                            dve2.inc(e.memset(v_all[:, :, 128:129], 1.0))
                        T(p01_wn_t)
                        A(p01_wn_a)
                        V(p01_ones_v)

                        for ob in range(NOB):
                            slot = ob % 2

                            def p1a_v1(e, ob=ob):
                                wt(e, ev[("xo", ob)])
                                if ob - 1 >= 0:
                                    e.wait_ge(pe1.h, ev[("ost", ob - 1)])
                                for c in range(NCH):
                                    tt = e.tensor_mul(xsq[:, c, :],
                                                      xol[ob][:, c, :],
                                                      xol[ob][:, c, :])
                                ev[("osq", ob)] = dve1.inc(tt)

                            def p1a_t1(e, ob=ob):
                                wt(e, ev[("xo", ob)])
                                if ob - 1 >= 0:
                                    e.wait_ge(act1.h,
                                              ev[("omath", ob - 1)]["mu"])
                                for c in range(NCH):
                                    e.matmul(ps_st1[0:1, 0:TBW], ones_red[:, 0:1],
                                             xol[ob][:, c, :], start=(c == 0),
                                             stop=(c == NCH - 1))
                                e.wait_ge(dve1.h, ev[("osq", ob)])
                                for c in range(NCH):
                                    mm = e.matmul(ps_st1[0:1, TBW:2 * TBW],
                                                  ones_red[:, 0:1], xsq[:, c, :],
                                                  start=(c == 0),
                                                  stop=(c == NCH - 1))
                                ev[("ost", ob)] = pe1.inc(mm)

                            V(p1a_v1)
                            T(p1a_t1)

                            ev[("omath", ob)] = {}
                            pm = ev[("omath", ob - 2)]["mur"] if ob >= 2 else None
                            t0ob = ob * TBW
                            stats_math(ps_st1, slot, robf[0:1, t0ob:t0ob + TBW],
                                       mrobf[0:1, t0ob:t0ob + TBW],
                                       ev[("ost", ob)], pm, ev[("omath", ob)])

                            def p1a_t2(e, ob=ob):
                                e.wait_ge(act1.h, ev[("omath", ob)]["r"])
                                if ob - 1 >= 0:
                                    e.wait_ge(act2.h, ev[("rrevo", ob - 1)])
                                else:
                                    e.wait_ge(act2.h, ev[("wn_ev", 1)])
                                t0 = ob * TBW
                                mm = e.matmul(ps_rr1[:, 0:TBW], oneB[0:1, :],
                                              robf[0:1, t0:t0 + TBW], start=True,
                                              stop=True)
                                ev[("rrmmo", ob)] = pe1.inc(mm)

                            def p1a_a2(e, ob=ob):
                                e.wait_ge(pe1.h, ev[("rrmmo", ob)])
                                cp = e.activation(rrown[:, ob, :], ps_rr1[:, 0:TBW],
                                                  AF.Copy, bias=0.0, scale=1.0)
                                ev[("rrevo", ob)] = act2.inc(cp)

                            T(p1a_t2)
                            A(p1a_a2)

                        for ob in range(NOB):
                            def p01_kv_t(e, ob=ob):
                                e.wait_ge(act2.h, ev[("rrevo", ob)])
                                if ob > 0:
                                    e.wait_ge(dve1.h, ev[("kvfix", ob - 1)])
                                t0 = ob * TBW
                                for half in range(2):
                                    w = wk_bf if half == 0 else wv_bf
                                    lo = half * TBW
                                    for c in range(NCH):
                                        e.matmul(ps_kv1[:, lo:lo + TBW],
                                                 w[:, c, :], xol[ob][:, c, :],
                                                 start=(c == 0), stop=False)
                                    boff = D + half * HD
                                    e.matmul(ps_kv1[:, lo:lo + TBW],
                                             brow_sb[0:1, boff:boff + P],
                                             onesrow[:], start=False, stop=False)
                                    mm = e.matmul(
                                        ps_kv1[:, lo:lo + TBW],
                                        wnkv[0:1, half * P:(half + 1) * P],
                                        mrobf[0:1, t0:t0 + TBW],
                                        start=False, stop=True)
                                ev[("kvmm", ob)] = pe2.inc(mm)

                            def p01_kv_v(e, ob=ob):
                                e.wait_ge(pe2.h, ev[("kvmm", ob)])
                                if ob > 0:
                                    e.wait_ge(pe2.h, ev[("vtr", ob - 1, 3)])
                                e.tensor_mul(kstage[:, ob, :],
                                             ps_kv1[:, 0:TBW], rrown[:, ob, :])
                                tt = e.tensor_mul(vtmp1[:],
                                                  ps_kv1[:, TBW:2 * TBW],
                                                  rrown[:, ob, :])
                                ev[("kvfix", ob)] = dve1.inc(tt)

                            for i in range(4):
                                ev[("vtrev", ob, i)] = act2.pre()

                            def p01_tr_t(e, ob=ob):
                                e.wait_ge(dve1.h, ev[("kvfix", ob)])
                                for i in range(4):
                                    prev = (ob, i - 1) if i > 0 else (ob - 1, 3)
                                    if ("vtrev",) + prev in ev:
                                        e.wait_ge(act2.h, ev[("vtrev",) + prev])
                                    tr = e.transpose(ps_tr1[:],
                                                     vtmp1[:, i * P:(i + 1) * P],
                                                     idn_bf[:])
                                    ev[("vtr", ob, i)] = pe2.inc(tr)

                            def p01_tr_a(e, ob=ob):
                                for i in range(4):
                                    e.wait_ge(pe2.h, ev[("vtr", ob, i)])
                                    cp = e.activation(
                                        vstage[:, ob * 4 + i, :], ps_tr1[:],
                                        AF.Copy, bias=0.0, scale=1.0)
                                    cp.then_inc(act2.h, 1)

                            T(p01_kv_t)
                            V(p01_kv_v)
                            T(p01_tr_t)
                            A(p01_tr_a)

                        def p01_ag_g(e):
                            e.wait_ge(dve1.h, ev[("kvfix", NOB - 1)])
                            dma(e, sAGB, kag_in,
                                kstage[:].rearrange("p a b -> p (a b)"))
                            e.wait_ge(act2.h, ev[("vtrev", NOB - 1, 3)])
                            dma(e, sAGB, vag_in.rearrange("(t p) c -> p t c", p=P),
                                vstage[:])
                            e.wait_ge(sAGB.h, sAGB.n)
                            cc = e.collective_compute(
                                "AllGather", mybir.AluOpType.bypass,
                                replica_groups=[list(range(8))],
                                ins=[kag_in.opt()], outs=[kag_out.opt()])
                            sCC.inc(cc)
                            cc = e.collective_compute(
                                "AllGather", mybir.AluOpType.bypass,
                                replica_groups=[list(range(8))],
                                ins=[vag_in.opt()], outs=[vag_out.opt()])
                            sCC.inc(cc)
                        G(p01_ag_g)

                        def p01_sc_sy(e):
                            e.wait_ge(sCC.h, sCC.n)
                            ktv = kt_all[:].rearrange(
                                "p (b t2 rr c) -> p b t2 rr c", b=B, t2=8, rr=8,
                                c=P)
                            vav = v_all[:].rearrange(
                                "p (b t2 rr) c -> p b t2 rr c", b=B, t2=8, rr=8)
                            for r in range(8):
                                src = kag_out[r * P:(r + 1) * P, :].rearrange(
                                    "p (b t c) -> p b t c", b=B, t=8, c=TBW // 4)
                                dma(e, sKV, ktv[:, :, :, r, :], src)
                                vsrc = vag_out[r * OWN:(r + 1) * OWN, :].rearrange(
                                    "(b t k) c -> k b t c", b=B, t=8, k=P)
                                dma(e, sKV, vav[:, :, :, r, 0:P], vsrc)
                        SY(p01_sc_sy)

                        for h in range(NH):
                            parh = h % 2

                            def p1b_g1(e, h=h):
                                if h + 1 < NH:
                                    if h - 1 >= 0:
                                        e.wait_ge(pe2.h,
                                                  ev[("qmm", h - 1, NOB - 1)])
                                    o0 = (h + 1) * P
                                    ev[("wq", h + 1)] = dma(
                                        e, sWQ[(h + 1) % 2], wqc[(h + 1) % 2][:],
                                        wq_d[0:D, o0:o0 + P].rearrange(
                                            "(c p) o -> p c o", p=P))

                            def p1b_t1(e, h=h, parh=parh):
                                wt(e, ev[("wq", h)])
                                if h == 0:
                                    e.wait_ge(dve2.h,
                                              ev[("omath", NOB - 1)]["mur"])
                                    e.wait_ge(act2.h, ev[("rrevo", NOB - 1)])
                                    e.wait_ge(pe2.h, ev[("kvmm", NOB - 1)])
                                if h > 0:
                                    e.wait_ge(act2.h, ev[("wnqev", h - 1)])
                                for c in range(NCH):
                                    mm = e.matmul(ps_w[0:1, 0:P], negcol[:],
                                                  wqc[parh][:, c, :],
                                                  start=(c == 0),
                                                  stop=(c == NCH - 1))
                                ev[("wnq", h)] = pe1.inc(mm)

                            def p1b_a1(e, h=h):
                                e.wait_ge(pe1.h, ev[("wnq", h)])
                                cp = e.activation(wnq[0:1, :], ps_w[0:1, 0:P],
                                                  AF.Copy, bias=0.0, scale=1.0)
                                ev[("wnqev", h)] = act2.inc(cp)

                            def p1b_t2(e, h=h, parh=parh):
                                e.wait_ge(act2.h, ev[("wnqev", h)])
                                for ob in range(NOB):
                                    lo = (ob % 2) * TBW
                                    pob = (h, ob - 1) if ob > 0 else (h - 1, NOB - 1)
                                    if ("qev",) + pob in ev:
                                        e.wait_ge(dve1.h, ev[("qev",) + pob])
                                    for c in range(NCH):
                                        e.matmul(ps_q[:, lo:lo + TBW],
                                                 wqc[parh][:, c, :],
                                                 xol[ob][:, c, :],
                                                 start=(c == 0), stop=False)
                                    boff = h * P
                                    e.matmul(ps_q[:, lo:lo + TBW],
                                             brow_sb[0:1, boff:boff + P],
                                             onesrow[:], start=False, stop=False)
                                    t0 = ob * TBW
                                    mm = e.matmul(ps_q[:, lo:lo + TBW], wnq[0:1, :],
                                                  mrobf[0:1, t0:t0 + TBW],
                                                  start=False, stop=True)
                                    ev[("qmm", h, ob)] = pe2.inc(mm)

                            def p1b_v1(e, h=h):
                                for ob in range(NOB):
                                    e.wait_ge(pe2.h, ev[("qmm", h, ob)])
                                    e.wait_ge(act2.h, ev[("rrevo", ob)])
                                    qp = ob % 2
                                    pob = (h, ob - 2) if ob >= 2 else (h - 1, ob + 2)
                                    if ("qdma",) + pob in ev:
                                        wt(e, ev[("qdma",) + pob])
                                    lo = qp * TBW
                                    tt = e.tensor_mul(qstage[:, qp, :],
                                                      ps_q[:, lo:lo + TBW],
                                                      rrown[:, ob, :])
                                    tt.then_inc(dve1.h, 1)

                            def p1b_sy(e, h=h):
                                for ob in range(NOB):
                                    e.wait_ge(dve1.h, ev[("qev", h, ob)])
                                    qp = ob % 2
                                    t0 = ob * TBW
                                    e.dma_start(
                                        qt_scr[h * P:(h + 1) * P, t0:t0 + TBW],
                                        qstage[:, qp, :]).then_inc(sQO[qp].h, 16)

                            G(p1b_g1)
                            T(p1b_t1)
                            A(p1b_a1)
                            for ob in range(NOB):
                                ev[("qev", h, ob)] = dve1.pre()
                                ct = sQO[ob % 2]
                                ev[("qdma", h, ob)] = (ct, ct.pre(16))
                            T(p1b_t2)
                            V(p1b_v1)
                            SY(p1b_sy)

                        pw_cols = [(wproj_d, wp_scr, oc) for oc in range(NCH)] + \
                                  [(wffn_d, wf_scr, oc) for oc in range(NCH)]

                        def pw_g(e):
                            for i, (src, dst, oc) in enumerate(pw_cols):
                                par = i % 2
                                if i >= 2:
                                    e.wait_ge(sPWO[par].h, 16 * (i // 2))
                                o0 = oc * P
                                stg = xsq[:, 4 * par:4 * par + 4, :].rearrange(
                                    "p a b -> p (a b)").rearrange(
                                    "p (c o) -> p c o", c=NCH)
                                e.dma_start(stg, src[0:D, o0:o0 + P].rearrange(
                                    "(c p) o -> p c o", p=P)).then_inc(
                                    sPWI[par].h, 16)
                                sPWI[par].n += 16

                        def pw_sy(e):
                            for i, (src, dst, oc) in enumerate(pw_cols):
                                par = i % 2
                                e.wait_ge(sPWI[par].h, 16 * (i // 2 + 1))
                                o0 = oc * P
                                stg = xsq[:, 4 * par:4 * par + 4, :].rearrange(
                                    "p a b -> p (a b)").rearrange(
                                    "p (c o) -> p c o", c=NCH)
                                e.dma_start(dst[0:D, o0:o0 + P].rearrange(
                                    "(c p) o -> p c o", p=P), stg).then_inc(
                                    sPWO[par].h, 16)
                                sPWO[par].n += 16
                        G(pw_g)
                        SY(pw_sy)

                        if debug:
                            def p1_dbg(e):
                                for c in allct:
                                    e.wait_ge(c.h, c.n)
                                dma(e, sDBG, dbg["kt"], kt_all[:])
                                dma(e, sDBG, dbg["va"],
                                    v_all[:].rearrange("p a b -> p (a b)"))
                                dma(e, sDBG, dbg["qt"], qt_scr)
                                dma(e, sDBG, dbg["orows"][0:1, :], robf[:])
                                dma(e, sDBG, dbg["orows"][1:2, :], mrobf[:])
                            SY(p1_dbg)
                        barrier()

                if nphase >= 3:
                    o_all = es_o.enter_context(
                        nc.sbuf_tensor("o_all", [P, NH, OWN], BF16, side="left"))
                    with ExitStack() as _es:
                        qbuf = _es.enter_context(nc.sbuf_tensor("qbuf", [P, 2, 1024], BF16, side="right"))
                        pbuf = _es.enter_context(nc.sbuf_tensor("pbuf", [P, 4, 512], BF16, side="right"))
                        rl_sb = _es.enter_context(nc.sbuf_tensor("rl_sb", [P, 1024], F32, side="right"))
                        ps_sA0 = _es.enter_context(nc.psum_tensor("ps_sA0", [P, 512], F32))
                        ps_sA1 = _es.enter_context(nc.psum_tensor("ps_sA1", [P, 512], F32))
                        ps_sB0 = _es.enter_context(nc.psum_tensor("ps_sB0", [P, 512], F32))
                        ps_sB1 = _es.enter_context(nc.psum_tensor("ps_sB1", [P, 512], F32))
                        ps_o = _es.enter_context(nc.psum_tensor("ps_o", [P, 1024], F32))
                        ps_l = _es.enter_context(nc.psum_tensor("ps_l", [P, 1024], F32))
                        ps_sc = [[ps_sA0, ps_sA1], [ps_sB0, ps_sB1]]
                        peS0, peL0 = pe1.n, pe2.n
                        actX0, dveT0, dveE0 = act1.n, dve1.n, dveE.n
                        sQI0 = [sQI[0].n, sQI[1].n]
                        units = []
                        for j in range(NJT):
                            units.append((1, j))
                            if j < 32:
                                units.append((0, j))
                        nU = len(units)
                        cseq = {}
                        ctxu = [[], []]
                        for u, (cx, j) in enumerate(units):
                            cseq[u] = len(ctxu[cx])
                            ctxu[cx].append(u)

                        def owner(j):
                            return 0 if j < 32 else 1

                        def loc0(cx, j):
                            m = j // 8
                            return 128 * m if cx == 0 else 128 * max(0, m - 4)

                        evS = lambda g: peS0 + g + 1
                        evL = lambda g: peL0 + g + 1
                        evX = lambda g: actX0 + g + 1
                        evT = lambda bh, j: dveT0 + bh * NJT + j + 1
                        evE = lambda bh: dveE0 + bh + 1
                        evQ = lambda bh: (sQI[bh % 2],
                                          sQI0[bh % 2] + 16 * (bh // 2 + 1))

                        def p2_sy(e):
                            entry(e)
                            for bh in range(NBH):
                                b, h = bh // NH, bh % NH
                                if bh >= 2:
                                    e.wait_ge(pe1.h, evS((bh - 1) * nU - 1))
                                t0 = b * 1024
                                evt = dma(e, sQI[bh % 2], qbuf[:, bh % 2, :],
                                          qt_scr[h * P:(h + 1) * P, t0:t0 + 1024])
                                assert evt == evQ(bh)
                        SY(p2_sy)

                        for bh in range(NBH):
                            b, h = bh // NH, bh % NH
                            g0 = bh * nU

                            def p2_t(e, bh=bh, b=b, h=h, g0=g0):
                                if bh == 0:
                                    entry(e)
                                qb = qbuf[:, bh % 2, :]

                                def scores(u):
                                    cx, j = units[u]
                                    c = cseq[u]
                                    l0 = loc0(cx, j)
                                    if u == 0:
                                        wt(e, evQ(bh))
                                    if c >= 2:
                                        e.wait_ge(act1.h,
                                                  evX(g0 + ctxu[cx][c - 2]))
                                    elif bh > 0:
                                        pu = ctxu[cx][len(ctxu[cx]) - 2 + c]
                                        e.wait_ge(act1.h, evX(g0 - nU + pu))
                                    k0 = b * S + j * P
                                    mm = e.matmul(ps_sc[cx][c % 2][:, l0:512],
                                                  kt_all[:, k0:k0 + P],
                                                  qb[:, 512 * cx + l0:
                                                     512 * cx + 512],
                                                  start=True, stop=True)
                                    assert pe1.inc(mm) == evS(g0 + u)

                                def pvl(u):
                                    cx, j = units[u]
                                    c = cseq[u]
                                    l0 = loc0(cx, j)
                                    gq0 = 512 * cx + l0
                                    if owner(j) == cx:
                                        e.wait_ge(dve1.h, evT(bh, j))
                                    else:
                                        e.wait_ge(act1.h, evX(g0 + u))
                                    if j == 0 and bh > 0:
                                        e.wait_ge(dveE.h, evE(bh - 1))
                                    pb = pbuf[:, u % 4, :]
                                    stp = (j == 31) if cx == 0 else (j == NJT - 1)
                                    e.matmul(ps_o[:, gq0:512 * cx + 512],
                                             v_all[:, b * NJT + j, 0:P],
                                             pb[:, l0:512], start=(j == 0),
                                             stop=stp)
                                    mm = e.matmul(ps_l[:, gq0:512 * cx + 512],
                                                  ones_red[:], pb[:, l0:512],
                                                  start=(j == 0), stop=stp)
                                    assert pe2.inc(mm) == evL(g0 + u)

                                scores(0)
                                scores(1)
                                scores(2)
                                for u in range(nU):
                                    pvl(u)
                                    if u + 3 < nU:
                                        scores(u + 3)

                            def p2_a(e, bh=bh, g0=g0):
                                if bh == 0:
                                    entry(e)
                                for u in range(nU):
                                    cx, j = units[u]
                                    c = cseq[u]
                                    l0 = loc0(cx, j)
                                    e.wait_ge(pe1.h, evS(g0 + u))
                                    if g0 + u >= 4:
                                        e.wait_ge(pe2.h, evL(g0 + u - 4))
                                    x = e.activation(pbuf[:, u % 4, l0:512],
                                                     ps_sc[cx][c % 2][:, l0:512],
                                                     AF.Exp, bias=0.0, scale=SCALE)
                                    assert act1.inc(x) == evX(g0 + u)

                            def p2_v(e, bh=bh, b=b, h=h, g0=g0):
                                if bh == 0:
                                    entry(e)
                                for j in range(NJT):
                                    cx = owner(j)
                                    u = ctxu[cx][cseq_of_owner[j]]
                                    l0 = loc0(cx, j)
                                    e.wait_ge(act1.h, evX(g0 + u))
                                    tt = e.tensor_mul(
                                        pbuf[:, u % 4, l0:l0 + P],
                                        pbuf[:, u % 4, l0:l0 + P],
                                        tmask[:, j % 8, :])
                                    assert dve1.inc(tt) == evT(bh, j)
                                e.wait_ge(pe2.h, evL(g0 + nU - 1))
                                dve2.inc(e.reciprocal(rl_sb[:], ps_l[:]))
                                e.wait_ge(dve2.h, dve2.n)
                                t0 = b * 1024
                                e.tensor_mul(o_all[:, h, t0:t0 + 512],
                                             ps_o[:, 0:512], rl_sb[:, 0:512])
                                tt = e.tensor_mul(o_all[:, h, t0 + 512:t0 + 1024],
                                                  ps_o[:, 512:1024],
                                                  rl_sb[:, 512:1024])
                                assert dveE.inc(tt) == evE(bh)

                            cseq_of_owner = {}
                            for j in range(NJT):
                                cx = owner(j)
                                for uu in ctxu[cx]:
                                    if units[uu][1] == j:
                                        cseq_of_owner[j] = cseq[uu]
                                        break

                            T(p2_t)
                            A(p2_a)
                            V(p2_v)

                        if debug:
                            def p2_dbg(e):
                                for c in allct:
                                    e.wait_ge(c.h, c.n)
                                dma(e, sDBG, dbg["oall"],
                                    o_all[:].rearrange("p a b -> p a b"))
                            SY(p2_dbg)
                        barrier()
            if nphase >= 4:
                with ExitStack() as _es:
                    z_ob = _es.enter_context(nc.sbuf_tensor("z_ob", [P, 2, NCH, TBW], BF16, side="right"))
                    bprow_sb = _es.enter_context(nc.sbuf_tensor("bprow_sb", [1, D], BF16, side="right"))
                    zsq = _es.enter_context(nc.sbuf_tensor("zsq", [P, NCH, TBW], BF16, side="right"))
                    xof = _es.enter_context(nc.sbuf_tensor("xof", [P, 2, NCH, TBW], BF16, side="right"))
                    x1st = _es.enter_context(nc.sbuf_tensor("x1st", [P, 2, TBW], F32, side="right"))
                    wpc0 = _es.enter_context(nc.sbuf_tensor("wpc0", [P, NCH, P], BF16, side="right"))
                    wpc1 = _es.enter_context(nc.sbuf_tensor("wpc1", [P, NCH, P], BF16, side="right"))
                    rr3 = _es.enter_context(nc.sbuf_tensor("rr3", [P, 2, TBW], BF16, side="right"))
                    rm3 = _es.enter_context(nc.sbuf_tensor("rm3", [P, 2, TBW], BF16, side="right"))
                    r3row = _es.enter_context(nc.sbuf_tensor("r3row", [1, 2, TBW], BF16, side="right"))
                    m3row = _es.enter_context(nc.sbuf_tensor("m3row", [1, 2, TBW], BF16, side="right"))
                    ttmp = _es.enter_context(nc.sbuf_tensor("ttmp", [P, TBW], F32, side="right"))
                    ps_z = _es.enter_context(nc.psum_tensor("ps_z", [P, 2 * TBW], F32))
                    ps_st3 = _es.enter_context(nc.psum_tensor("ps_st3", [1, 2 * TBW], F32))
                    ps_u3 = _es.enter_context(nc.psum_tensor("ps_u3", [P, TBW], F32))
                    wpc = [wpc0, wpc1]

                    def p3_head(ob):
                        par = ob % 2

                        def p3_sy1(e, ob=ob, par=par):
                            pass

                        def p3_g(e, ob=ob, par=par):
                            if ob == 0:
                                entry(e)
                                ev["bprow"] = dma(e, sC, bprow_sb[:], bprow_d)
                            if ob >= 2:
                                e.wait_ge(dve2.h, ev[("res3", ob - 2)])
                            t0 = ob * TBW
                            ev[("xof", ob)] = dma(
                                e, sXOF[par], xof[:, par, :, :],
                                xt_own[0:D, t0:t0 + TBW].rearrange(
                                    "(c p) t -> p c t", p=P))
                            pass

                        def p3_syw(e, ob=ob):
                            for oc in range(NCH):
                                pc = (ob, oc - 2) if oc >= 2 else (ob - 1, oc + 14)
                                if ("zmm",) + pc in ev:
                                    e.wait_ge(pe1.h, ev[("zmm",) + pc])
                                o0 = oc * P
                                ev[("wp", ob, oc)] = dma(
                                    e, sWP[oc % 2], wpc[oc % 2][:],
                                    wp_scr[0:D, o0:o0 + P].rearrange(
                                        "(c p) o -> p c o", p=P))

                        def p3_t1(e, ob=ob, par=par):
                            if ob == 0:
                                entry(e)
                                wt(e, ev["bprow"])
                            for oc in range(NCH):
                                wt(e, ev[("wp", ob, oc)])
                                pc = (ob, oc - 2) if oc >= 2 else (ob - 1, oc + 14)
                                if ("zev",) + pc in ev:
                                    e.wait_ge(act1.h, ev[("zev",) + pc])
                                lo = (oc % 2) * TBW
                                for c in range(NCH):
                                    e.matmul(ps_z[:, lo:lo + TBW],
                                             wpc[oc % 2][:, c, :],
                                             o_all[:, c, ob * TBW:(ob + 1) * TBW],
                                             start=(c == 0), stop=False)
                                o0 = oc * P
                                mm = e.matmul(ps_z[:, lo:lo + TBW],
                                              bprow_sb[0:1, o0:o0 + P], onesrow[:],
                                              start=False, stop=True)
                                mm.then_inc(pe1.h, 1)

                        def p3_a1(e, ob=ob, par=par):
                            if ob == 0:
                                entry(e)
                            for oc in range(NCH):
                                e.wait_ge(pe1.h, ev[("zmm", ob, oc)])
                                if oc == 0 and ob >= 2:
                                    e.wait_ge(dve2.h, ev[("res3", ob - 2)])
                                    e.wait_ge(pe1.h, ev[("st3", ob - 2)])
                                lo = (oc % 2) * TBW
                                cp = e.activation(z_ob[:, par, oc, :],
                                                  ps_z[:, lo:lo + TBW], AF.Copy,
                                                  bias=0.0, scale=1.0)
                                cp.then_inc(act1.h, 1)

                        for oc in range(NCH):
                            ev[("zmm", ob, oc)] = pe1.pre()
                            ev[("zev", ob, oc)] = act1.pre()
                        for c in range(NCH):
                            ct = sX1O[c % 2]
                            ev[("x1dmac", ob, c)] = (ct, ct.pre(16))
                        G(p3_g)
                        SY(p3_syw)
                        T(p3_t1)
                        A(p3_a1)

                    def p3_tail(ob):
                        par = ob % 2

                        def p3_v1(e, ob=ob, par=par):
                            e.wait_ge(act1.h, ev[("zev", ob, NCH - 1)])
                            if ob > 0:
                                e.wait_ge(pe1.h, ev[("st3", ob - 1)])
                            for c in range(NCH):
                                tt = e.tensor_mul(zsq[:, c, :],
                                                  z_ob[:, par, c, :],
                                                  z_ob[:, par, c, :])
                            ev[("zsq", ob)] = dve1.inc(tt)

                        def p3_t2(e, ob=ob, par=par):
                            e.wait_ge(act1.h, ev[("zev", ob, NCH - 1)])
                            if ob > 0:
                                e.wait_ge(act1.h, ev[("math3", ob - 1)]["mu"])
                            for c in range(NCH):
                                e.matmul(ps_st3[0:1, 0:TBW], ones_red[:, 0:1],
                                         z_ob[:, par, c, :], start=(c == 0),
                                         stop=(c == NCH - 1))
                            e.wait_ge(dve1.h, ev[("zsq", ob)])
                            for c in range(NCH):
                                mm = e.matmul(ps_st3[0:1, TBW:2 * TBW],
                                              ones_red[:, 0:1], zsq[:, c, :],
                                              start=(c == 0), stop=(c == NCH - 1))
                            ev[("st3", ob)] = pe1.inc(mm)
                        V(p3_v1)
                        T(p3_t2)

                        ev[("math3", ob)] = {}
                        pm = ev[("math3", ob - 1)]["mur"] if ob >= 1 else None
                        stats_math(ps_st3, par, r3row[0:1, par, :],
                                   m3row[0:1, par, :], ev[("st3", ob)], pm,
                                   ev[("math3", ob)])
                        md = ev[("math3", ob)]

                        def p3_t3(e, ob=ob, par=par, md=md):
                            e.wait_ge(act1.h, md["r"])
                            if ob >= 2:
                                e.wait_ge(dve2.h, ev[("res3", ob - 2)])
                            mm = e.matmul(ps_u3[:, 0:TBW], oneB[0:1, :],
                                          r3row[0:1, par, :], start=True,
                                          stop=True)
                            ev[("rr3mm", ob)] = pe1.inc(mm)

                        def p3_a4(e, ob=ob, par=par):
                            e.wait_ge(pe1.h, ev[("rr3mm", ob)])
                            cp = e.activation(rr3[:, par, :], ps_u3[:, 0:TBW],
                                              AF.Copy, bias=0.0, scale=1.0)
                            ev[("rr3ev", ob)] = act2.inc(cp)

                        def p3_t4(e, ob=ob, par=par, md=md):
                            e.wait_ge(dve2.h, md["mur"])
                            e.wait_ge(act2.h, ev[("rr3ev", ob)])
                            mm = e.matmul(ps_u3[:, 0:TBW], oneB[0:1, :],
                                          m3row[0:1, par, :], start=True,
                                          stop=True)
                            ev[("rm3mm", ob)] = pe1.inc(mm)

                        def p3_a5(e, ob=ob, par=par):
                            e.wait_ge(pe1.h, ev[("rm3mm", ob)])
                            cp = e.activation(rm3[:, par, :], ps_u3[:, 0:TBW],
                                              AF.Copy, bias=0.0, scale=1.0)
                            ev[("rm3ev", ob)] = act2.inc(cp)

                        def p3_v4(e, ob=ob, par=par):
                            e.wait_ge(act2.h, ev[("rm3ev", ob)])
                            wt(e, ev[("xof", ob)])
                            for c in range(NCH):
                                pc = (ob, c - 2) if c >= 2 else (ob - 1, c + 14)
                                if ("x1dmac",) + pc in ev:
                                    wt(e, ev[("x1dmac",) + pc])
                                ed(e, dve2, e.tensor_mul(ttmp[:],
                                                         z_ob[:, par, c, :],
                                                         rr3[:, par, :]))
                                ed(e, dve2, e.scalar_tensor_tensor(
                                    ttmp[:], rm3[:, par, :], -1.0, ttmp[:],
                                    ALU.mult, ALU.add))
                                tt = e.tensor_add(x1st[:, c % 2, :], ttmp[:],
                                                  xof[:, par, c, :])
                                ev[("res3c", ob, c)] = dve2.inc(tt)
                                e.wait_ge(dve2.h, dve2.n)
                            ev[("res3", ob)] = ev[("res3c", ob, NCH - 1)]

                        def p3_sy2(e, ob=ob):
                            t0 = ob * TBW
                            for c in range(NCH):
                                e.wait_ge(dve2.h, ev[("res3c", ob, c)])
                                e.dma_start(
                                    x1_scr[:, c, t0:t0 + TBW],
                                    x1st[:, c % 2, :]).then_inc(sX1O[c % 2].h, 16)
                            ev[("x1dma", ob)] = ev[("x1dmac", ob, NCH - 1)]
                        T(p3_t3)
                        A(p3_a4)
                        T(p3_t4)
                        A(p3_a5)
                        V(p3_v4)
                        SY(p3_sy2)

                    for ob in range(NOB):
                        p3_head(ob)
                        if ob >= 1:
                            p3_tail(ob - 1)
                    p3_tail(NOB - 1)

                    if debug:
                        def p3_dbg(e):
                            for c in allct:
                                e.wait_ge(c.h, c.n)
                            dma(e, sDBG, dbg["x1"], x1_scr)
                        SY(p3_dbg)
                    barrier()
            if nphase >= 3:
                es_o.close()

            if nphase >= 5:
                with ExitStack() as _es:
                    x1b = _es.enter_context(nc.sbuf_tensor("x1b", [P, 2, NCH, TBW], BF16, side="right"))
                    sq4 = _es.enter_context(nc.sbuf_tensor("sq4", [P, NCH, TBW], BF16, side="right"))
                    sq4b = _es.enter_context(nc.sbuf_tensor("sq4b", [P, NCH, TBW], BF16, side="right"))
                    ln2 = _es.enter_context(nc.sbuf_tensor("ln2", [P, 2, NCH, TBW], BF16, side="right"))
                    z2 = _es.enter_context(nc.sbuf_tensor("z2", [P, 2, NCH, TBW], BF16, side="right"))
                    x2st = _es.enter_context(nc.sbuf_tensor("x2st", [P, 2, TBW], F32, side="right"))
                    wfc0 = _es.enter_context(nc.sbuf_tensor("wfc0", [P, NCH, P], BF16, side="right"))
                    wfc1 = _es.enter_context(nc.sbuf_tensor("wfc1", [P, NCH, P], BF16, side="right"))
                    wa_bf = _es.enter_context(nc.sbuf_tensor("wa_bf", [P, NCH, P], BF16, side="right"))
                    wb_bf = _es.enter_context(nc.sbuf_tensor("wb_bf", [P, D], BF16, side="right"))
                    wr_bf = _es.enter_context(nc.sbuf_tensor("wr_bf", [P, NCH, 8], BF16, side="right"))
                    e8_bf = _es.enter_context(nc.sbuf_tensor("e8_bf", [8, P], BF16, side="right"))
                    idn_f4 = _es.enter_context(nc.sbuf_tensor("idn_f4", [P, P], F32, side="right"))
                    ax_sb = _es.enter_context(nc.sbuf_tensor("ax_sb", [P, TBW], BF16, side="right"))
                    sc_sb = _es.enter_context(nc.sbuf_tensor("sc_sb", [P, TBW], BF16, side="right"))
                    we8 = _es.enter_context(nc.sbuf_tensor("we8", [P, 4 * P + 64], F32, side="right"))
                    w8t = _es.enter_context(nc.sbuf_tensor("w8t", [8, TBW], BF16, side="right"))
                    tcol = _es.enter_context(nc.sbuf_tensor("tcol", [P, 8], F32, side="right"))
                    lg_sb = _es.enter_context(nc.sbuf_tensor("lg_sb", [P, 32], F32, side="right"))
                    rr4 = _es.enter_context(nc.sbuf_tensor("rr4", [P, 2, TBW], BF16, side="right"))
                    rm4 = _es.enter_context(nc.sbuf_tensor("rm4", [P, 2, TBW], BF16, side="right"))
                    r4row = _es.enter_context(nc.sbuf_tensor("r4row", [1, 2, TBW], BF16, side="right"))
                    m4row = _es.enter_context(nc.sbuf_tensor("m4row", [1, 2, TBW], BF16, side="right"))
                    tt4 = _es.enter_context(nc.sbuf_tensor("tt4", [P, TBW], F32, side="right"))
                    ps_z4 = _es.enter_context(nc.psum_tensor("ps_z4", [P, 2 * TBW], F32))
                    ps_u4 = _es.enter_context(nc.psum_tensor("ps_u4", [P, 2 * TBW], F32))
                    ps_lg = _es.enter_context(nc.psum_tensor("ps_lg", [P, 32], F32))
                    ps_w4 = _es.enter_context(nc.psum_tensor("ps_w4", [P, TBW], F32))
                    ps_st4 = _es.enter_context(nc.psum_tensor("ps_st4", [1, 2 * TBW], F32))
                    wfc = [wfc0, wfc1]

                    def p4_g0(e):
                        entry(e)
                        dma(e, sC, wa_bf[:], wa_d.rearrange("(c p) o -> p c o", p=P))
                        dma(e, sC, wb_bf[:], wb_d)
                        dma(e, sC, wr_bf[:], wr_d.rearrange("(c p) o -> p c o", p=P))
                        dma(e, sC, e8_bf[:], e8_d)
                        for k in ("wa", "wb", "wr", "e8"):
                            ev[k] = (sC, sC.n)

                    def p4_sy0(e):
                        entry(e)
                        ev["idn4"] = dma(e, sCs, idn_f4[:], idn_d)

                    def p4_v_init(e):
                        entry(e)
                        dve2.inc(e.memset(we8[:], 0.0))
                    G(p4_g0)
                    SY(p4_sy0)
                    V(p4_v_init)

                    def p4_headmid(ob):
                        par = ob % 2

                        def p4_g1(e, ob=ob, par=par):
                            if ob >= 2:
                                e.wait_ge(dve2.h, ev[("res4", ob - 2)])
                                e.wait_ge(pe1.h, ev[("st4", ob - 2)])
                            t0 = ob * TBW
                            ev[("x1l", ob)] = dma(e, sX1I, x1b[:, par, :, :],
                                                  x1_scr[:, :, t0:t0 + TBW])
                            if ob == 0:
                                wt(e, ev["wr"])

                        def p4_v0(e, ob=ob, par=par):
                            if ob == 0:
                                entry(e)
                            wt(e, ev[("x1l", ob)])
                            if ob > 0:
                                e.wait_ge(pe1.h, ev[("st4", ob - 1)])
                            for c in range(NCH):
                                tt = e.tensor_mul(sq4[:, c, :], x1b[:, par, c, :],
                                                  x1b[:, par, c, :])
                            ev[("x1sq", ob)] = dve1.inc(tt)

                        def p4_t0(e, ob=ob, par=par):
                            if ob == 0:
                                entry(e)
                            wt(e, ev[("x1l", ob)])
                            if ob == 1:
                                e.wait_ge(act1.h, ev[("math4", 0)]["mu"])
                            elif ob >= 2:
                                e.wait_ge(act1.h, ev[("math4b", ob - 2)]["mu"])
                            for c in range(NCH):
                                e.matmul(ps_st4[0:1, 0:TBW], ones_red[:, 0:1],
                                         x1b[:, par, c, :], start=(c == 0),
                                         stop=(c == NCH - 1))
                            e.wait_ge(dve1.h, ev[("x1sq", ob)])
                            for c in range(NCH):
                                mm = e.matmul(ps_st4[0:1, TBW:2 * TBW],
                                              ones_red[:, 0:1], sq4[:, c, :],
                                              start=(c == 0), stop=(c == NCH - 1))
                            ev[("st4", ob)] = pe1.inc(mm)

                        G(p4_g1)
                        V(p4_v0)
                        T(p4_t0)

                        ev[("math4", ob)] = {}
                        pm = (ev[("math4b", ob - 2)]["mur"] if ob >= 2
                              else (ev[("math4", 0)]["mur"] if ob == 1 else None))
                        stats_math(ps_st4, par, r4row[0:1, par, :],
                                   m4row[0:1, par, :], ev[("st4", ob)], pm,
                                   ev[("math4", ob)])
                        md4 = ev[("math4", ob)]

                        def p4_t1(e, ob=ob, par=par, md=md4):
                            e.wait_ge(act1.h, md["r"])
                            if ob > 0:
                                e.wait_ge(act1.h, ev[("w8tev", ob - 1, 3)])
                            if ob >= 2:
                                e.wait_ge(dve2.h, ev[("res4", ob - 2)])
                            mm = e.matmul(ps_w4[:, 0:TBW], oneB[0:1, :],
                                          r4row[0:1, par, :], start=True,
                                          stop=True)
                            ev[("rr4mm", ob)] = pe1.inc(mm)

                        def p4_a3(e, ob=ob, par=par):
                            e.wait_ge(pe1.h, ev[("rr4mm", ob)])
                            cp = e.activation(rr4[:, par, :], ps_w4[:, 0:TBW],
                                              AF.Copy, bias=0.0, scale=1.0)
                            ev[("rr4ev", ob)] = act2.inc(cp)

                        def p4_t2(e, ob=ob, par=par, md=md4):
                            e.wait_ge(dve2.h, md["mur"])
                            e.wait_ge(act2.h, ev[("rr4ev", ob)])
                            mm = e.matmul(ps_w4[:, 0:TBW], oneB[0:1, :],
                                          m4row[0:1, par, :], start=True,
                                          stop=True)
                            ev[("rm4mm", ob)] = pe1.inc(mm)

                        def p4_a4(e, ob=ob, par=par):
                            e.wait_ge(pe1.h, ev[("rm4mm", ob)])
                            cp = e.activation(rm4[:, par, :], ps_w4[:, 0:TBW],
                                              AF.Copy, bias=0.0, scale=1.0)
                            ev[("rm4ev", ob)] = act2.inc(cp)

                        def p4_v3(e, ob=ob, par=par):
                            e.wait_ge(act2.h, ev[("rm4ev", ob)])
                            if ob >= 2:
                                e.wait_ge(pe1.h, ev[("z4mm", ob - 2, NCH - 1)])
                            for c in range(NCH):
                                ed(e, dve2, e.tensor_mul(ln2[:, par, c, :],
                                                         x1b[:, par, c, :],
                                                         rr4[:, par, :]))
                                tt = e.scalar_tensor_tensor(
                                    ln2[:, par, c, :], rm4[:, par, :], -1.0,
                                    ln2[:, par, c, :], ALU.mult, ALU.add)
                                if c < NCH - 1:
                                    ed(e, dve2, tt)
                            ev[("ln2", ob)] = dve1.inc(tt)
                        T(p4_t1)
                        A(p4_a3)
                        T(p4_t2)
                        A(p4_a4)
                        V(p4_v3)

                        for si in range(4):
                            ev[("lgev", ob, si)] = act2.pre()

                        def p4_t3(e, ob=ob, par=par):
                            e.wait_ge(dve1.h, ev[("ln2", ob)])
                            if ob == 0:
                                wt(e, ev["wa"])
                            else:
                                e.wait_ge(dve1.h, ev[("sc", ob - 1)])
                                e.wait_ge(dve2.h, ev[("top2", ob - 1, 3)])
                            for c in range(NCH):
                                mm = e.matmul(ps_u4[:, 0:TBW], wa_bf[:, c, :],
                                              ln2[:, par, c, :], start=(c == 0),
                                              stop=(c == NCH - 1))
                            ev[("axmm", ob)] = pe1.inc(mm)
                            for si in range(4):
                                if si > 0:
                                    e.wait_ge(act2.h, ev[("lgev", ob, si - 1)])
                                elif ob > 0:
                                    e.wait_ge(act2.h, ev[("lgev", ob - 1, 3)])
                                for c in range(NCH):
                                    mm = e.matmul(ps_lg[:, si * 8:(si + 1) * 8],
                                                  ln2[:, par, c,
                                                      si * P:(si + 1) * P],
                                                  wr_bf[:, c, :], start=(c == 0),
                                                  stop=(c == NCH - 1))
                                ev[("lg", ob, si)] = pe1.inc(mm)

                        def p4_a5b(e, ob=ob):
                            for si in range(4):
                                e.wait_ge(pe1.h, ev[("lg", ob, si)])
                                if si == 0 and ob > 0:
                                    e.wait_ge(dve2.h, ev[("lgdone", ob - 1, 3)])
                                cp = e.activation(lg_sb[:, si * 8:(si + 1) * 8],
                                                  ps_lg[:, si * 8:(si + 1) * 8],
                                                  AF.Copy, bias=0.0, scale=1.0)
                                cp.then_inc(act2.h, 1)

                        def p4_a5(e, ob=ob):
                            e.wait_ge(pe1.h, ev[("axmm", ob)])
                            if ob > 0:
                                e.wait_ge(dve1.h, ev[("sc", ob - 1)])
                            cp = e.activation(ax_sb[:], ps_u4[:, 0:TBW], AF.Copy,
                                              bias=0.0, scale=1.0)
                            ev[("axev", ob)] = act1.inc(cp)
                        T(p4_t3)
                        A(p4_a5b)
                        A(p4_a5)

                        for si in range(4):
                            ev[("eexp", ob, si)] = act2.pre()

                        def p4_v4(e, ob=ob):
                            for si in range(4):
                                e.wait_ge(act2.h, ev[("lgev", ob, si)])
                                lg = lg_sb[:, si * 8:(si + 1) * 8]
                                m1 = tcol[:, 0:1]
                                m2 = tcol[:, 1:2]
                                dm = tcol[:, 2:3]
                                w1 = tcol[:, 3:4]
                                w2 = tcol[:, 4:5]
                                eecol = tcol[:, 5:6]
                                eq1 = we8[:, 512:520]
                                lgm = we8[:, 520:528]
                                eq2 = we8[:, 528:536]
                                ed(e, dve2, e.reduce_max(m1, lg, axis=AX))
                                ed(e, dve2, e.tensor_scalar(eq1, lg, m1, None,
                                                            ALU.is_equal))
                                ed(e, dve2, e.scalar_tensor_tensor(
                                    lgm, eq1, -1e30, lg, ALU.mult, ALU.add))
                                ed(e, dve2, e.reduce_max(m2, lgm, axis=AX))
                                ed(e, dve2, e.tensor_scalar(eq2, lgm, m2, None,
                                                            ALU.is_equal))
                                tt = e.tensor_sub(dm, m2, m1)
                                ev[("dm", ob, si)] = dve2.inc(tt)
                                ev[("lgdone", ob, si)] = ev[("dm", ob, si)]
                                e.wait_ge(act2.h, ev[("eexp", ob, si)])
                                ed(e, dve2, e.tensor_scalar(w1, eecol, 1.0, None,
                                                            ALU.add))
                                ed(e, dve2, e.reciprocal(w1, w1))
                                ed(e, dve2, e.tensor_scalar(w1, w1, 2.0, None,
                                                            ALU.mult))
                                ed(e, dve2, e.tensor_scalar(w2, w1, -1.0, 2.0,
                                                            ALU.mult, ALU.add))
                                ed(e, dve2, e.tensor_scalar(
                                    we8[:, si * P:si * P + 8], eq1, w1, None,
                                    ALU.mult))
                                tt = e.scalar_tensor_tensor(
                                    we8[:, si * P:si * P + 8], eq2, w2,
                                    we8[:, si * P:si * P + 8], ALU.mult, ALU.add)
                                ev[("top2", ob, si)] = dve2.inc(tt)

                        def p4_a6(e, ob=ob):
                            for si in range(4):
                                e.wait_ge(dve2.h, ev[("dm", ob, si)])
                                if si > 0:
                                    e.wait_ge(dve2.h, ev[("top2", ob, si - 1)])
                                elif ob > 0:
                                    e.wait_ge(dve2.h, ev[("top2", ob - 1, 3)])
                                x = e.activation(tcol[:, 5:6], tcol[:, 2:3],
                                                 AF.Exp, bias=0.0, scale=1.0)
                                x.then_inc(act2.h, 1)
                        V(p4_v4)
                        A(p4_a6)

                        for si in range(4):
                            ev[("w8tev", ob, si)] = act1.pre()

                        def p4_t4(e, ob=ob):
                            for si in range(4):
                                e.wait_ge(dve2.h, ev[("top2", ob, si)])
                                if si == 0:
                                    e.wait_ge(act2.h, ev[("rm4ev", ob)])
                                    wt(e, ev["idn4"])
                                    if ob > 0:
                                        e.wait_ge(act1.h,
                                                  ev[("w8tev", ob - 1, 3)])
                                else:
                                    e.wait_ge(act1.h, ev[("w8tev", ob, si - 1)])
                                tr = e.transpose(ps_w4[:, 0:P],
                                                 we8[:, si * P:(si + 1) * P],
                                                 idn_f4[:])
                                ev[("w8tr", ob, si)] = pe2.inc(tr)

                        def p4_a7(e, ob=ob):
                            for si in range(4):
                                e.wait_ge(pe2.h, ev[("w8tr", ob, si)])
                                cp = e.activation(w8t[0:8, si * P:(si + 1) * P],
                                                  ps_w4[0:8, 0:P], AF.Copy,
                                                  bias=0.0, scale=1.0)
                                cp.then_inc(act1.h, 1)

                        def p4_t5(e, ob=ob):
                            e.wait_ge(act1.h, ev[("w8tev", ob, 3)])
                            if ob == 0:
                                wt(e, ev["e8"])
                            mm = e.matmul(ps_u4[:, TBW:2 * TBW], e8_bf[:], w8t[:],
                                          start=True, stop=True)
                            ev[("wxmm", ob)] = pe2.inc(mm)

                        def p4_v5(e, ob=ob):
                            e.wait_ge(pe2.h, ev[("wxmm", ob)])
                            e.wait_ge(act1.h, ev[("axev", ob)])
                            tt = e.tensor_mul(sc_sb[:], ax_sb[:],
                                              ps_u4[:, TBW:2 * TBW])
                            ev[("sc", ob)] = dve1.inc(tt)
                        T(p4_t4)
                        A(p4_a7)
                        T(p4_t5)
                        V(p4_v5)

                        for oc in range(NCH):
                            ev[("z4mm", ob, oc)] = pe1.pre()
                            ev[("z4ev", ob, oc)] = act1.pre()

                        def p4_g1b(e, ob=ob):
                            for oc in range(NCH):
                                pc = (ob, oc - 2) if oc >= 2 else (ob - 1, oc + 14)
                                if ("z4mm",) + pc in ev:
                                    e.wait_ge(pe1.h, ev[("z4mm",) + pc])
                                o0 = oc * P
                                ev[("wf", ob, oc)] = dma(
                                    e, sWF[oc % 2], wfc[oc % 2][:],
                                    wf_scr[0:D, o0:o0 + P].rearrange(
                                        "(c p) o -> p c o", p=P))
                        SY(p4_g1b)

                        def p4_t6(e, ob=ob, par=par):
                            e.wait_ge(dve1.h, ev[("sc", ob)])
                            if ob == 0:
                                wt(e, ev["wb"])
                            for oc in range(NCH):
                                wt(e, ev[("wf", ob, oc)])
                                pc = (ob, oc - 2) if oc >= 2 else (ob - 1, oc + 14)
                                if ("z4ev",) + pc in ev:
                                    e.wait_ge(act1.h, ev[("z4ev",) + pc])
                                lo = (oc % 2) * TBW
                                for c in range(NCH):
                                    e.matmul(ps_z4[:, lo:lo + TBW],
                                             wfc[oc % 2][:, c, :],
                                             ln2[:, par, c, :],
                                             start=(c == 0), stop=False)
                                o0 = oc * P
                                mm = e.matmul(ps_z4[:, lo:lo + TBW],
                                              wb_bf[:, o0:o0 + P], sc_sb[:],
                                              start=False, stop=True)
                                mm.then_inc(pe1.h, 1)

                        def p4_a8(e, ob=ob, par=par):
                            for oc in range(NCH):
                                e.wait_ge(pe1.h, ev[("z4mm", ob, oc)])
                                if oc == 0 and ob >= 2:
                                    e.wait_ge(dve2.h, ev[("res4", ob - 2)])
                                    e.wait_ge(pe1.h, ev[("st4b", ob - 2)])
                                lo = (oc % 2) * TBW
                                cp = e.activation(z2[:, par, oc, :],
                                                  ps_z4[:, lo:lo + TBW], AF.Copy,
                                                  bias=0.0, scale=1.0)
                                cp.then_inc(act1.h, 1)
                        T(p4_t6)
                        A(p4_a8)

                    def p4_tail(ob):
                        par = ob % 2

                        def p4_v6(e, ob=ob, par=par):
                            e.wait_ge(act1.h, ev[("z4ev", ob, NCH - 1)])
                            if ob > 0:
                                e.wait_ge(pe1.h, ev[("st4b", ob - 1)])
                            for c in range(NCH):
                                tt = e.tensor_mul(sq4b[:, c, :], z2[:, par, c, :],
                                                  z2[:, par, c, :])
                            ev[("z4sq", ob)] = dve1.inc(tt)

                        def p4_t7(e, ob=ob, par=par):
                            e.wait_ge(act1.h, ev[("z4ev", ob, NCH - 1)])
                            if ob + 1 < NOB:
                                e.wait_ge(act1.h, ev[("math4", ob + 1)]["mu"])
                            elif ob > 0:
                                e.wait_ge(act1.h, ev[("math4b", ob - 1)]["mu"])
                            for c in range(NCH):
                                e.matmul(ps_st4[0:1, 0:TBW], ones_red[:, 0:1],
                                         z2[:, par, c, :], start=(c == 0),
                                         stop=(c == NCH - 1))
                            e.wait_ge(dve1.h, ev[("z4sq", ob)])
                            for c in range(NCH):
                                mm = e.matmul(ps_st4[0:1, TBW:2 * TBW],
                                              ones_red[:, 0:1], sq4b[:, c, :],
                                              start=(c == 0), stop=(c == NCH - 1))
                            ev[("st4b", ob)] = pe1.inc(mm)
                        V(p4_v6)
                        T(p4_t7)

                        ev[("math4b", ob)] = {}
                        pmb = (ev[("math4b", ob - 1)]["mur"] if ob >= 1
                               else ev[("math4", 0)]["mur"])
                        stats_math(ps_st4, par, r4row[0:1, par, :],
                                   m4row[0:1, par, :], ev[("st4b", ob)], pmb,
                                   ev[("math4b", ob)])
                        md4b = ev[("math4b", ob)]

                        def p4_t8(e, ob=ob, par=par, md=md4b):
                            e.wait_ge(act1.h, md["r"])
                            e.wait_ge(act1.h, ev[("w8tev", ob, 3)])
                            mm = e.matmul(ps_w4[:, 0:TBW], oneB[0:1, :],
                                          r4row[0:1, par, :], start=True,
                                          stop=True)
                            ev[("rr4bmm", ob)] = pe1.inc(mm)

                        def p4_a11(e, ob=ob, par=par):
                            e.wait_ge(pe1.h, ev[("rr4bmm", ob)])
                            cp = e.activation(rr4[:, par, :], ps_w4[:, 0:TBW],
                                              AF.Copy, bias=0.0, scale=1.0)
                            ev[("rr4bev", ob)] = act2.inc(cp)

                        def p4_t9(e, ob=ob, par=par, md=md4b):
                            e.wait_ge(dve2.h, md["mur"])
                            e.wait_ge(act2.h, ev[("rr4bev", ob)])
                            mm = e.matmul(ps_w4[:, 0:TBW], oneB[0:1, :],
                                          m4row[0:1, par, :], start=True,
                                          stop=True)
                            ev[("rm4bmm", ob)] = pe1.inc(mm)

                        def p4_a12(e, ob=ob, par=par):
                            e.wait_ge(pe1.h, ev[("rm4bmm", ob)])
                            cp = e.activation(rm4[:, par, :], ps_w4[:, 0:TBW],
                                              AF.Copy, bias=0.0, scale=1.0)
                            ev[("rm4bev", ob)] = act2.inc(cp)

                        def p4_v9(e, ob=ob, par=par):
                            e.wait_ge(act2.h, ev[("rm4bev", ob)])
                            for c in range(NCH):
                                pc = (ob, c - 2) if c >= 2 else (ob - 1, c + 14)
                                if ("x2dmac",) + pc in ev:
                                    wt(e, ev[("x2dmac",) + pc])
                                ed(e, dve2, e.tensor_mul(tt4[:], z2[:, par, c, :],
                                                         rr4[:, par, :]))
                                ed(e, dve2, e.scalar_tensor_tensor(
                                    tt4[:], rm4[:, par, :], -1.0, tt4[:],
                                    ALU.mult, ALU.add))
                                tt = e.tensor_add(x2st[:, c % 2, :], tt4[:],
                                                  x1b[:, par, c, :])
                                ev[("res4c", ob, c)] = dve2.inc(tt)
                                e.wait_ge(dve2.h, dve2.n)
                            ev[("res4", ob)] = ev[("res4c", ob, NCH - 1)]

                        def p4_sy2(e, ob=ob):
                            t0 = ob * TBW
                            for c in range(NCH):
                                e.wait_ge(dve2.h, ev[("res4c", ob, c)])
                                e.dma_start(
                                    out_d[:, c, t0:t0 + TBW],
                                    x2st[:, c % 2, :]).then_inc(sX2O[c % 2].h, 16)
                        T(p4_t8)
                        A(p4_a11)
                        T(p4_t9)
                        A(p4_a12)
                        V(p4_v9)
                        SY(p4_sy2)

                    for ob in range(NOB):
                        for c in range(NCH):
                            ct = sX2O[c % 2]
                            ev[("x2dmac", ob, c)] = (ct, ct.pre(16))
                        p4_headmid(ob)
                        if ob >= 1:
                            p4_tail(ob - 1)
                    p4_tail(NOB - 1)

        def fin(e):
            for c in allct:
                e.wait_ge(c.h, c.n)
        SY(fin)
    return nc


def own_cols(c):
    cols = []
    for b in range(B):
        for t in range(8):
            a = b * S + (c + 8 * t) * P
            cols.extend(range(a, a + P))
    return np.array(cols)


def host_prep_shared(inputs):
    f = np.float32
    x = np.asarray(inputs["L0_attn_x"], f)
    wqkv = np.asarray(inputs["L0_attn_W_qkv"], f)
    sh = {}
    sh["xt_full"] = np.ascontiguousarray(x.reshape(TOK, D).T)
    sh["wq"] = np.ascontiguousarray(wqkv[:, 0:D])
    sh["wk"] = np.ascontiguousarray(wqkv[:, D:D + HD])
    sh["wv"] = np.ascontiguousarray(wqkv[:, D + HD:D + 2 * HD])
    sh["brow"] = np.asarray(inputs["L0_attn_b_qkv"], f).reshape(1, -1)
    sh["wproj"] = np.ascontiguousarray(np.asarray(inputs["L0_attn_W_proj"], f))
    sh["bprow"] = np.asarray(inputs["L0_attn_b_proj"], f).reshape(1, -1)
    sh["wffn"] = np.ascontiguousarray(np.asarray(inputs["L0_ffn_weight"], f).T)
    sh["wa"] = np.ascontiguousarray(
        np.asarray(inputs["L0_ffn_lora_A"], f).reshape(128, D).T)
    sh["wb"] = np.ascontiguousarray(
        np.asarray(inputs["L0_ffn_lora_B"], f).transpose(0, 2, 1).reshape(128, D))
    sh["wr"] = np.ascontiguousarray(np.asarray(inputs["L0_ffn_router_w"], f).T)
    e8m = np.zeros((8, 128), f)
    for e in range(8):
        e8m[e, e * 16:(e + 1) * 16] = 1.0
    sh["e8"] = e8m
    sh["idn"] = np.eye(P, dtype=f)
    return sh


def host_prep_core(sh, c):
    m = {k: v for k, v in sh.items() if k != "xt_full"}
    cols = own_cols(c)
    m["xt_own"] = np.ascontiguousarray(sh["xt_full"][:, cols])
    dltv = 128.0 * (c - np.arange(8, dtype=np.float32))
    m["dlt"] = np.ascontiguousarray(
        np.broadcast_to(dltv[None, :], (P, 8)).astype(np.float32))
    return m


def host_gather(results):
    full = np.empty((TOK, D), np.float32)
    for c, r in enumerate(results):
        o = np.asarray(r["out"])
        xt = o.transpose(1, 0, 2).reshape(D, OWN)
        full[own_cols(c), :] = xt.T
    return full.reshape(B, S, D)


def kernel(**inputs):
    from concourse.bass_utils import run_bass_kernel_spmd
    sh = host_prep_shared(inputs)
    in_maps = [host_prep_core(sh, c) for c in range(8)]
    nc = build(nphase=5, debug=False)
    res = run_bass_kernel_spmd(nc, in_maps, core_ids=list(range(8)))
    return host_gather(res.results)


# version 07
# speedup vs baseline: 1.0175x; 1.0000x over previous
import numpy as np
from contextlib import ExitStack

import concourse.bass as bass
import concourse.mybir as mybir

F32 = mybir.dt.float32
BF16 = mybir.dt.bfloat16
AF = mybir.ActivationFunctionType
ALU = mybir.AluOpType
AX = mybir.AxisListType.X

P = 128
D = 2048
NCH = 16
NH = 16
HD = 128
B = 2
S = 8192
TOK = B * S
TBW = 512
NTB = TOK // TBW
OWN = 2048
NOB = 4
NJT = 64
EPS = 1e-5
SCALE = float(HD) ** -0.5
NBH = B * NH


class Ct:
    def __init__(self, stack, nc, name):
        self.h = stack.enter_context(nc.semaphore(name))
        self.n = 0

    def inc(self, instr, k=1):
        instr.then_inc(self.h, k)
        self.n += k
        return self.n

    def pre(self, k=1):
        self.n += k
        return self.n


def build(nphase=5, debug=False):
    nc = bass.Bass("TRN2", target_bir_lowering=False)

    xt_own = nc.dram_tensor("xt_own", [D, OWN], F32, kind="ExternalInput").ap()
    wq_d = nc.dram_tensor("wq", [D, D], F32, kind="ExternalInput").ap()
    wk_d = nc.dram_tensor("wk", [D, HD], F32, kind="ExternalInput").ap()
    wv_d = nc.dram_tensor("wv", [D, HD], F32, kind="ExternalInput").ap()
    brow_d = nc.dram_tensor("brow", [1, D + 2 * HD], F32, kind="ExternalInput").ap()
    wproj_d = nc.dram_tensor("wproj", [D, D], F32, kind="ExternalInput").ap()
    bprow_d = nc.dram_tensor("bprow", [1, D], F32, kind="ExternalInput").ap()
    wffn_d = nc.dram_tensor("wffn", [D, D], F32, kind="ExternalInput").ap()
    wa_d = nc.dram_tensor("wa", [D, 128], F32, kind="ExternalInput").ap()
    wb_d = nc.dram_tensor("wb", [128, D], F32, kind="ExternalInput").ap()
    wr_d = nc.dram_tensor("wr", [D, 8], F32, kind="ExternalInput").ap()
    dlt_d = nc.dram_tensor("dlt", [P, 8], F32, kind="ExternalInput").ap()
    e8_d = nc.dram_tensor("e8", [8, 128], F32, kind="ExternalInput").ap()
    idn_d = nc.dram_tensor("idn", [P, P], F32, kind="ExternalInput").ap()

    out_d = nc.dram_tensor("out", [P, NCH, OWN], F32, kind="ExternalOutput").ap()
    qt_scr = nc.dram_tensor("qt_scr", [NH * P, OWN], BF16, kind="Internal").ap()
    kag_in = nc.dram_tensor("kag_in", [P, OWN], BF16, kind="Internal").ap()
    wp_scr = nc.dram_tensor("wp_scr", [D, D], BF16, kind="Internal").ap()
    wf_scr = nc.dram_tensor("wf_scr", [D, D], BF16, kind="Internal").ap()
    vag_in = nc.dram_tensor("vag_in", [OWN, P], BF16, kind="Internal").ap()
    kag_out = nc.dram_tensor("kag_out", [8 * P, OWN], BF16, kind="Internal").ap()
    vag_out = nc.dram_tensor("vag_out", [8 * OWN, P], BF16, kind="Internal").ap()
    x1_scr = nc.dram_tensor("x1_scr", [P, NCH, OWN], F32, kind="Internal").ap()

    dbg = {}
    if debug:
        dbg["kt"] = nc.dram_tensor("dbg_kt", [P, TOK], BF16, kind="ExternalOutput").ap()
        dbg["va"] = nc.dram_tensor("dbg_va", [P, B * NJT * 129], BF16, kind="ExternalOutput").ap()
        dbg["rows"] = nc.dram_tensor("dbg_rows", [2, TOK], BF16, kind="ExternalOutput").ap()
        if nphase >= 2:
            dbg["qt"] = nc.dram_tensor("dbg_qt", [NH * P, OWN], BF16, kind="ExternalOutput").ap()
            dbg["orows"] = nc.dram_tensor("dbg_orows", [2, OWN], BF16, kind="ExternalOutput").ap()
        if nphase >= 3:
            dbg["oall"] = nc.dram_tensor("dbg_oall", [P, NH, OWN], BF16, kind="ExternalOutput").ap()
        if nphase >= 4:
            dbg["x1"] = nc.dram_tensor("dbg_x1", [P, NCH, OWN], F32, kind="ExternalOutput").ap()

    st = ExitStack()
    with st:
        ec = st.enter_context

        def sbl(name, shape, dt):
            return ec(nc.sbuf_tensor(name, shape, dt, side="left"))

        ones_red = sbl("ones_red", [P, P], BF16)
        onesrow = sbl("onesrow", [1, TBW], BF16)
        negcol = sbl("negcol", [P, 1], BF16)
        oneB = sbl("oneB", [1, P], BF16)
        idn_f = sbl("idn_f", [P, P], F32)
        idn_bf = sbl("idn_bf", [P, P], BF16)
        dD = sbl("dD", [P, P], F32)
        dlt_sb = sbl("dlt_sb", [P, 8], F32)
        tmask = sbl("tmask", [P, 8, P], BF16)
        robf = sbl("robf", [1, OWN], BF16)
        mrobf = sbl("mrobf", [1, OWN], BF16)
        brow_sb = sbl("brow_sb", [1, D + 2 * HD], BF16)
        mu_t = sbl("mu_t", [1, 2, TBW], F32)
        msq_t = sbl("msq_t", [1, 2, TBW], F32)
        rf_t = sbl("rf_t", [1, 2, TBW], F32)
        scrap = sbl("scrap", [1, 8], F32)
        epscol = sbl("epscol", [P, 1], F32)

        sB = Ct(st, nc, "sB")
        sC = Ct(st, nc, "sC")
        sCs = Ct(st, nc, "sCs")
        sDBG = Ct(st, nc, "sDBG")
        sXB = [Ct(st, nc, f"sXB{i}") for i in range(2)]
        sXO = Ct(st, nc, "sXO")
        sWQ = [Ct(st, nc, f"sWQ{i}") for i in range(2)]
        sQO = [Ct(st, nc, f"sQO{i}") for i in range(2)]
        sQI = [Ct(st, nc, f"sQI{i}") for i in range(2)]
        sWP = [Ct(st, nc, f"sWP{i}") for i in range(2)]
        sWF = [Ct(st, nc, f"sWF{i}") for i in range(2)]
        sXOF = [Ct(st, nc, f"sXOF{i}") for i in range(2)]
        sAGB = Ct(st, nc, "sAGB")
        sPWI = [Ct(st, nc, f"sPWI{i}") for i in range(2)]
        sPWO = [Ct(st, nc, f"sPWO{i}") for i in range(2)]
        sCC = Ct(st, nc, "sCC")
        sKV = Ct(st, nc, "sKV")
        sX1O = [Ct(st, nc, f"sX1O{i}") for i in range(2)]
        sX1I = Ct(st, nc, "sX1I")
        sX2O = [Ct(st, nc, f"sX2O{i}") for i in range(2)]
        pe1 = Ct(st, nc, "pe1")
        pe2 = Ct(st, nc, "pe2")
        act1 = Ct(st, nc, "act1")
        act2 = Ct(st, nc, "act2")
        dve1 = Ct(st, nc, "dve1")
        dve2 = Ct(st, nc, "dve2")
        dveE = Ct(st, nc, "dveE")
        allct = ([sC, sCs, sDBG, sXO, sAGB, sCC, sKV, sX1I] + sPWI + sPWO +
                 sXOF + sX2O +
                 sWQ + sQO + sQI +
                 sWP + sWF + sX1O + [pe1, pe2, act1, act2, dve1, dve2, dveE])

        def ed(e, ct, instr):
            ct.inc(instr)
            e.wait_ge(ct.h, ct.n)

        def dma(e, ct, out, in_):
            v = ct.inc(e.dma_start(out, in_), 16)
            return (ct, v)

        def wt(e, evt):
            e.wait_ge(evt[0].h, evt[1])

        block = ec(nc.Block())
        T = block.tensor
        V = block.vector
        A = block.scalar
        G = block.gpsimd
        SY = block.sync

        def entry(e):
            e.wait_ge(sB.h, sB.n)

        def barrier():
            vals = [(c, c.n) for c in allct]

            def bt(e):
                for c, v in vals:
                    e.wait_ge(c.h, v)
                sB.inc(e.ldweights(ones_red[:, 0:1]))

            def bv(e):
                for c, v in vals:
                    e.wait_ge(c.h, v)
                sB.inc(e.tensor_copy(scrap[0:1, 0:2], scrap[0:1, 2:4]))

            def ba(e):
                for c, v in vals:
                    e.wait_ge(c.h, v)
                sB.inc(e.activation(scrap[0:1, 4:6], scrap[0:1, 6:8], AF.Copy,
                                    bias=0.0, scale=1.0))
            T(bt)
            V(bv)
            A(ba)

        ev = {}

        def stats_math(ps, slot, rrow, murrow, ev_st, prev_mur, evout, W=TBW):
            def m_a1(e):
                e.wait_ge(pe1.h, ev_st)
                if prev_mur is not None:
                    e.wait_ge(dve2.h, prev_mur)
                e.activation(mu_t[0:1, slot, 0:W], ps[0:1, 0:W],
                             AF.Copy, bias=0.0, scale=1.0 / D)
                cp = e.activation(msq_t[0:1, slot, 0:W], ps[0:1, W:2 * W],
                                  AF.Copy, bias=0.0, scale=1.0 / D)
                evout["mu"] = act1.inc(cp)

            def m_v1(e):
                e.wait_ge(act1.h, evout["mu"])
                ed(e, dve2, e.tensor_mul(rf_t[0:1, slot, 0:W],
                                         mu_t[0:1, slot, 0:W],
                                         mu_t[0:1, slot, 0:W]))
                tt = e.tensor_sub(rf_t[0:1, slot, 0:W], msq_t[0:1, slot, 0:W],
                                  rf_t[0:1, slot, 0:W])
                evout["var"] = dve2.inc(tt)

            def m_a2(e):
                e.wait_ge(dve2.h, evout["var"])
                ed(e, act2, e.activation(rf_t[0:1, slot, 0:W],
                                         rf_t[0:1, slot, 0:W],
                                         AF.Ln, bias=epscol[0:1, 0:1], scale=1.0))
                ed(e, act2, e.activation(rf_t[0:1, slot, 0:W],
                                         rf_t[0:1, slot, 0:W],
                                         AF.Exp, bias=0.0, scale=-0.5))
                cp = e.activation(rrow, rf_t[0:1, slot, 0:W],
                                  AF.Copy, bias=0.0, scale=1.0)
                evout["r"] = act1.inc(cp)

            def m_v2(e):
                e.wait_ge(act1.h, evout["r"])
                tt = e.tensor_mul(murrow, mu_t[0:1, slot, 0:W],
                                  rf_t[0:1, slot, 0:W])
                evout["mur"] = dve2.inc(tt)
            A(m_a1)
            V(m_v1)
            A(m_a2)
            V(m_v2)

        def s_sy(e):
            dma(e, sCs, dlt_sb[:], dlt_d)
            dma(e, sCs, idn_f[:], idn_d)

        def s_g(e):
            dma(e, sC, brow_sb[:], brow_d)
            dma(e, sC, idn_bf[:], idn_d)
            dve2.inc(e.iota(dD[:], [[1, P]], channel_multiplier=-1,
                            allow_small_or_imprecise_dtypes=True))

        def s_v(e):
            e.memset(ones_red[:], 1.0)
            e.memset(onesrow[:], 1.0)
            e.memset(negcol[:], -1.0)
            e.memset(oneB[:], 1.0)
            e.memset(scrap[:], 1.0)
            e.memset(epscol[:], EPS)
            e.wait_ge(dve2.h, dve2.n)
            e.wait_ge(sCs.h, sCs.n)
            for r in range(8):
                dve1.inc(e.tensor_scalar(tmask[:, r, :], dD[:],
                                         dlt_sb[:, r:r + 1], 0.0,
                                         ALU.add, ALU.is_ge))
        SY(s_sy)
        G(s_g)
        V(s_v)
        barrier()

        es_o = ExitStack()
        if True:
            with ExitStack() as _es:
                kt_all = _es.enter_context(nc.sbuf_tensor("kt_all", [P, TOK], BF16, side="right"))
                v_all = _es.enter_context(nc.sbuf_tensor("v_all", [P, B * NJT, 129], BF16, side="right"))
                if nphase >= 2:
                    with ExitStack() as _es:
                        xo0 = _es.enter_context(nc.sbuf_tensor("xo0", [P, NCH, TBW], BF16, side="right"))
                        xo1 = _es.enter_context(nc.sbuf_tensor("xo1", [P, NCH, TBW], BF16, side="right"))
                        xo2 = _es.enter_context(nc.sbuf_tensor("xo2", [P, NCH, TBW], BF16, side="right"))
                        xo3 = _es.enter_context(nc.sbuf_tensor("xo3", [P, NCH, TBW], BF16, side="right"))
                        xsq = _es.enter_context(nc.sbuf_tensor("xsq", [P, NCH, TBW], BF16, side="right"))
                        wqc0 = _es.enter_context(nc.sbuf_tensor("wqc0", [P, NCH, P], BF16, side="right"))
                        wqc1 = _es.enter_context(nc.sbuf_tensor("wqc1", [P, NCH, P], BF16, side="right"))
                        rrown = _es.enter_context(nc.sbuf_tensor("rrown", [P, NOB, TBW], BF16, side="right"))
                        wnq = _es.enter_context(nc.sbuf_tensor("wnq", [1, P], BF16, side="right"))
                        qstage = _es.enter_context(nc.sbuf_tensor("qstage", [P, 2, TBW], BF16, side="right"))
                        wk_bf = _es.enter_context(nc.sbuf_tensor("wk_bf", [P, NCH, HD], BF16, side="right"))
                        wv_bf = _es.enter_context(nc.sbuf_tensor("wv_bf", [P, NCH, HD], BF16, side="right"))
                        wnkv = _es.enter_context(nc.sbuf_tensor("wnkv", [1, 2 * P], BF16, side="right"))
                        kstage = _es.enter_context(nc.sbuf_tensor("kstage", [P, NOB, TBW], BF16, side="right"))
                        vstage = _es.enter_context(nc.sbuf_tensor("vstage", [P, NCH, P], BF16, side="right"))
                        vtmp1 = _es.enter_context(nc.sbuf_tensor("vtmp1", [P, TBW], BF16, side="right"))
                        ps_q = _es.enter_context(nc.psum_tensor("ps_q", [P, 2 * TBW], F32))
                        ps_st1 = _es.enter_context(nc.psum_tensor("ps_st1", [1, 2 * TBW], F32))
                        ps_rr1 = _es.enter_context(nc.psum_tensor("ps_rr1", [P, TBW], F32))
                        ps_kv1 = _es.enter_context(nc.psum_tensor("ps_kv1", [P, 2 * TBW], F32))
                        ps_tr1 = _es.enter_context(nc.psum_tensor("ps_tr1", [P, P], BF16))
                        ps_w = ps_rr1
                        xol = [xo0, xo1, xo2, xo3]
                        wqc = [wqc0, wqc1]

                        def p1a_g(e):
                            entry(e)
                            for ob in range(NOB):
                                t0 = ob * TBW
                                dma(e, sXO, xol[ob][:],
                                    xt_own[0:D, t0:t0 + TBW].rearrange(
                                        "(c p) t -> p c t", p=P))
                            for ob in range(NOB):
                                ev[("xo", ob)] = (sXO, sXO.n)
                            dma(e, sC, wk_bf[:],
                                wk_d.rearrange("(c p) o -> p c o", p=P))
                            ev["wkv"] = dma(e, sC, wv_bf[:],
                                            wv_d.rearrange("(c p) o -> p c o", p=P))
                            ev[("wq", 0)] = dma(
                                e, sWQ[0], wqc[0][:],
                                wq_d[0:D, 0:P].rearrange("(c p) o -> p c o", p=P))
                        G(p1a_g)

                        ev[("wn_ev", 0)] = act2.pre()
                        ev[("wn_ev", 1)] = act2.pre()

                        def p01_wn_t(e):
                            entry(e)
                            wt(e, ev["wkv"])
                            for half in range(2):
                                if half == 1:
                                    e.wait_ge(act2.h, ev[("wn_ev", 0)])
                                w = wk_bf if half == 0 else wv_bf
                                for c in range(NCH):
                                    mm = e.matmul(ps_w[0:1, 0:P], negcol[:],
                                                  w[:, c, :], start=(c == 0),
                                                  stop=(c == NCH - 1))
                                ev[("wn", half)] = pe1.inc(mm)

                        def p01_wn_a(e):
                            entry(e)
                            for half in range(2):
                                e.wait_ge(pe1.h, ev[("wn", half)])
                                cp = e.activation(wnkv[0:1, half * P:(half + 1) * P],
                                                  ps_w[0:1, 0:P], AF.Copy,
                                                  bias=0.0, scale=1.0)
                                cp.then_inc(act2.h, 1)

                        def p01_ones_v(e):
                            entry(e)
                            dve2.inc(e.memset(v_all[:, :, 128:129], 1.0))
                        T(p01_wn_t)
                        A(p01_wn_a)
                        V(p01_ones_v)

                        for ob in range(NOB):
                            slot = ob % 2

                            def p1a_v1(e, ob=ob):
                                wt(e, ev[("xo", ob)])
                                if ob - 1 >= 0:
                                    e.wait_ge(pe1.h, ev[("ost", ob - 1)])
                                for c in range(NCH):
                                    tt = e.tensor_mul(xsq[:, c, :],
                                                      xol[ob][:, c, :],
                                                      xol[ob][:, c, :])
                                ev[("osq", ob)] = dve1.inc(tt)

                            def p1a_t1(e, ob=ob):
                                wt(e, ev[("xo", ob)])
                                if ob - 1 >= 0:
                                    e.wait_ge(act1.h,
                                              ev[("omath", ob - 1)]["mu"])
                                for c in range(NCH):
                                    e.matmul(ps_st1[0:1, 0:TBW], ones_red[:, 0:1],
                                             xol[ob][:, c, :], start=(c == 0),
                                             stop=(c == NCH - 1))
                                e.wait_ge(dve1.h, ev[("osq", ob)])
                                for c in range(NCH):
                                    mm = e.matmul(ps_st1[0:1, TBW:2 * TBW],
                                                  ones_red[:, 0:1], xsq[:, c, :],
                                                  start=(c == 0),
                                                  stop=(c == NCH - 1))
                                ev[("ost", ob)] = pe1.inc(mm)

                            V(p1a_v1)
                            T(p1a_t1)

                            ev[("omath", ob)] = {}
                            pm = ev[("omath", ob - 2)]["mur"] if ob >= 2 else None
                            t0ob = ob * TBW
                            stats_math(ps_st1, slot, robf[0:1, t0ob:t0ob + TBW],
                                       mrobf[0:1, t0ob:t0ob + TBW],
                                       ev[("ost", ob)], pm, ev[("omath", ob)])

                            def p1a_t2(e, ob=ob):
                                e.wait_ge(act1.h, ev[("omath", ob)]["r"])
                                if ob - 1 >= 0:
                                    e.wait_ge(act2.h, ev[("rrevo", ob - 1)])
                                else:
                                    e.wait_ge(act2.h, ev[("wn_ev", 1)])
                                t0 = ob * TBW
                                mm = e.matmul(ps_rr1[:, 0:TBW], oneB[0:1, :],
                                              robf[0:1, t0:t0 + TBW], start=True,
                                              stop=True)
                                ev[("rrmmo", ob)] = pe1.inc(mm)

                            def p1a_a2(e, ob=ob):
                                e.wait_ge(pe1.h, ev[("rrmmo", ob)])
                                cp = e.activation(rrown[:, ob, :], ps_rr1[:, 0:TBW],
                                                  AF.Copy, bias=0.0, scale=1.0)
                                ev[("rrevo", ob)] = act2.inc(cp)

                            T(p1a_t2)
                            A(p1a_a2)

                        for ob in range(NOB):
                            def p01_kv_t(e, ob=ob):
                                e.wait_ge(act2.h, ev[("rrevo", ob)])
                                if ob > 0:
                                    e.wait_ge(dve1.h, ev[("kvfix", ob - 1)])
                                t0 = ob * TBW
                                for half in range(2):
                                    w = wk_bf if half == 0 else wv_bf
                                    lo = half * TBW
                                    for c in range(NCH):
                                        e.matmul(ps_kv1[:, lo:lo + TBW],
                                                 w[:, c, :], xol[ob][:, c, :],
                                                 start=(c == 0), stop=False)
                                    boff = D + half * HD
                                    e.matmul(ps_kv1[:, lo:lo + TBW],
                                             brow_sb[0:1, boff:boff + P],
                                             onesrow[:], start=False, stop=False)
                                    mm = e.matmul(
                                        ps_kv1[:, lo:lo + TBW],
                                        wnkv[0:1, half * P:(half + 1) * P],
                                        mrobf[0:1, t0:t0 + TBW],
                                        start=False, stop=True)
                                ev[("kvmm", ob)] = pe2.inc(mm)

                            def p01_kv_v(e, ob=ob):
                                e.wait_ge(pe2.h, ev[("kvmm", ob)])
                                if ob > 0:
                                    e.wait_ge(pe2.h, ev[("vtr", ob - 1, 3)])
                                e.tensor_mul(kstage[:, ob, :],
                                             ps_kv1[:, 0:TBW], rrown[:, ob, :])
                                tt = e.tensor_mul(vtmp1[:],
                                                  ps_kv1[:, TBW:2 * TBW],
                                                  rrown[:, ob, :])
                                ev[("kvfix", ob)] = dve1.inc(tt)

                            for i in range(4):
                                ev[("vtrev", ob, i)] = act2.pre()

                            def p01_tr_t(e, ob=ob):
                                e.wait_ge(dve1.h, ev[("kvfix", ob)])
                                for i in range(4):
                                    prev = (ob, i - 1) if i > 0 else (ob - 1, 3)
                                    if ("vtrev",) + prev in ev:
                                        e.wait_ge(act2.h, ev[("vtrev",) + prev])
                                    tr = e.transpose(ps_tr1[:],
                                                     vtmp1[:, i * P:(i + 1) * P],
                                                     idn_bf[:])
                                    ev[("vtr", ob, i)] = pe2.inc(tr)

                            def p01_tr_a(e, ob=ob):
                                for i in range(4):
                                    e.wait_ge(pe2.h, ev[("vtr", ob, i)])
                                    cp = e.activation(
                                        vstage[:, ob * 4 + i, :], ps_tr1[:],
                                        AF.Copy, bias=0.0, scale=1.0)
                                    cp.then_inc(act2.h, 1)

                            T(p01_kv_t)
                            V(p01_kv_v)
                            T(p01_tr_t)
                            A(p01_tr_a)

                        def p01_ag_g(e):
                            e.wait_ge(dve1.h, ev[("kvfix", NOB - 1)])
                            dma(e, sAGB, kag_in,
                                kstage[:].rearrange("p a b -> p (a b)"))
                            e.wait_ge(act2.h, ev[("vtrev", NOB - 1, 3)])
                            dma(e, sAGB, vag_in.rearrange("(t p) c -> p t c", p=P),
                                vstage[:])
                            e.wait_ge(sAGB.h, sAGB.n)
                            cc = e.collective_compute(
                                "AllGather", mybir.AluOpType.bypass,
                                replica_groups=[list(range(8))],
                                ins=[kag_in.opt()], outs=[kag_out.opt()])
                            sCC.inc(cc)
                            cc = e.collective_compute(
                                "AllGather", mybir.AluOpType.bypass,
                                replica_groups=[list(range(8))],
                                ins=[vag_in.opt()], outs=[vag_out.opt()])
                            sCC.inc(cc)
                        G(p01_ag_g)

                        def p01_sc_sy(e):
                            e.wait_ge(sCC.h, sCC.n)
                            ktv = kt_all[:].rearrange(
                                "p (b t2 rr c) -> p b t2 rr c", b=B, t2=8, rr=8,
                                c=P)
                            vav = v_all[:].rearrange(
                                "p (b t2 rr) c -> p b t2 rr c", b=B, t2=8, rr=8)
                            for r in range(8):
                                src = kag_out[r * P:(r + 1) * P, :].rearrange(
                                    "p (b t c) -> p b t c", b=B, t=8, c=TBW // 4)
                                dma(e, sKV, ktv[:, :, :, r, :], src)
                                vsrc = vag_out[r * OWN:(r + 1) * OWN, :].rearrange(
                                    "(b t k) c -> k b t c", b=B, t=8, k=P)
                                dma(e, sKV, vav[:, :, :, r, 0:P], vsrc)
                        SY(p01_sc_sy)

                        for h in range(NH):
                            parh = h % 2

                            def p1b_g1(e, h=h):
                                if h + 1 < NH:
                                    if h - 1 >= 0:
                                        e.wait_ge(pe2.h,
                                                  ev[("qmm", h - 1, NOB - 1)])
                                    o0 = (h + 1) * P
                                    ev[("wq", h + 1)] = dma(
                                        e, sWQ[(h + 1) % 2], wqc[(h + 1) % 2][:],
                                        wq_d[0:D, o0:o0 + P].rearrange(
                                            "(c p) o -> p c o", p=P))

                            def p1b_t1(e, h=h, parh=parh):
                                wt(e, ev[("wq", h)])
                                if h == 0:
                                    e.wait_ge(dve2.h,
                                              ev[("omath", NOB - 1)]["mur"])
                                    e.wait_ge(act2.h, ev[("rrevo", NOB - 1)])
                                    e.wait_ge(pe2.h, ev[("kvmm", NOB - 1)])
                                if h > 0:
                                    e.wait_ge(act2.h, ev[("wnqev", h - 1)])
                                for c in range(NCH):
                                    mm = e.matmul(ps_w[0:1, 0:P], negcol[:],
                                                  wqc[parh][:, c, :],
                                                  start=(c == 0),
                                                  stop=(c == NCH - 1))
                                ev[("wnq", h)] = pe1.inc(mm)

                            def p1b_a1(e, h=h):
                                e.wait_ge(pe1.h, ev[("wnq", h)])
                                cp = e.activation(wnq[0:1, :], ps_w[0:1, 0:P],
                                                  AF.Copy, bias=0.0, scale=1.0)
                                ev[("wnqev", h)] = act2.inc(cp)

                            def p1b_t2(e, h=h, parh=parh):
                                e.wait_ge(act2.h, ev[("wnqev", h)])
                                for ob in range(NOB):
                                    lo = (ob % 2) * TBW
                                    pob = (h, ob - 1) if ob > 0 else (h - 1, NOB - 1)
                                    if ("qev",) + pob in ev:
                                        e.wait_ge(dve1.h, ev[("qev",) + pob])
                                    for c in range(NCH):
                                        e.matmul(ps_q[:, lo:lo + TBW],
                                                 wqc[parh][:, c, :],
                                                 xol[ob][:, c, :],
                                                 start=(c == 0), stop=False)
                                    boff = h * P
                                    e.matmul(ps_q[:, lo:lo + TBW],
                                             brow_sb[0:1, boff:boff + P],
                                             onesrow[:], start=False, stop=False)
                                    t0 = ob * TBW
                                    mm = e.matmul(ps_q[:, lo:lo + TBW], wnq[0:1, :],
                                                  mrobf[0:1, t0:t0 + TBW],
                                                  start=False, stop=True)
                                    ev[("qmm", h, ob)] = pe2.inc(mm)

                            def p1b_v1(e, h=h):
                                for ob in range(NOB):
                                    e.wait_ge(pe2.h, ev[("qmm", h, ob)])
                                    e.wait_ge(act2.h, ev[("rrevo", ob)])
                                    qp = ob % 2
                                    pob = (h, ob - 2) if ob >= 2 else (h - 1, ob + 2)
                                    if ("qdma",) + pob in ev:
                                        wt(e, ev[("qdma",) + pob])
                                    lo = qp * TBW
                                    tt = e.tensor_mul(qstage[:, qp, :],
                                                      ps_q[:, lo:lo + TBW],
                                                      rrown[:, ob, :])
                                    tt.then_inc(dve1.h, 1)

                            def p1b_sy(e, h=h):
                                for ob in range(NOB):
                                    e.wait_ge(dve1.h, ev[("qev", h, ob)])
                                    qp = ob % 2
                                    t0 = ob * TBW
                                    e.dma_start(
                                        qt_scr[h * P:(h + 1) * P, t0:t0 + TBW],
                                        qstage[:, qp, :]).then_inc(sQO[qp].h, 16)

                            G(p1b_g1)
                            T(p1b_t1)
                            A(p1b_a1)
                            for ob in range(NOB):
                                ev[("qev", h, ob)] = dve1.pre()
                                ct = sQO[ob % 2]
                                ev[("qdma", h, ob)] = (ct, ct.pre(16))
                            T(p1b_t2)
                            V(p1b_v1)
                            SY(p1b_sy)

                        pw_cols = [(wproj_d, wp_scr, oc) for oc in range(NCH)] + \
                                  [(wffn_d, wf_scr, oc) for oc in range(NCH)]

                        def pw_g(e):
                            for i, (src, dst, oc) in enumerate(pw_cols):
                                par = i % 2
                                if i >= 2:
                                    e.wait_ge(sPWO[par].h, 16 * (i // 2))
                                o0 = oc * P
                                stg = xsq[:, 4 * par:4 * par + 4, :].rearrange(
                                    "p a b -> p (a b)").rearrange(
                                    "p (c o) -> p c o", c=NCH)
                                e.dma_start(stg, src[0:D, o0:o0 + P].rearrange(
                                    "(c p) o -> p c o", p=P)).then_inc(
                                    sPWI[par].h, 16)
                                sPWI[par].n += 16

                        def pw_sy(e):
                            for i, (src, dst, oc) in enumerate(pw_cols):
                                par = i % 2
                                e.wait_ge(sPWI[par].h, 16 * (i // 2 + 1))
                                o0 = oc * P
                                stg = xsq[:, 4 * par:4 * par + 4, :].rearrange(
                                    "p a b -> p (a b)").rearrange(
                                    "p (c o) -> p c o", c=NCH)
                                e.dma_start(dst[0:D, o0:o0 + P].rearrange(
                                    "(c p) o -> p c o", p=P), stg).then_inc(
                                    sPWO[par].h, 16)
                                sPWO[par].n += 16
                        G(pw_g)
                        SY(pw_sy)

                        if debug:
                            def p1_dbg(e):
                                for c in allct:
                                    e.wait_ge(c.h, c.n)
                                dma(e, sDBG, dbg["kt"], kt_all[:])
                                dma(e, sDBG, dbg["va"],
                                    v_all[:].rearrange("p a b -> p (a b)"))
                                dma(e, sDBG, dbg["qt"], qt_scr)
                                dma(e, sDBG, dbg["orows"][0:1, :], robf[:])
                                dma(e, sDBG, dbg["orows"][1:2, :], mrobf[:])
                            SY(p1_dbg)
                        barrier()

                if nphase >= 3:
                    o_all = es_o.enter_context(
                        nc.sbuf_tensor("o_all", [P, NH, OWN], BF16, side="left"))
                    with ExitStack() as _es:
                        qbuf = _es.enter_context(nc.sbuf_tensor("qbuf", [P, 2, 1024], BF16, side="right"))
                        pbuf = _es.enter_context(nc.sbuf_tensor("pbuf", [P, 4, 512], BF16, side="right"))
                        rl_sb = _es.enter_context(nc.sbuf_tensor("rl_sb", [P, 1024], F32, side="right"))
                        ps_sA0 = _es.enter_context(nc.psum_tensor("ps_sA0", [P, 512], F32))
                        ps_sA1 = _es.enter_context(nc.psum_tensor("ps_sA1", [P, 512], F32))
                        ps_sB0 = _es.enter_context(nc.psum_tensor("ps_sB0", [P, 512], F32))
                        ps_sB1 = _es.enter_context(nc.psum_tensor("ps_sB1", [P, 512], F32))
                        ps_o = _es.enter_context(nc.psum_tensor("ps_o", [P, 1024], F32))
                        ps_l = _es.enter_context(nc.psum_tensor("ps_l", [P, 1024], F32))
                        ps_sc = [[ps_sA0, ps_sA1], [ps_sB0, ps_sB1]]
                        peS0, peL0 = pe1.n, pe2.n
                        actX0, dveT0, dveE0 = act1.n, dve1.n, dveE.n
                        sQI0 = [sQI[0].n, sQI[1].n]
                        units = []
                        for j in range(NJT):
                            units.append((1, j))
                            if j < 32:
                                units.append((0, j))
                        nU = len(units)
                        cseq = {}
                        ctxu = [[], []]
                        for u, (cx, j) in enumerate(units):
                            cseq[u] = len(ctxu[cx])
                            ctxu[cx].append(u)

                        def owner(j):
                            return 0 if j < 32 else 1

                        def loc0(cx, j):
                            m = j // 8
                            return 128 * m if cx == 0 else 128 * max(0, m - 4)

                        evS = lambda g: peS0 + g + 1
                        evL = lambda g: peL0 + g + 1
                        evX = lambda g: actX0 + g + 1
                        evT = lambda bh, j: dveT0 + bh * NJT + j + 1
                        evE = lambda bh: dveE0 + bh + 1
                        evQ = lambda bh: (sQI[bh % 2],
                                          sQI0[bh % 2] + 16 * (bh // 2 + 1))

                        def p2_sy(e):
                            entry(e)
                            for bh in range(NBH):
                                b, h = bh // NH, bh % NH
                                if bh >= 2:
                                    e.wait_ge(pe1.h, evS((bh - 1) * nU - 1))
                                t0 = b * 1024
                                evt = dma(e, sQI[bh % 2], qbuf[:, bh % 2, :],
                                          qt_scr[h * P:(h + 1) * P, t0:t0 + 1024])
                                assert evt == evQ(bh)
                        SY(p2_sy)

                        for bh in range(NBH):
                            b, h = bh // NH, bh % NH
                            g0 = bh * nU

                            def p2_t(e, bh=bh, b=b, h=h, g0=g0):
                                if bh == 0:
                                    entry(e)
                                qb = qbuf[:, bh % 2, :]

                                def scores(u):
                                    cx, j = units[u]
                                    c = cseq[u]
                                    l0 = loc0(cx, j)
                                    if u == 0:
                                        wt(e, evQ(bh))
                                    if c >= 2:
                                        e.wait_ge(act1.h,
                                                  evX(g0 + ctxu[cx][c - 2]))
                                    elif bh > 0:
                                        pu = ctxu[cx][len(ctxu[cx]) - 2 + c]
                                        e.wait_ge(act1.h, evX(g0 - nU + pu))
                                    k0 = b * S + j * P
                                    mm = e.matmul(ps_sc[cx][c % 2][:, l0:512],
                                                  kt_all[:, k0:k0 + P],
                                                  qb[:, 512 * cx + l0:
                                                     512 * cx + 512],
                                                  start=True, stop=True)
                                    assert pe1.inc(mm) == evS(g0 + u)

                                def pvl(u):
                                    cx, j = units[u]
                                    c = cseq[u]
                                    l0 = loc0(cx, j)
                                    gq0 = 512 * cx + l0
                                    if owner(j) == cx:
                                        e.wait_ge(dve1.h, evT(bh, j))
                                    else:
                                        e.wait_ge(act1.h, evX(g0 + u))
                                    if j == 0 and bh > 0:
                                        e.wait_ge(dveE.h, evE(bh - 1))
                                    pb = pbuf[:, u % 4, :]
                                    stp = (j == 31) if cx == 0 else (j == NJT - 1)
                                    e.matmul(ps_o[:, gq0:512 * cx + 512],
                                             v_all[:, b * NJT + j, 0:P],
                                             pb[:, l0:512], start=(j == 0),
                                             stop=stp)
                                    mm = e.matmul(ps_l[:, gq0:512 * cx + 512],
                                                  ones_red[:], pb[:, l0:512],
                                                  start=(j == 0), stop=stp)
                                    assert pe2.inc(mm) == evL(g0 + u)

                                scores(0)
                                scores(1)
                                scores(2)
                                for u in range(nU):
                                    pvl(u)
                                    if u + 3 < nU:
                                        scores(u + 3)

                            def p2_a(e, bh=bh, g0=g0):
                                if bh == 0:
                                    entry(e)
                                for u in range(nU):
                                    cx, j = units[u]
                                    c = cseq[u]
                                    l0 = loc0(cx, j)
                                    e.wait_ge(pe1.h, evS(g0 + u))
                                    if g0 + u >= 4:
                                        e.wait_ge(pe2.h, evL(g0 + u - 4))
                                    x = e.activation(pbuf[:, u % 4, l0:512],
                                                     ps_sc[cx][c % 2][:, l0:512],
                                                     AF.Exp, bias=0.0, scale=SCALE)
                                    assert act1.inc(x) == evX(g0 + u)

                            def p2_v(e, bh=bh, b=b, h=h, g0=g0):
                                if bh == 0:
                                    entry(e)
                                for j in range(NJT):
                                    cx = owner(j)
                                    u = ctxu[cx][cseq_of_owner[j]]
                                    l0 = loc0(cx, j)
                                    e.wait_ge(act1.h, evX(g0 + u))
                                    tt = e.tensor_mul(
                                        pbuf[:, u % 4, l0:l0 + P],
                                        pbuf[:, u % 4, l0:l0 + P],
                                        tmask[:, j % 8, :])
                                    assert dve1.inc(tt) == evT(bh, j)
                                e.wait_ge(pe2.h, evL(g0 + nU - 1))
                                dve2.inc(e.reciprocal(rl_sb[:], ps_l[:]))
                                e.wait_ge(dve2.h, dve2.n)
                                t0 = b * 1024
                                e.tensor_mul(o_all[:, h, t0:t0 + 512],
                                             ps_o[:, 0:512], rl_sb[:, 0:512])
                                tt = e.tensor_mul(o_all[:, h, t0 + 512:t0 + 1024],
                                                  ps_o[:, 512:1024],
                                                  rl_sb[:, 512:1024])
                                assert dveE.inc(tt) == evE(bh)

                            cseq_of_owner = {}
                            for j in range(NJT):
                                cx = owner(j)
                                for uu in ctxu[cx]:
                                    if units[uu][1] == j:
                                        cseq_of_owner[j] = cseq[uu]
                                        break

                            T(p2_t)
                            A(p2_a)
                            V(p2_v)

                        if debug:
                            def p2_dbg(e):
                                for c in allct:
                                    e.wait_ge(c.h, c.n)
                                dma(e, sDBG, dbg["oall"],
                                    o_all[:].rearrange("p a b -> p a b"))
                            SY(p2_dbg)
                        barrier()
            if nphase >= 4:
                with ExitStack() as _es:
                    z_ob = _es.enter_context(nc.sbuf_tensor("z_ob", [P, 2, NCH, TBW], BF16, side="right"))
                    bprow_sb = _es.enter_context(nc.sbuf_tensor("bprow_sb", [1, D], BF16, side="right"))
                    zsq = _es.enter_context(nc.sbuf_tensor("zsq", [P, NCH, TBW], BF16, side="right"))
                    xof = _es.enter_context(nc.sbuf_tensor("xof", [P, 2, NCH, TBW], BF16, side="right"))
                    x1st = _es.enter_context(nc.sbuf_tensor("x1st", [P, 2, TBW], F32, side="right"))
                    wpc0 = _es.enter_context(nc.sbuf_tensor("wpc0", [P, NCH, P], BF16, side="right"))
                    wpc1 = _es.enter_context(nc.sbuf_tensor("wpc1", [P, NCH, P], BF16, side="right"))
                    rr3 = _es.enter_context(nc.sbuf_tensor("rr3", [P, 2, TBW], BF16, side="right"))
                    rm3 = _es.enter_context(nc.sbuf_tensor("rm3", [P, 2, TBW], BF16, side="right"))
                    r3row = _es.enter_context(nc.sbuf_tensor("r3row", [1, 2, TBW], BF16, side="right"))
                    m3row = _es.enter_context(nc.sbuf_tensor("m3row", [1, 2, TBW], BF16, side="right"))
                    ttmp = _es.enter_context(nc.sbuf_tensor("ttmp", [P, TBW], F32, side="right"))
                    ps_z = _es.enter_context(nc.psum_tensor("ps_z", [P, 2 * TBW], F32))
                    ps_st3 = _es.enter_context(nc.psum_tensor("ps_st3", [1, 2 * TBW], F32))
                    ps_u3 = _es.enter_context(nc.psum_tensor("ps_u3", [P, TBW], F32))
                    wpc = [wpc0, wpc1]

                    def p3_head(ob):
                        par = ob % 2

                        def p3_sy1(e, ob=ob, par=par):
                            pass

                        def p3_g(e, ob=ob, par=par):
                            if ob == 0:
                                entry(e)
                                ev["bprow"] = dma(e, sC, bprow_sb[:], bprow_d)
                            if ob >= 2:
                                e.wait_ge(dve2.h, ev[("res3", ob - 2)])
                            t0 = ob * TBW
                            ev[("xof", ob)] = dma(
                                e, sXOF[par], xof[:, par, :, :],
                                xt_own[0:D, t0:t0 + TBW].rearrange(
                                    "(c p) t -> p c t", p=P))
                            pass

                        def p3_syw(e, ob=ob):
                            for oc in range(NCH):
                                pc = (ob, oc - 2) if oc >= 2 else (ob - 1, oc + 14)
                                if ("zmm",) + pc in ev:
                                    e.wait_ge(pe1.h, ev[("zmm",) + pc])
                                o0 = oc * P
                                ev[("wp", ob, oc)] = dma(
                                    e, sWP[oc % 2], wpc[oc % 2][:],
                                    wp_scr[0:D, o0:o0 + P].rearrange(
                                        "(c p) o -> p c o", p=P))

                        def p3_t1(e, ob=ob, par=par):
                            if ob == 0:
                                entry(e)
                                wt(e, ev["bprow"])
                            for oc in range(NCH):
                                wt(e, ev[("wp", ob, oc)])
                                pc = (ob, oc - 2) if oc >= 2 else (ob - 1, oc + 14)
                                if ("zev",) + pc in ev:
                                    e.wait_ge(act1.h, ev[("zev",) + pc])
                                lo = (oc % 2) * TBW
                                for c in range(NCH):
                                    e.matmul(ps_z[:, lo:lo + TBW],
                                             wpc[oc % 2][:, c, :],
                                             o_all[:, c, ob * TBW:(ob + 1) * TBW],
                                             start=(c == 0), stop=False)
                                o0 = oc * P
                                mm = e.matmul(ps_z[:, lo:lo + TBW],
                                              bprow_sb[0:1, o0:o0 + P], onesrow[:],
                                              start=False, stop=True)
                                mm.then_inc(pe1.h, 1)

                        def p3_a1(e, ob=ob, par=par):
                            if ob == 0:
                                entry(e)
                            for oc in range(NCH):
                                e.wait_ge(pe1.h, ev[("zmm", ob, oc)])
                                if ob >= 2:
                                    e.wait_ge(dve2.h, ev[("res3c", ob - 2, oc)])
                                if oc == 0 and ob >= 2:
                                    e.wait_ge(pe1.h, ev[("st3", ob - 2)])
                                lo = (oc % 2) * TBW
                                cp = e.activation(z_ob[:, par, oc, :],
                                                  ps_z[:, lo:lo + TBW], AF.Copy,
                                                  bias=0.0, scale=1.0)
                                cp.then_inc(act1.h, 1)

                        for oc in range(NCH):
                            ev[("zmm", ob, oc)] = pe1.pre()
                            ev[("zev", ob, oc)] = act1.pre()
                        for c in range(NCH):
                            ct = sX1O[c % 2]
                            ev[("x1dmac", ob, c)] = (ct, ct.pre(16))
                        G(p3_g)
                        SY(p3_syw)
                        T(p3_t1)
                        A(p3_a1)

                    def p3_tail_pre(ob):
                        par = ob % 2

                        def p3_v1(e, ob=ob, par=par):
                            e.wait_ge(act1.h, ev[("zev", ob, NCH - 1)])
                            if ob > 0:
                                e.wait_ge(pe1.h, ev[("st3", ob - 1)])
                            for c in range(NCH):
                                tt = e.tensor_mul(zsq[:, c, :],
                                                  z_ob[:, par, c, :],
                                                  z_ob[:, par, c, :])
                            ev[("zsq", ob)] = dve1.inc(tt)

                        def p3_t2(e, ob=ob, par=par):
                            e.wait_ge(act1.h, ev[("zev", ob, NCH - 1)])
                            if ob > 0:
                                e.wait_ge(act1.h, ev[("math3", ob - 1)]["mu"])
                            for c in range(NCH):
                                e.matmul(ps_st3[0:1, 0:TBW], ones_red[:, 0:1],
                                         z_ob[:, par, c, :], start=(c == 0),
                                         stop=(c == NCH - 1))
                            e.wait_ge(dve1.h, ev[("zsq", ob)])
                            for c in range(NCH):
                                mm = e.matmul(ps_st3[0:1, TBW:2 * TBW],
                                              ones_red[:, 0:1], zsq[:, c, :],
                                              start=(c == 0), stop=(c == NCH - 1))
                            ev[("st3", ob)] = pe1.inc(mm)
                        V(p3_v1)
                        T(p3_t2)
                        p3_tail_pre2(ob)

                    def p3_tail_pre2(ob):
                        par = ob % 2

                        ev[("math3", ob)] = {}
                        pm = ev[("math3", ob - 1)]["mur"] if ob >= 1 else None
                        stats_math(ps_st3, par, r3row[0:1, par, :],
                                   m3row[0:1, par, :], ev[("st3", ob)], pm,
                                   ev[("math3", ob)])
                        md = ev[("math3", ob)]

                        def p3_t3(e, ob=ob, par=par, md=md):
                            e.wait_ge(act1.h, md["r"])
                            if ob >= 2:
                                e.wait_ge(dve2.h, ev[("res3", ob - 2)])
                            mm = e.matmul(ps_u3[:, 0:TBW], oneB[0:1, :],
                                          r3row[0:1, par, :], start=True,
                                          stop=True)
                            ev[("rr3mm", ob)] = pe1.inc(mm)

                        def p3_a4(e, ob=ob, par=par):
                            e.wait_ge(pe1.h, ev[("rr3mm", ob)])
                            cp = e.activation(rr3[:, par, :], ps_u3[:, 0:TBW],
                                              AF.Copy, bias=0.0, scale=1.0)
                            ev[("rr3ev", ob)] = act2.inc(cp)

                        def p3_t4(e, ob=ob, par=par, md=md):
                            e.wait_ge(dve2.h, md["mur"])
                            e.wait_ge(act2.h, ev[("rr3ev", ob)])
                            mm = e.matmul(ps_u3[:, 0:TBW], oneB[0:1, :],
                                          m3row[0:1, par, :], start=True,
                                          stop=True)
                            ev[("rm3mm", ob)] = pe1.inc(mm)

                        def p3_a5(e, ob=ob, par=par):
                            e.wait_ge(pe1.h, ev[("rm3mm", ob)])
                            cp = e.activation(rm3[:, par, :], ps_u3[:, 0:TBW],
                                              AF.Copy, bias=0.0, scale=1.0)
                            ev[("rm3ev", ob)] = act2.inc(cp)

                        def p3_v4(e, ob=ob, par=par):
                            e.wait_ge(act2.h, ev[("rm3ev", ob)])
                            wt(e, ev[("xof", ob)])
                            for c in range(NCH):
                                pc = (ob, c - 2) if c >= 2 else (ob - 1, c + 14)
                                if ("x1dmac",) + pc in ev:
                                    wt(e, ev[("x1dmac",) + pc])
                                ed(e, dve2, e.tensor_mul(ttmp[:],
                                                         z_ob[:, par, c, :],
                                                         rr3[:, par, :]))
                                ed(e, dve2, e.scalar_tensor_tensor(
                                    ttmp[:], rm3[:, par, :], -1.0, ttmp[:],
                                    ALU.mult, ALU.add))
                                tt = e.tensor_add(x1st[:, c % 2, :], ttmp[:],
                                                  xof[:, par, c, :])
                                ev[("res3c", ob, c)] = dve2.inc(tt)
                                e.wait_ge(dve2.h, dve2.n)
                            ev[("res3", ob)] = ev[("res3c", ob, NCH - 1)]

                        def p3_sy2(e, ob=ob):
                            t0 = ob * TBW
                            for c in range(NCH):
                                e.wait_ge(dve2.h, ev[("res3c", ob, c)])
                                e.dma_start(
                                    x1_scr[:, c, t0:t0 + TBW],
                                    x1st[:, c % 2, :]).then_inc(sX1O[c % 2].h, 16)
                            ev[("x1dma", ob)] = ev[("x1dmac", ob, NCH - 1)]
                        T(p3_t3)
                        A(p3_a4)
                        T(p3_t4)
                        A(p3_a5)
                        p3_res_defs[ob] = (p3_v4, p3_sy2)

                    def p3_tail_res(ob):
                        v4, sy2 = p3_res_defs[ob]
                        V(v4)
                        SY(sy2)

                    p3_res_defs = {}
                    for ob in range(NOB):
                        if ob >= 2:
                            p3_tail_res(ob - 2)
                        p3_head(ob)
                        if ob >= 1:
                            p3_tail_pre(ob - 1)
                    p3_tail_res(NOB - 2)
                    p3_tail_pre(NOB - 1)
                    p3_tail_res(NOB - 1)

                    if debug:
                        def p3_dbg(e):
                            for c in allct:
                                e.wait_ge(c.h, c.n)
                            dma(e, sDBG, dbg["x1"], x1_scr)
                        SY(p3_dbg)
                    barrier()
            if nphase >= 3:
                es_o.close()

            if nphase >= 5:
                with ExitStack() as _es:
                    x1b = _es.enter_context(nc.sbuf_tensor("x1b", [P, 2, NCH, TBW], BF16, side="right"))
                    sq4 = _es.enter_context(nc.sbuf_tensor("sq4", [P, NCH, TBW], BF16, side="right"))
                    sq4b = _es.enter_context(nc.sbuf_tensor("sq4b", [P, NCH, TBW], BF16, side="right"))
                    ln2 = _es.enter_context(nc.sbuf_tensor("ln2", [P, 2, NCH, TBW], BF16, side="right"))
                    z2 = _es.enter_context(nc.sbuf_tensor("z2", [P, 2, NCH, TBW], BF16, side="right"))
                    x2st = _es.enter_context(nc.sbuf_tensor("x2st", [P, 2, TBW], F32, side="right"))
                    wfc0 = _es.enter_context(nc.sbuf_tensor("wfc0", [P, NCH, P], BF16, side="right"))
                    wfc1 = _es.enter_context(nc.sbuf_tensor("wfc1", [P, NCH, P], BF16, side="right"))
                    wa_bf = _es.enter_context(nc.sbuf_tensor("wa_bf", [P, NCH, P], BF16, side="right"))
                    wb_bf = _es.enter_context(nc.sbuf_tensor("wb_bf", [P, D], BF16, side="right"))
                    wr_bf = _es.enter_context(nc.sbuf_tensor("wr_bf", [P, NCH, 8], BF16, side="right"))
                    e8_bf = _es.enter_context(nc.sbuf_tensor("e8_bf", [8, P], BF16, side="right"))
                    idn_f4 = _es.enter_context(nc.sbuf_tensor("idn_f4", [P, P], F32, side="right"))
                    ax_sb = _es.enter_context(nc.sbuf_tensor("ax_sb", [P, TBW], BF16, side="right"))
                    sc_sb = _es.enter_context(nc.sbuf_tensor("sc_sb", [P, TBW], BF16, side="right"))
                    we8 = _es.enter_context(nc.sbuf_tensor("we8", [P, 4 * P + 64], F32, side="right"))
                    w8t = _es.enter_context(nc.sbuf_tensor("w8t", [8, TBW], BF16, side="right"))
                    tcol = _es.enter_context(nc.sbuf_tensor("tcol", [P, 8], F32, side="right"))
                    lg_sb = _es.enter_context(nc.sbuf_tensor("lg_sb", [P, 32], F32, side="right"))
                    rr4 = _es.enter_context(nc.sbuf_tensor("rr4", [P, 2, TBW], BF16, side="right"))
                    rm4 = _es.enter_context(nc.sbuf_tensor("rm4", [P, 2, TBW], BF16, side="right"))
                    r4row = _es.enter_context(nc.sbuf_tensor("r4row", [1, 2, TBW], BF16, side="right"))
                    m4row = _es.enter_context(nc.sbuf_tensor("m4row", [1, 2, TBW], BF16, side="right"))
                    tt4 = _es.enter_context(nc.sbuf_tensor("tt4", [P, TBW], F32, side="right"))
                    ps_z4 = _es.enter_context(nc.psum_tensor("ps_z4", [P, 2 * TBW], F32))
                    ps_u4 = _es.enter_context(nc.psum_tensor("ps_u4", [P, 2 * TBW], F32))
                    ps_lg = _es.enter_context(nc.psum_tensor("ps_lg", [P, 32], F32))
                    ps_w4 = _es.enter_context(nc.psum_tensor("ps_w4", [P, TBW], F32))
                    ps_st4 = _es.enter_context(nc.psum_tensor("ps_st4", [1, 2 * TBW], F32))
                    wfc = [wfc0, wfc1]

                    def p4_g0(e):
                        entry(e)
                        dma(e, sC, wa_bf[:], wa_d.rearrange("(c p) o -> p c o", p=P))
                        dma(e, sC, wb_bf[:], wb_d)
                        dma(e, sC, wr_bf[:], wr_d.rearrange("(c p) o -> p c o", p=P))
                        dma(e, sC, e8_bf[:], e8_d)
                        for k in ("wa", "wb", "wr", "e8"):
                            ev[k] = (sC, sC.n)

                    def p4_sy0(e):
                        entry(e)
                        ev["idn4"] = dma(e, sCs, idn_f4[:], idn_d)

                    def p4_v_init(e):
                        entry(e)
                        dve2.inc(e.memset(we8[:], 0.0))
                    G(p4_g0)
                    SY(p4_sy0)
                    V(p4_v_init)

                    def p4_headmid(ob):
                        par = ob % 2

                        def p4_g1(e, ob=ob, par=par):
                            if ob >= 2:
                                e.wait_ge(dve2.h, ev[("res4", ob - 2)])
                                e.wait_ge(pe1.h, ev[("st4", ob - 2)])
                            t0 = ob * TBW
                            ev[("x1l", ob)] = dma(e, sX1I, x1b[:, par, :, :],
                                                  x1_scr[:, :, t0:t0 + TBW])
                            if ob == 0:
                                wt(e, ev["wr"])

                        def p4_v0(e, ob=ob, par=par):
                            if ob == 0:
                                entry(e)
                            wt(e, ev[("x1l", ob)])
                            if ob > 0:
                                e.wait_ge(pe1.h, ev[("st4", ob - 1)])
                            for c in range(NCH):
                                tt = e.tensor_mul(sq4[:, c, :], x1b[:, par, c, :],
                                                  x1b[:, par, c, :])
                            ev[("x1sq", ob)] = dve1.inc(tt)

                        def p4_t0(e, ob=ob, par=par):
                            if ob == 0:
                                entry(e)
                            wt(e, ev[("x1l", ob)])
                            if ob == 1:
                                e.wait_ge(act1.h, ev[("math4", 0)]["mu"])
                            elif ob >= 2:
                                e.wait_ge(act1.h, ev[("math4b", ob - 2)]["mu"])
                            for c in range(NCH):
                                e.matmul(ps_st4[0:1, 0:TBW], ones_red[:, 0:1],
                                         x1b[:, par, c, :], start=(c == 0),
                                         stop=(c == NCH - 1))
                            e.wait_ge(dve1.h, ev[("x1sq", ob)])
                            for c in range(NCH):
                                mm = e.matmul(ps_st4[0:1, TBW:2 * TBW],
                                              ones_red[:, 0:1], sq4[:, c, :],
                                              start=(c == 0), stop=(c == NCH - 1))
                            ev[("st4", ob)] = pe1.inc(mm)

                        G(p4_g1)
                        V(p4_v0)
                        T(p4_t0)

                        ev[("math4", ob)] = {}
                        pm = (ev[("math4b", ob - 2)]["mur"] if ob >= 2
                              else (ev[("math4", 0)]["mur"] if ob == 1 else None))
                        stats_math(ps_st4, par, r4row[0:1, par, :],
                                   m4row[0:1, par, :], ev[("st4", ob)], pm,
                                   ev[("math4", ob)])
                        md4 = ev[("math4", ob)]

                        def p4_t1(e, ob=ob, par=par, md=md4):
                            e.wait_ge(act1.h, md["r"])
                            if ob > 0:
                                e.wait_ge(act1.h, ev[("w8tev", ob - 1, 3)])
                            if ob >= 2:
                                e.wait_ge(dve2.h, ev[("res4", ob - 2)])
                            mm = e.matmul(ps_w4[:, 0:TBW], oneB[0:1, :],
                                          r4row[0:1, par, :], start=True,
                                          stop=True)
                            ev[("rr4mm", ob)] = pe1.inc(mm)

                        def p4_a3(e, ob=ob, par=par):
                            e.wait_ge(pe1.h, ev[("rr4mm", ob)])
                            cp = e.activation(rr4[:, par, :], ps_w4[:, 0:TBW],
                                              AF.Copy, bias=0.0, scale=1.0)
                            ev[("rr4ev", ob)] = act2.inc(cp)

                        def p4_t2(e, ob=ob, par=par, md=md4):
                            e.wait_ge(dve2.h, md["mur"])
                            e.wait_ge(act2.h, ev[("rr4ev", ob)])
                            mm = e.matmul(ps_w4[:, 0:TBW], oneB[0:1, :],
                                          m4row[0:1, par, :], start=True,
                                          stop=True)
                            ev[("rm4mm", ob)] = pe1.inc(mm)

                        def p4_a4(e, ob=ob, par=par):
                            e.wait_ge(pe1.h, ev[("rm4mm", ob)])
                            cp = e.activation(rm4[:, par, :], ps_w4[:, 0:TBW],
                                              AF.Copy, bias=0.0, scale=1.0)
                            ev[("rm4ev", ob)] = act2.inc(cp)

                        def p4_v3(e, ob=ob, par=par):
                            e.wait_ge(act2.h, ev[("rm4ev", ob)])
                            if ob >= 2:
                                e.wait_ge(pe1.h, ev[("z4mm", ob - 2, NCH - 1)])
                            for c in range(NCH):
                                ed(e, dve2, e.tensor_mul(ln2[:, par, c, :],
                                                         x1b[:, par, c, :],
                                                         rr4[:, par, :]))
                                tt = e.scalar_tensor_tensor(
                                    ln2[:, par, c, :], rm4[:, par, :], -1.0,
                                    ln2[:, par, c, :], ALU.mult, ALU.add)
                                if c < NCH - 1:
                                    ed(e, dve2, tt)
                            ev[("ln2", ob)] = dve1.inc(tt)
                        T(p4_t1)
                        A(p4_a3)
                        T(p4_t2)
                        A(p4_a4)
                        V(p4_v3)

                        for si in range(4):
                            ev[("lgev", ob, si)] = act2.pre()

                        def p4_t3(e, ob=ob, par=par):
                            e.wait_ge(dve1.h, ev[("ln2", ob)])
                            if ob == 0:
                                wt(e, ev["wa"])
                            else:
                                e.wait_ge(dve1.h, ev[("sc", ob - 1)])
                                e.wait_ge(dve2.h, ev[("top2", ob - 1, 3)])
                            for c in range(NCH):
                                mm = e.matmul(ps_u4[:, 0:TBW], wa_bf[:, c, :],
                                              ln2[:, par, c, :], start=(c == 0),
                                              stop=(c == NCH - 1))
                            ev[("axmm", ob)] = pe1.inc(mm)
                            for si in range(4):
                                if si > 0:
                                    e.wait_ge(act2.h, ev[("lgev", ob, si - 1)])
                                elif ob > 0:
                                    e.wait_ge(act2.h, ev[("lgev", ob - 1, 3)])
                                for c in range(NCH):
                                    mm = e.matmul(ps_lg[:, si * 8:(si + 1) * 8],
                                                  ln2[:, par, c,
                                                      si * P:(si + 1) * P],
                                                  wr_bf[:, c, :], start=(c == 0),
                                                  stop=(c == NCH - 1))
                                ev[("lg", ob, si)] = pe1.inc(mm)

                        def p4_a5b(e, ob=ob):
                            for si in range(4):
                                e.wait_ge(pe1.h, ev[("lg", ob, si)])
                                if si == 0 and ob > 0:
                                    e.wait_ge(dve2.h, ev[("lgdone", ob - 1, 3)])
                                cp = e.activation(lg_sb[:, si * 8:(si + 1) * 8],
                                                  ps_lg[:, si * 8:(si + 1) * 8],
                                                  AF.Copy, bias=0.0, scale=1.0)
                                cp.then_inc(act2.h, 1)

                        def p4_a5(e, ob=ob):
                            e.wait_ge(pe1.h, ev[("axmm", ob)])
                            if ob > 0:
                                e.wait_ge(dve1.h, ev[("sc", ob - 1)])
                            cp = e.activation(ax_sb[:], ps_u4[:, 0:TBW], AF.Copy,
                                              bias=0.0, scale=1.0)
                            ev[("axev", ob)] = act1.inc(cp)
                        T(p4_t3)
                        A(p4_a5b)
                        A(p4_a5)

                        for si in range(4):
                            ev[("eexp", ob, si)] = act2.pre()

                        def p4_v4(e, ob=ob):
                            for si in range(4):
                                e.wait_ge(act2.h, ev[("lgev", ob, si)])
                                lg = lg_sb[:, si * 8:(si + 1) * 8]
                                m1 = tcol[:, 0:1]
                                m2 = tcol[:, 1:2]
                                dm = tcol[:, 2:3]
                                w1 = tcol[:, 3:4]
                                w2 = tcol[:, 4:5]
                                eecol = tcol[:, 5:6]
                                eq1 = we8[:, 512:520]
                                lgm = we8[:, 520:528]
                                eq2 = we8[:, 528:536]
                                ed(e, dve2, e.reduce_max(m1, lg, axis=AX))
                                ed(e, dve2, e.tensor_scalar(eq1, lg, m1, None,
                                                            ALU.is_equal))
                                ed(e, dve2, e.scalar_tensor_tensor(
                                    lgm, eq1, -1e30, lg, ALU.mult, ALU.add))
                                ed(e, dve2, e.reduce_max(m2, lgm, axis=AX))
                                ed(e, dve2, e.tensor_scalar(eq2, lgm, m2, None,
                                                            ALU.is_equal))
                                tt = e.tensor_sub(dm, m2, m1)
                                ev[("dm", ob, si)] = dve2.inc(tt)
                                ev[("lgdone", ob, si)] = ev[("dm", ob, si)]
                                e.wait_ge(act2.h, ev[("eexp", ob, si)])
                                ed(e, dve2, e.tensor_scalar(w1, eecol, 1.0, None,
                                                            ALU.add))
                                ed(e, dve2, e.reciprocal(w1, w1))
                                ed(e, dve2, e.tensor_scalar(w1, w1, 2.0, None,
                                                            ALU.mult))
                                ed(e, dve2, e.tensor_scalar(w2, w1, -1.0, 2.0,
                                                            ALU.mult, ALU.add))
                                ed(e, dve2, e.tensor_scalar(
                                    we8[:, si * P:si * P + 8], eq1, w1, None,
                                    ALU.mult))
                                tt = e.scalar_tensor_tensor(
                                    we8[:, si * P:si * P + 8], eq2, w2,
                                    we8[:, si * P:si * P + 8], ALU.mult, ALU.add)
                                ev[("top2", ob, si)] = dve2.inc(tt)

                        def p4_a6(e, ob=ob):
                            for si in range(4):
                                e.wait_ge(dve2.h, ev[("dm", ob, si)])
                                if si > 0:
                                    e.wait_ge(dve2.h, ev[("top2", ob, si - 1)])
                                elif ob > 0:
                                    e.wait_ge(dve2.h, ev[("top2", ob - 1, 3)])
                                x = e.activation(tcol[:, 5:6], tcol[:, 2:3],
                                                 AF.Exp, bias=0.0, scale=1.0)
                                x.then_inc(act2.h, 1)
                        V(p4_v4)
                        A(p4_a6)

                        for si in range(4):
                            ev[("w8tev", ob, si)] = act1.pre()

                        def p4_t4(e, ob=ob):
                            for si in range(4):
                                e.wait_ge(dve2.h, ev[("top2", ob, si)])
                                if si == 0:
                                    e.wait_ge(act2.h, ev[("rm4ev", ob)])
                                    wt(e, ev["idn4"])
                                    if ob > 0:
                                        e.wait_ge(act1.h,
                                                  ev[("w8tev", ob - 1, 3)])
                                else:
                                    e.wait_ge(act1.h, ev[("w8tev", ob, si - 1)])
                                tr = e.transpose(ps_w4[:, 0:P],
                                                 we8[:, si * P:(si + 1) * P],
                                                 idn_f4[:])
                                ev[("w8tr", ob, si)] = pe2.inc(tr)

                        def p4_a7(e, ob=ob):
                            for si in range(4):
                                e.wait_ge(pe2.h, ev[("w8tr", ob, si)])
                                cp = e.activation(w8t[0:8, si * P:(si + 1) * P],
                                                  ps_w4[0:8, 0:P], AF.Copy,
                                                  bias=0.0, scale=1.0)
                                cp.then_inc(act1.h, 1)

                        def p4_t5(e, ob=ob):
                            e.wait_ge(act1.h, ev[("w8tev", ob, 3)])
                            if ob == 0:
                                wt(e, ev["e8"])
                            mm = e.matmul(ps_u4[:, TBW:2 * TBW], e8_bf[:], w8t[:],
                                          start=True, stop=True)
                            ev[("wxmm", ob)] = pe2.inc(mm)

                        def p4_v5(e, ob=ob):
                            e.wait_ge(pe2.h, ev[("wxmm", ob)])
                            e.wait_ge(act1.h, ev[("axev", ob)])
                            tt = e.tensor_mul(sc_sb[:], ax_sb[:],
                                              ps_u4[:, TBW:2 * TBW])
                            ev[("sc", ob)] = dve1.inc(tt)
                        T(p4_t4)
                        A(p4_a7)
                        T(p4_t5)
                        V(p4_v5)

                        for oc in range(NCH):
                            ev[("z4mm", ob, oc)] = pe1.pre()
                            ev[("z4ev", ob, oc)] = act1.pre()

                        def p4_g1b(e, ob=ob):
                            for oc in range(NCH):
                                pc = (ob, oc - 2) if oc >= 2 else (ob - 1, oc + 14)
                                if ("z4mm",) + pc in ev:
                                    e.wait_ge(pe1.h, ev[("z4mm",) + pc])
                                o0 = oc * P
                                ev[("wf", ob, oc)] = dma(
                                    e, sWF[oc % 2], wfc[oc % 2][:],
                                    wf_scr[0:D, o0:o0 + P].rearrange(
                                        "(c p) o -> p c o", p=P))
                        SY(p4_g1b)

                        def p4_t6(e, ob=ob, par=par):
                            e.wait_ge(dve1.h, ev[("sc", ob)])
                            if ob == 0:
                                wt(e, ev["wb"])
                            for oc in range(NCH):
                                wt(e, ev[("wf", ob, oc)])
                                pc = (ob, oc - 2) if oc >= 2 else (ob - 1, oc + 14)
                                if ("z4ev",) + pc in ev:
                                    e.wait_ge(act1.h, ev[("z4ev",) + pc])
                                lo = (oc % 2) * TBW
                                for c in range(NCH):
                                    e.matmul(ps_z4[:, lo:lo + TBW],
                                             wfc[oc % 2][:, c, :],
                                             ln2[:, par, c, :],
                                             start=(c == 0), stop=False)
                                o0 = oc * P
                                mm = e.matmul(ps_z4[:, lo:lo + TBW],
                                              wb_bf[:, o0:o0 + P], sc_sb[:],
                                              start=False, stop=True)
                                mm.then_inc(pe1.h, 1)

                        def p4_a8(e, ob=ob, par=par):
                            for oc in range(NCH):
                                e.wait_ge(pe1.h, ev[("z4mm", ob, oc)])
                                if oc == 0 and ob >= 2:
                                    e.wait_ge(dve2.h, ev[("res4", ob - 2)])
                                    e.wait_ge(pe1.h, ev[("st4b", ob - 2)])
                                lo = (oc % 2) * TBW
                                cp = e.activation(z2[:, par, oc, :],
                                                  ps_z4[:, lo:lo + TBW], AF.Copy,
                                                  bias=0.0, scale=1.0)
                                cp.then_inc(act1.h, 1)
                        T(p4_t6)
                        A(p4_a8)

                    def p4_tail(ob):
                        par = ob % 2

                        def p4_v6(e, ob=ob, par=par):
                            e.wait_ge(act1.h, ev[("z4ev", ob, NCH - 1)])
                            if ob > 0:
                                e.wait_ge(pe1.h, ev[("st4b", ob - 1)])
                            for c in range(NCH):
                                tt = e.tensor_mul(sq4b[:, c, :], z2[:, par, c, :],
                                                  z2[:, par, c, :])
                            ev[("z4sq", ob)] = dve1.inc(tt)

                        def p4_t7(e, ob=ob, par=par):
                            e.wait_ge(act1.h, ev[("z4ev", ob, NCH - 1)])
                            if ob + 1 < NOB:
                                e.wait_ge(act1.h, ev[("math4", ob + 1)]["mu"])
                            elif ob > 0:
                                e.wait_ge(act1.h, ev[("math4b", ob - 1)]["mu"])
                            for c in range(NCH):
                                e.matmul(ps_st4[0:1, 0:TBW], ones_red[:, 0:1],
                                         z2[:, par, c, :], start=(c == 0),
                                         stop=(c == NCH - 1))
                            e.wait_ge(dve1.h, ev[("z4sq", ob)])
                            for c in range(NCH):
                                mm = e.matmul(ps_st4[0:1, TBW:2 * TBW],
                                              ones_red[:, 0:1], sq4b[:, c, :],
                                              start=(c == 0), stop=(c == NCH - 1))
                            ev[("st4b", ob)] = pe1.inc(mm)
                        V(p4_v6)
                        T(p4_t7)

                        ev[("math4b", ob)] = {}
                        pmb = (ev[("math4b", ob - 1)]["mur"] if ob >= 1
                               else ev[("math4", 0)]["mur"])
                        stats_math(ps_st4, par, r4row[0:1, par, :],
                                   m4row[0:1, par, :], ev[("st4b", ob)], pmb,
                                   ev[("math4b", ob)])
                        md4b = ev[("math4b", ob)]

                        def p4_t8(e, ob=ob, par=par, md=md4b):
                            e.wait_ge(act1.h, md["r"])
                            e.wait_ge(act1.h, ev[("w8tev", ob, 3)])
                            mm = e.matmul(ps_w4[:, 0:TBW], oneB[0:1, :],
                                          r4row[0:1, par, :], start=True,
                                          stop=True)
                            ev[("rr4bmm", ob)] = pe1.inc(mm)

                        def p4_a11(e, ob=ob, par=par):
                            e.wait_ge(pe1.h, ev[("rr4bmm", ob)])
                            cp = e.activation(rr4[:, par, :], ps_w4[:, 0:TBW],
                                              AF.Copy, bias=0.0, scale=1.0)
                            ev[("rr4bev", ob)] = act2.inc(cp)

                        def p4_t9(e, ob=ob, par=par, md=md4b):
                            e.wait_ge(dve2.h, md["mur"])
                            e.wait_ge(act2.h, ev[("rr4bev", ob)])
                            mm = e.matmul(ps_w4[:, 0:TBW], oneB[0:1, :],
                                          m4row[0:1, par, :], start=True,
                                          stop=True)
                            ev[("rm4bmm", ob)] = pe1.inc(mm)

                        def p4_a12(e, ob=ob, par=par):
                            e.wait_ge(pe1.h, ev[("rm4bmm", ob)])
                            cp = e.activation(rm4[:, par, :], ps_w4[:, 0:TBW],
                                              AF.Copy, bias=0.0, scale=1.0)
                            ev[("rm4bev", ob)] = act2.inc(cp)

                        def p4_v9(e, ob=ob, par=par):
                            e.wait_ge(act2.h, ev[("rm4bev", ob)])
                            for c in range(NCH):
                                pc = (ob, c - 2) if c >= 2 else (ob - 1, c + 14)
                                if ("x2dmac",) + pc in ev:
                                    wt(e, ev[("x2dmac",) + pc])
                                ed(e, dve2, e.tensor_mul(tt4[:], z2[:, par, c, :],
                                                         rr4[:, par, :]))
                                ed(e, dve2, e.scalar_tensor_tensor(
                                    tt4[:], rm4[:, par, :], -1.0, tt4[:],
                                    ALU.mult, ALU.add))
                                tt = e.tensor_add(x2st[:, c % 2, :], tt4[:],
                                                  x1b[:, par, c, :])
                                ev[("res4c", ob, c)] = dve2.inc(tt)
                                e.wait_ge(dve2.h, dve2.n)
                            ev[("res4", ob)] = ev[("res4c", ob, NCH - 1)]

                        def p4_sy2(e, ob=ob):
                            t0 = ob * TBW
                            for c in range(NCH):
                                e.wait_ge(dve2.h, ev[("res4c", ob, c)])
                                e.dma_start(
                                    out_d[:, c, t0:t0 + TBW],
                                    x2st[:, c % 2, :]).then_inc(sX2O[c % 2].h, 16)
                        T(p4_t8)
                        A(p4_a11)
                        T(p4_t9)
                        A(p4_a12)
                        V(p4_v9)
                        SY(p4_sy2)

                    for ob in range(NOB):
                        for c in range(NCH):
                            ct = sX2O[c % 2]
                            ev[("x2dmac", ob, c)] = (ct, ct.pre(16))
                        p4_headmid(ob)
                        if ob >= 1:
                            p4_tail(ob - 1)
                    p4_tail(NOB - 1)

        def fin(e):
            for c in allct:
                e.wait_ge(c.h, c.n)
        SY(fin)
    return nc


def own_cols(c):
    cols = []
    for b in range(B):
        for t in range(8):
            a = b * S + (c + 8 * t) * P
            cols.extend(range(a, a + P))
    return np.array(cols)


def host_prep_shared(inputs):
    f = np.float32
    x = np.asarray(inputs["L0_attn_x"], f)
    wqkv = np.asarray(inputs["L0_attn_W_qkv"], f)
    sh = {}
    sh["xt_full"] = np.ascontiguousarray(x.reshape(TOK, D).T)
    sh["wq"] = np.ascontiguousarray(wqkv[:, 0:D])
    sh["wk"] = np.ascontiguousarray(wqkv[:, D:D + HD])
    sh["wv"] = np.ascontiguousarray(wqkv[:, D + HD:D + 2 * HD])
    sh["brow"] = np.asarray(inputs["L0_attn_b_qkv"], f).reshape(1, -1)
    sh["wproj"] = np.ascontiguousarray(np.asarray(inputs["L0_attn_W_proj"], f))
    sh["bprow"] = np.asarray(inputs["L0_attn_b_proj"], f).reshape(1, -1)
    sh["wffn"] = np.ascontiguousarray(np.asarray(inputs["L0_ffn_weight"], f).T)
    sh["wa"] = np.ascontiguousarray(
        np.asarray(inputs["L0_ffn_lora_A"], f).reshape(128, D).T)
    sh["wb"] = np.ascontiguousarray(
        np.asarray(inputs["L0_ffn_lora_B"], f).transpose(0, 2, 1).reshape(128, D))
    sh["wr"] = np.ascontiguousarray(np.asarray(inputs["L0_ffn_router_w"], f).T)
    e8m = np.zeros((8, 128), f)
    for e in range(8):
        e8m[e, e * 16:(e + 1) * 16] = 1.0
    sh["e8"] = e8m
    sh["idn"] = np.eye(P, dtype=f)
    return sh


def host_prep_core(sh, c):
    m = {k: v for k, v in sh.items() if k != "xt_full"}
    cols = own_cols(c)
    m["xt_own"] = np.ascontiguousarray(sh["xt_full"][:, cols])
    dltv = 128.0 * (c - np.arange(8, dtype=np.float32))
    m["dlt"] = np.ascontiguousarray(
        np.broadcast_to(dltv[None, :], (P, 8)).astype(np.float32))
    return m


def host_gather(results):
    full = np.empty((TOK, D), np.float32)
    for c, r in enumerate(results):
        o = np.asarray(r["out"])
        xt = o.transpose(1, 0, 2).reshape(D, OWN)
        full[own_cols(c), :] = xt.T
    return full.reshape(B, S, D)


def kernel(**inputs):
    from concourse.bass_utils import run_bass_kernel_spmd
    sh = host_prep_shared(inputs)
    in_maps = [host_prep_core(sh, c) for c in range(8)]
    nc = build(nphase=5, debug=False)
    res = run_bass_kernel_spmd(nc, in_maps, core_ids=list(range(8)))
    return host_gather(res.results)


# version 08
# speedup vs baseline: 1.0524x; 1.0344x over previous
import numpy as np
from contextlib import ExitStack

import concourse.bass as bass
import concourse.mybir as mybir

F32 = mybir.dt.float32
BF16 = mybir.dt.bfloat16
AF = mybir.ActivationFunctionType
ALU = mybir.AluOpType
AX = mybir.AxisListType.X

P = 128
D = 2048
NCH = 16
NH = 16
HD = 128
B = 2
S = 8192
TOK = B * S
TBW = 512
NTB = TOK // TBW
OWN = 2048
NOB = 4
NJT = 64
EPS = 1e-5
SCALE = float(HD) ** -0.5
NBH = B * NH


class Ct:
    def __init__(self, stack, nc, name):
        self.h = stack.enter_context(nc.semaphore(name))
        self.n = 0

    def inc(self, instr, k=1):
        instr.then_inc(self.h, k)
        self.n += k
        return self.n

    def pre(self, k=1):
        self.n += k
        return self.n


def build(nphase=5, debug=False):
    nc = bass.Bass("TRN2", target_bir_lowering=False)

    xt_own = nc.dram_tensor("xt_own", [D, OWN], F32, kind="ExternalInput").ap()
    wq_d = nc.dram_tensor("wq", [D, D], F32, kind="ExternalInput").ap()
    wk_d = nc.dram_tensor("wk", [D, HD], F32, kind="ExternalInput").ap()
    wv_d = nc.dram_tensor("wv", [D, HD], F32, kind="ExternalInput").ap()
    brow_d = nc.dram_tensor("brow", [1, D + 2 * HD], F32, kind="ExternalInput").ap()
    wproj_d = nc.dram_tensor("wproj", [D, D], F32, kind="ExternalInput").ap()
    bprow_d = nc.dram_tensor("bprow", [1, D], F32, kind="ExternalInput").ap()
    wffn_d = nc.dram_tensor("wffn", [D, D], F32, kind="ExternalInput").ap()
    wa_d = nc.dram_tensor("wa", [D, 128], F32, kind="ExternalInput").ap()
    wb_d = nc.dram_tensor("wb", [128, D], F32, kind="ExternalInput").ap()
    wr_d = nc.dram_tensor("wr", [D, 8], F32, kind="ExternalInput").ap()
    dlt_d = nc.dram_tensor("dlt", [P, 8], F32, kind="ExternalInput").ap()
    e8_d = nc.dram_tensor("e8", [8, 128], F32, kind="ExternalInput").ap()
    idn_d = nc.dram_tensor("idn", [P, P], F32, kind="ExternalInput").ap()

    out_d = nc.dram_tensor("out", [P, NCH, OWN], F32, kind="ExternalOutput").ap()
    qt_scr = nc.dram_tensor("qt_scr", [NH * P, OWN], BF16, kind="Internal").ap()
    kag_in = nc.dram_tensor("kag_in", [P, OWN], BF16, kind="Internal").ap()
    wp_scr = nc.dram_tensor("wp_scr", [D, D], BF16, kind="Internal").ap()
    wf_scr = nc.dram_tensor("wf_scr", [D, D], BF16, kind="Internal").ap()
    vag_in = nc.dram_tensor("vag_in", [OWN, P], BF16, kind="Internal").ap()
    kag_out = nc.dram_tensor("kag_out", [8 * P, OWN], BF16, kind="Internal").ap()
    vag_out = nc.dram_tensor("vag_out", [8 * OWN, P], BF16, kind="Internal").ap()
    x1_scr = nc.dram_tensor("x1_scr", [P, NCH, OWN], F32, kind="Internal").ap()

    dbg = {}
    if debug:
        dbg["kt"] = nc.dram_tensor("dbg_kt", [P, TOK], BF16, kind="ExternalOutput").ap()
        dbg["va"] = nc.dram_tensor("dbg_va", [P, B * NJT * 129], BF16, kind="ExternalOutput").ap()
        dbg["rows"] = nc.dram_tensor("dbg_rows", [2, TOK], BF16, kind="ExternalOutput").ap()
        if nphase >= 2:
            dbg["qt"] = nc.dram_tensor("dbg_qt", [NH * P, OWN], BF16, kind="ExternalOutput").ap()
            dbg["orows"] = nc.dram_tensor("dbg_orows", [2, OWN], BF16, kind="ExternalOutput").ap()
        if nphase >= 3:
            dbg["oall"] = nc.dram_tensor("dbg_oall", [P, NH, OWN], BF16, kind="ExternalOutput").ap()
        if nphase >= 4:
            dbg["x1"] = nc.dram_tensor("dbg_x1", [P, NCH, OWN], F32, kind="ExternalOutput").ap()

    st = ExitStack()
    with st:
        ec = st.enter_context

        def sbl(name, shape, dt):
            return ec(nc.sbuf_tensor(name, shape, dt, side="left"))

        ones_red = sbl("ones_red", [P, P], BF16)
        onesrow = sbl("onesrow", [1, TBW], BF16)
        negcol = sbl("negcol", [P, 1], BF16)
        oneB = sbl("oneB", [1, P], BF16)
        idn_f = sbl("idn_f", [P, P], F32)
        idn_bf = sbl("idn_bf", [P, P], BF16)
        dD = sbl("dD", [P, P], F32)
        dlt_sb = sbl("dlt_sb", [P, 8], F32)
        tmask = sbl("tmask", [P, 8, P], BF16)
        robf = sbl("robf", [1, OWN], BF16)
        mrobf = sbl("mrobf", [1, OWN], BF16)
        brow_sb = sbl("brow_sb", [1, D + 2 * HD], BF16)
        mu_t = sbl("mu_t", [1, 2, TBW], F32)
        msq_t = sbl("msq_t", [1, 2, TBW], F32)
        rf_t = sbl("rf_t", [1, 2, TBW], F32)
        scrap = sbl("scrap", [1, 8], F32)
        epscol = sbl("epscol", [P, 1], F32)

        sB = Ct(st, nc, "sB")
        sC = Ct(st, nc, "sC")
        sCs = Ct(st, nc, "sCs")
        sDBG = Ct(st, nc, "sDBG")
        sXB = [Ct(st, nc, f"sXB{i}") for i in range(2)]
        sXO = Ct(st, nc, "sXO")
        sWQ = [Ct(st, nc, f"sWQ{i}") for i in range(2)]
        sQO = [Ct(st, nc, f"sQO{i}") for i in range(2)]
        sQI = [Ct(st, nc, f"sQI{i}") for i in range(2)]
        sWP = [Ct(st, nc, f"sWP{i}") for i in range(2)]
        sWF = [Ct(st, nc, f"sWF{i}") for i in range(2)]
        sXOF = [Ct(st, nc, f"sXOF{i}") for i in range(2)]
        sAGB = Ct(st, nc, "sAGB")
        sPWI = [Ct(st, nc, f"sPWI{i}") for i in range(2)]
        sPWO = [Ct(st, nc, f"sPWO{i}") for i in range(2)]
        sCC = Ct(st, nc, "sCC")
        sKV = Ct(st, nc, "sKV")
        sX1O = [Ct(st, nc, f"sX1O{i}") for i in range(2)]
        sX1I = [Ct(st, nc, f"sX1I{i}") for i in range(3)]
        sX2O = [Ct(st, nc, f"sX2O{i}") for i in range(2)]
        pe1 = Ct(st, nc, "pe1")
        pe2 = Ct(st, nc, "pe2")
        act1 = Ct(st, nc, "act1")
        act2 = Ct(st, nc, "act2")
        dve1 = Ct(st, nc, "dve1")
        dve2 = Ct(st, nc, "dve2")
        dveE = Ct(st, nc, "dveE")
        allct = ([sC, sCs, sDBG, sXO, sAGB, sCC, sKV] + sX1I + sPWI + sPWO +
                 sXOF + sX2O +
                 sWQ + sQO + sQI +
                 sWP + sWF + sX1O + [pe1, pe2, act1, act2, dve1, dve2, dveE])

        def ed(e, ct, instr):
            ct.inc(instr)
            e.wait_ge(ct.h, ct.n)

        def dma(e, ct, out, in_):
            v = ct.inc(e.dma_start(out, in_), 16)
            return (ct, v)

        def wt(e, evt):
            e.wait_ge(evt[0].h, evt[1])

        block = ec(nc.Block())
        T = block.tensor
        V = block.vector
        A = block.scalar
        G = block.gpsimd
        SY = block.sync

        def entry(e):
            e.wait_ge(sB.h, sB.n)

        def barrier():
            vals = [(c, c.n) for c in allct]

            def bt(e):
                for c, v in vals:
                    e.wait_ge(c.h, v)
                sB.inc(e.ldweights(ones_red[:, 0:1]))

            def bv(e):
                for c, v in vals:
                    e.wait_ge(c.h, v)
                sB.inc(e.tensor_copy(scrap[0:1, 0:2], scrap[0:1, 2:4]))

            def ba(e):
                for c, v in vals:
                    e.wait_ge(c.h, v)
                sB.inc(e.activation(scrap[0:1, 4:6], scrap[0:1, 6:8], AF.Copy,
                                    bias=0.0, scale=1.0))
            T(bt)
            V(bv)
            A(ba)

        ev = {}

        def stats_math(ps, slot, rrow, murrow, ev_st, prev_mur, evout, W=TBW):
            def m_a1(e):
                e.wait_ge(pe1.h, ev_st)
                if prev_mur is not None:
                    e.wait_ge(dve2.h, prev_mur)
                e.activation(mu_t[0:1, slot, 0:W], ps[0:1, 0:W],
                             AF.Copy, bias=0.0, scale=1.0 / D)
                cp = e.activation(msq_t[0:1, slot, 0:W], ps[0:1, W:2 * W],
                                  AF.Copy, bias=0.0, scale=1.0 / D)
                evout["mu"] = act1.inc(cp)

            def m_v1(e):
                e.wait_ge(act1.h, evout["mu"])
                ed(e, dve2, e.tensor_mul(rf_t[0:1, slot, 0:W],
                                         mu_t[0:1, slot, 0:W],
                                         mu_t[0:1, slot, 0:W]))
                tt = e.tensor_sub(rf_t[0:1, slot, 0:W], msq_t[0:1, slot, 0:W],
                                  rf_t[0:1, slot, 0:W])
                evout["var"] = dve2.inc(tt)

            def m_a2(e):
                e.wait_ge(dve2.h, evout["var"])
                ed(e, act2, e.activation(rf_t[0:1, slot, 0:W],
                                         rf_t[0:1, slot, 0:W],
                                         AF.Ln, bias=epscol[0:1, 0:1], scale=1.0))
                ed(e, act2, e.activation(rf_t[0:1, slot, 0:W],
                                         rf_t[0:1, slot, 0:W],
                                         AF.Exp, bias=0.0, scale=-0.5))
                cp = e.activation(rrow, rf_t[0:1, slot, 0:W],
                                  AF.Copy, bias=0.0, scale=1.0)
                evout["r"] = act1.inc(cp)

            def m_v2(e):
                e.wait_ge(act1.h, evout["r"])
                tt = e.tensor_mul(murrow, mu_t[0:1, slot, 0:W],
                                  rf_t[0:1, slot, 0:W])
                evout["mur"] = dve2.inc(tt)
            A(m_a1)
            V(m_v1)
            A(m_a2)
            V(m_v2)

        def s_sy(e):
            dma(e, sCs, dlt_sb[:], dlt_d)
            dma(e, sCs, idn_f[:], idn_d)

        def s_g(e):
            dma(e, sC, brow_sb[:], brow_d)
            dma(e, sC, idn_bf[:], idn_d)
            dve2.inc(e.iota(dD[:], [[1, P]], channel_multiplier=-1,
                            allow_small_or_imprecise_dtypes=True))

        def s_v(e):
            e.memset(ones_red[:], 1.0)
            e.memset(onesrow[:], 1.0)
            e.memset(negcol[:], -1.0)
            e.memset(oneB[:], 1.0)
            e.memset(scrap[:], 1.0)
            e.memset(epscol[:], EPS)
            e.wait_ge(dve2.h, dve2.n)
            e.wait_ge(sCs.h, sCs.n)
            for r in range(8):
                dve1.inc(e.tensor_scalar(tmask[:, r, :], dD[:],
                                         dlt_sb[:, r:r + 1], 0.0,
                                         ALU.add, ALU.is_ge))
        SY(s_sy)
        G(s_g)
        V(s_v)
        barrier()

        es_o = ExitStack()
        if True:
            with ExitStack() as _es:
                kt_all = _es.enter_context(nc.sbuf_tensor("kt_all", [P, TOK], BF16, side="right"))
                v_all = _es.enter_context(nc.sbuf_tensor("v_all", [P, B * NJT, 129], BF16, side="right"))
                if nphase >= 2:
                    with ExitStack() as _es:
                        xo0 = _es.enter_context(nc.sbuf_tensor("xo0", [P, NCH, TBW], BF16, side="right"))
                        xo1 = _es.enter_context(nc.sbuf_tensor("xo1", [P, NCH, TBW], BF16, side="right"))
                        xo2 = _es.enter_context(nc.sbuf_tensor("xo2", [P, NCH, TBW], BF16, side="right"))
                        xo3 = _es.enter_context(nc.sbuf_tensor("xo3", [P, NCH, TBW], BF16, side="right"))
                        xsq = _es.enter_context(nc.sbuf_tensor("xsq", [P, NCH, TBW], BF16, side="right"))
                        wqc0 = _es.enter_context(nc.sbuf_tensor("wqc0", [P, NCH, P], BF16, side="right"))
                        wqc1 = _es.enter_context(nc.sbuf_tensor("wqc1", [P, NCH, P], BF16, side="right"))
                        rrown = _es.enter_context(nc.sbuf_tensor("rrown", [P, NOB, TBW], BF16, side="right"))
                        wnq = _es.enter_context(nc.sbuf_tensor("wnq", [1, P], BF16, side="right"))
                        qstage = _es.enter_context(nc.sbuf_tensor("qstage", [P, 2, TBW], BF16, side="right"))
                        wk_bf = _es.enter_context(nc.sbuf_tensor("wk_bf", [P, NCH, HD], BF16, side="right"))
                        wv_bf = _es.enter_context(nc.sbuf_tensor("wv_bf", [P, NCH, HD], BF16, side="right"))
                        wnkv = _es.enter_context(nc.sbuf_tensor("wnkv", [1, 2 * P], BF16, side="right"))
                        kstage = _es.enter_context(nc.sbuf_tensor("kstage", [P, NOB, TBW], BF16, side="right"))
                        vstage = _es.enter_context(nc.sbuf_tensor("vstage", [P, NCH, P], BF16, side="right"))
                        vtmp1 = _es.enter_context(nc.sbuf_tensor("vtmp1", [P, TBW], BF16, side="right"))
                        ps_q = _es.enter_context(nc.psum_tensor("ps_q", [P, 2 * TBW], F32))
                        ps_st1 = _es.enter_context(nc.psum_tensor("ps_st1", [1, 2 * TBW], F32))
                        ps_rr1 = _es.enter_context(nc.psum_tensor("ps_rr1", [P, TBW], F32))
                        ps_kv1 = _es.enter_context(nc.psum_tensor("ps_kv1", [P, 2 * TBW], F32))
                        ps_tr1 = _es.enter_context(nc.psum_tensor("ps_tr1", [P, P], BF16))
                        ps_w = ps_rr1
                        xol = [xo0, xo1, xo2, xo3]
                        wqc = [wqc0, wqc1]

                        def p1a_g(e):
                            entry(e)
                            for ob in range(NOB):
                                t0 = ob * TBW
                                dma(e, sXO, xol[ob][:],
                                    xt_own[0:D, t0:t0 + TBW].rearrange(
                                        "(c p) t -> p c t", p=P))
                            for ob in range(NOB):
                                ev[("xo", ob)] = (sXO, sXO.n)
                            dma(e, sC, wk_bf[:],
                                wk_d.rearrange("(c p) o -> p c o", p=P))
                            ev["wkv"] = dma(e, sC, wv_bf[:],
                                            wv_d.rearrange("(c p) o -> p c o", p=P))
                            ev[("wq", 0)] = dma(
                                e, sWQ[0], wqc[0][:],
                                wq_d[0:D, 0:P].rearrange("(c p) o -> p c o", p=P))
                        G(p1a_g)

                        ev[("wn_ev", 0)] = act2.pre()
                        ev[("wn_ev", 1)] = act2.pre()

                        def p01_wn_t(e):
                            entry(e)
                            wt(e, ev["wkv"])
                            for half in range(2):
                                if half == 1:
                                    e.wait_ge(act2.h, ev[("wn_ev", 0)])
                                w = wk_bf if half == 0 else wv_bf
                                for c in range(NCH):
                                    mm = e.matmul(ps_w[0:1, 0:P], negcol[:],
                                                  w[:, c, :], start=(c == 0),
                                                  stop=(c == NCH - 1))
                                ev[("wn", half)] = pe1.inc(mm)

                        def p01_wn_a(e):
                            entry(e)
                            for half in range(2):
                                e.wait_ge(pe1.h, ev[("wn", half)])
                                cp = e.activation(wnkv[0:1, half * P:(half + 1) * P],
                                                  ps_w[0:1, 0:P], AF.Copy,
                                                  bias=0.0, scale=1.0)
                                cp.then_inc(act2.h, 1)

                        def p01_ones_v(e):
                            entry(e)
                            dve2.inc(e.memset(v_all[:, :, 128:129], 1.0))
                        T(p01_wn_t)
                        A(p01_wn_a)
                        V(p01_ones_v)

                        for ob in range(NOB):
                            slot = ob % 2

                            def p1a_v1(e, ob=ob):
                                wt(e, ev[("xo", ob)])
                                if ob - 1 >= 0:
                                    e.wait_ge(pe1.h, ev[("ost", ob - 1)])
                                for c in range(NCH):
                                    tt = e.tensor_mul(xsq[:, c, :],
                                                      xol[ob][:, c, :],
                                                      xol[ob][:, c, :])
                                ev[("osq", ob)] = dve1.inc(tt)

                            def p1a_t1(e, ob=ob):
                                wt(e, ev[("xo", ob)])
                                if ob - 1 >= 0:
                                    e.wait_ge(act1.h,
                                              ev[("omath", ob - 1)]["mu"])
                                for c in range(NCH):
                                    e.matmul(ps_st1[0:1, 0:TBW], ones_red[:, 0:1],
                                             xol[ob][:, c, :], start=(c == 0),
                                             stop=(c == NCH - 1))
                                e.wait_ge(dve1.h, ev[("osq", ob)])
                                for c in range(NCH):
                                    mm = e.matmul(ps_st1[0:1, TBW:2 * TBW],
                                                  ones_red[:, 0:1], xsq[:, c, :],
                                                  start=(c == 0),
                                                  stop=(c == NCH - 1))
                                ev[("ost", ob)] = pe1.inc(mm)

                            V(p1a_v1)
                            T(p1a_t1)

                            ev[("omath", ob)] = {}
                            pm = ev[("omath", ob - 2)]["mur"] if ob >= 2 else None
                            t0ob = ob * TBW
                            stats_math(ps_st1, slot, robf[0:1, t0ob:t0ob + TBW],
                                       mrobf[0:1, t0ob:t0ob + TBW],
                                       ev[("ost", ob)], pm, ev[("omath", ob)])

                            def p1a_t2(e, ob=ob):
                                e.wait_ge(act1.h, ev[("omath", ob)]["r"])
                                if ob - 1 >= 0:
                                    e.wait_ge(act2.h, ev[("rrevo", ob - 1)])
                                else:
                                    e.wait_ge(act2.h, ev[("wn_ev", 1)])
                                t0 = ob * TBW
                                mm = e.matmul(ps_rr1[:, 0:TBW], oneB[0:1, :],
                                              robf[0:1, t0:t0 + TBW], start=True,
                                              stop=True)
                                ev[("rrmmo", ob)] = pe1.inc(mm)

                            def p1a_a2(e, ob=ob):
                                e.wait_ge(pe1.h, ev[("rrmmo", ob)])
                                cp = e.activation(rrown[:, ob, :], ps_rr1[:, 0:TBW],
                                                  AF.Copy, bias=0.0, scale=1.0)
                                ev[("rrevo", ob)] = act2.inc(cp)

                            T(p1a_t2)
                            A(p1a_a2)

                        for ob in range(NOB):
                            def p01_kv_t(e, ob=ob):
                                e.wait_ge(act2.h, ev[("rrevo", ob)])
                                if ob > 0:
                                    e.wait_ge(dve1.h, ev[("kvfix", ob - 1)])
                                t0 = ob * TBW
                                for half in range(2):
                                    w = wk_bf if half == 0 else wv_bf
                                    lo = half * TBW
                                    for c in range(NCH):
                                        e.matmul(ps_kv1[:, lo:lo + TBW],
                                                 w[:, c, :], xol[ob][:, c, :],
                                                 start=(c == 0), stop=False)
                                    boff = D + half * HD
                                    e.matmul(ps_kv1[:, lo:lo + TBW],
                                             brow_sb[0:1, boff:boff + P],
                                             onesrow[:], start=False, stop=False)
                                    mm = e.matmul(
                                        ps_kv1[:, lo:lo + TBW],
                                        wnkv[0:1, half * P:(half + 1) * P],
                                        mrobf[0:1, t0:t0 + TBW],
                                        start=False, stop=True)
                                ev[("kvmm", ob)] = pe2.inc(mm)

                            def p01_kv_v(e, ob=ob):
                                e.wait_ge(pe2.h, ev[("kvmm", ob)])
                                if ob > 0:
                                    e.wait_ge(pe2.h, ev[("vtr", ob - 1, 3)])
                                e.tensor_mul(kstage[:, ob, :],
                                             ps_kv1[:, 0:TBW], rrown[:, ob, :])
                                tt = e.tensor_mul(vtmp1[:],
                                                  ps_kv1[:, TBW:2 * TBW],
                                                  rrown[:, ob, :])
                                ev[("kvfix", ob)] = dve1.inc(tt)

                            for i in range(4):
                                ev[("vtrev", ob, i)] = act2.pre()

                            def p01_tr_t(e, ob=ob):
                                e.wait_ge(dve1.h, ev[("kvfix", ob)])
                                for i in range(4):
                                    prev = (ob, i - 1) if i > 0 else (ob - 1, 3)
                                    if ("vtrev",) + prev in ev:
                                        e.wait_ge(act2.h, ev[("vtrev",) + prev])
                                    tr = e.transpose(ps_tr1[:],
                                                     vtmp1[:, i * P:(i + 1) * P],
                                                     idn_bf[:])
                                    ev[("vtr", ob, i)] = pe2.inc(tr)

                            def p01_tr_a(e, ob=ob):
                                for i in range(4):
                                    e.wait_ge(pe2.h, ev[("vtr", ob, i)])
                                    cp = e.activation(
                                        vstage[:, ob * 4 + i, :], ps_tr1[:],
                                        AF.Copy, bias=0.0, scale=1.0)
                                    cp.then_inc(act2.h, 1)

                            T(p01_kv_t)
                            V(p01_kv_v)
                            T(p01_tr_t)
                            A(p01_tr_a)

                        def p01_ag_g(e):
                            e.wait_ge(dve1.h, ev[("kvfix", NOB - 1)])
                            dma(e, sAGB, kag_in,
                                kstage[:].rearrange("p a b -> p (a b)"))
                            e.wait_ge(act2.h, ev[("vtrev", NOB - 1, 3)])
                            dma(e, sAGB, vag_in.rearrange("(t p) c -> p t c", p=P),
                                vstage[:])
                            e.wait_ge(sAGB.h, sAGB.n)
                            cc = e.collective_compute(
                                "AllGather", mybir.AluOpType.bypass,
                                replica_groups=[list(range(8))],
                                ins=[kag_in.opt()], outs=[kag_out.opt()])
                            sCC.inc(cc)
                            cc = e.collective_compute(
                                "AllGather", mybir.AluOpType.bypass,
                                replica_groups=[list(range(8))],
                                ins=[vag_in.opt()], outs=[vag_out.opt()])
                            sCC.inc(cc)
                        G(p01_ag_g)

                        def p01_sc_sy(e):
                            e.wait_ge(sCC.h, sCC.n)
                            ktv = kt_all[:].rearrange(
                                "p (b t2 rr c) -> p b t2 rr c", b=B, t2=8, rr=8,
                                c=P)
                            vav = v_all[:].rearrange(
                                "p (b t2 rr) c -> p b t2 rr c", b=B, t2=8, rr=8)
                            for r in range(8):
                                src = kag_out[r * P:(r + 1) * P, :].rearrange(
                                    "p (b t c) -> p b t c", b=B, t=8, c=TBW // 4)
                                dma(e, sKV, ktv[:, :, :, r, :], src)
                                vsrc = vag_out[r * OWN:(r + 1) * OWN, :].rearrange(
                                    "(b t k) c -> k b t c", b=B, t=8, k=P)
                                dma(e, sKV, vav[:, :, :, r, 0:P], vsrc)
                        SY(p01_sc_sy)

                        for h in range(NH):
                            parh = h % 2

                            def p1b_g1(e, h=h):
                                if h + 1 < NH:
                                    if h - 1 >= 0:
                                        e.wait_ge(pe2.h,
                                                  ev[("qmm", h - 1, NOB - 1)])
                                    o0 = (h + 1) * P
                                    ev[("wq", h + 1)] = dma(
                                        e, sWQ[(h + 1) % 2], wqc[(h + 1) % 2][:],
                                        wq_d[0:D, o0:o0 + P].rearrange(
                                            "(c p) o -> p c o", p=P))

                            def p1b_t1(e, h=h, parh=parh):
                                wt(e, ev[("wq", h)])
                                if h == 0:
                                    e.wait_ge(dve2.h,
                                              ev[("omath", NOB - 1)]["mur"])
                                    e.wait_ge(act2.h, ev[("rrevo", NOB - 1)])
                                    e.wait_ge(pe2.h, ev[("kvmm", NOB - 1)])
                                if h > 0:
                                    e.wait_ge(act2.h, ev[("wnqev", h - 1)])
                                for c in range(NCH):
                                    mm = e.matmul(ps_w[0:1, 0:P], negcol[:],
                                                  wqc[parh][:, c, :],
                                                  start=(c == 0),
                                                  stop=(c == NCH - 1))
                                ev[("wnq", h)] = pe1.inc(mm)

                            def p1b_a1(e, h=h):
                                e.wait_ge(pe1.h, ev[("wnq", h)])
                                cp = e.activation(wnq[0:1, :], ps_w[0:1, 0:P],
                                                  AF.Copy, bias=0.0, scale=1.0)
                                ev[("wnqev", h)] = act2.inc(cp)

                            def p1b_t2(e, h=h, parh=parh):
                                e.wait_ge(act2.h, ev[("wnqev", h)])
                                for ob in range(NOB):
                                    lo = (ob % 2) * TBW
                                    pob = (h, ob - 1) if ob > 0 else (h - 1, NOB - 1)
                                    if ("qev",) + pob in ev:
                                        e.wait_ge(dve1.h, ev[("qev",) + pob])
                                    for c in range(NCH):
                                        e.matmul(ps_q[:, lo:lo + TBW],
                                                 wqc[parh][:, c, :],
                                                 xol[ob][:, c, :],
                                                 start=(c == 0), stop=False)
                                    boff = h * P
                                    e.matmul(ps_q[:, lo:lo + TBW],
                                             brow_sb[0:1, boff:boff + P],
                                             onesrow[:], start=False, stop=False)
                                    t0 = ob * TBW
                                    mm = e.matmul(ps_q[:, lo:lo + TBW], wnq[0:1, :],
                                                  mrobf[0:1, t0:t0 + TBW],
                                                  start=False, stop=True)
                                    ev[("qmm", h, ob)] = pe2.inc(mm)

                            def p1b_v1(e, h=h):
                                for ob in range(NOB):
                                    e.wait_ge(pe2.h, ev[("qmm", h, ob)])
                                    e.wait_ge(act2.h, ev[("rrevo", ob)])
                                    qp = ob % 2
                                    pob = (h, ob - 2) if ob >= 2 else (h - 1, ob + 2)
                                    if ("qdma",) + pob in ev:
                                        wt(e, ev[("qdma",) + pob])
                                    lo = qp * TBW
                                    tt = e.tensor_mul(qstage[:, qp, :],
                                                      ps_q[:, lo:lo + TBW],
                                                      rrown[:, ob, :])
                                    tt.then_inc(dve1.h, 1)

                            def p1b_sy(e, h=h):
                                for ob in range(NOB):
                                    e.wait_ge(dve1.h, ev[("qev", h, ob)])
                                    qp = ob % 2
                                    t0 = ob * TBW
                                    e.dma_start(
                                        qt_scr[h * P:(h + 1) * P, t0:t0 + TBW],
                                        qstage[:, qp, :]).then_inc(sQO[qp].h, 16)

                            G(p1b_g1)
                            T(p1b_t1)
                            A(p1b_a1)
                            for ob in range(NOB):
                                ev[("qev", h, ob)] = dve1.pre()
                                ct = sQO[ob % 2]
                                ev[("qdma", h, ob)] = (ct, ct.pre(16))
                            T(p1b_t2)
                            V(p1b_v1)
                            SY(p1b_sy)

                        pw_cols = [(wproj_d, wp_scr, oc) for oc in range(NCH)] + \
                                  [(wffn_d, wf_scr, oc) for oc in range(NCH)]

                        def pw_g(e):
                            for i, (src, dst, oc) in enumerate(pw_cols):
                                par = i % 2
                                if i >= 2:
                                    e.wait_ge(sPWO[par].h, 16 * (i // 2))
                                o0 = oc * P
                                stg = xsq[:, 4 * par:4 * par + 4, :].rearrange(
                                    "p a b -> p (a b)").rearrange(
                                    "p (c o) -> p c o", c=NCH)
                                e.dma_start(stg, src[0:D, o0:o0 + P].rearrange(
                                    "(c p) o -> p c o", p=P)).then_inc(
                                    sPWI[par].h, 16)
                                sPWI[par].n += 16

                        def pw_sy(e):
                            for i, (src, dst, oc) in enumerate(pw_cols):
                                par = i % 2
                                e.wait_ge(sPWI[par].h, 16 * (i // 2 + 1))
                                o0 = oc * P
                                stg = xsq[:, 4 * par:4 * par + 4, :].rearrange(
                                    "p a b -> p (a b)").rearrange(
                                    "p (c o) -> p c o", c=NCH)
                                e.dma_start(dst[0:D, o0:o0 + P].rearrange(
                                    "(c p) o -> p c o", p=P), stg).then_inc(
                                    sPWO[par].h, 16)
                                sPWO[par].n += 16
                        G(pw_g)
                        SY(pw_sy)

                        if debug:
                            def p1_dbg(e):
                                for c in allct:
                                    e.wait_ge(c.h, c.n)
                                dma(e, sDBG, dbg["kt"], kt_all[:])
                                dma(e, sDBG, dbg["va"],
                                    v_all[:].rearrange("p a b -> p (a b)"))
                                dma(e, sDBG, dbg["qt"], qt_scr)
                                dma(e, sDBG, dbg["orows"][0:1, :], robf[:])
                                dma(e, sDBG, dbg["orows"][1:2, :], mrobf[:])
                            SY(p1_dbg)
                        barrier()

                if nphase >= 3:
                    o_all = es_o.enter_context(
                        nc.sbuf_tensor("o_all", [P, NH, OWN], BF16, side="left"))
                    with ExitStack() as _es:
                        qbuf = _es.enter_context(nc.sbuf_tensor("qbuf", [P, 2, 1024], BF16, side="right"))
                        pbuf = _es.enter_context(nc.sbuf_tensor("pbuf", [P, 4, 512], BF16, side="right"))
                        rl_sb = _es.enter_context(nc.sbuf_tensor("rl_sb", [P, 1024], F32, side="right"))
                        ps_sA0 = _es.enter_context(nc.psum_tensor("ps_sA0", [P, 512], F32))
                        ps_sA1 = _es.enter_context(nc.psum_tensor("ps_sA1", [P, 512], F32))
                        ps_sB0 = _es.enter_context(nc.psum_tensor("ps_sB0", [P, 512], F32))
                        ps_sB1 = _es.enter_context(nc.psum_tensor("ps_sB1", [P, 512], F32))
                        ps_o = _es.enter_context(nc.psum_tensor("ps_o", [P, 1024], F32))
                        ps_l = _es.enter_context(nc.psum_tensor("ps_l", [P, 1024], F32))
                        ps_sc = [[ps_sA0, ps_sA1], [ps_sB0, ps_sB1]]
                        peS0, peL0 = pe1.n, pe2.n
                        actX0, dveT0, dveE0 = act1.n, dve1.n, dveE.n
                        sQI0 = [sQI[0].n, sQI[1].n]
                        units = []
                        for j in range(NJT):
                            units.append((1, j))
                            if j < 32:
                                units.append((0, j))
                        nU = len(units)
                        cseq = {}
                        ctxu = [[], []]
                        for u, (cx, j) in enumerate(units):
                            cseq[u] = len(ctxu[cx])
                            ctxu[cx].append(u)

                        def owner(j):
                            return 0 if j < 32 else 1

                        def loc0(cx, j):
                            m = j // 8
                            return 128 * m if cx == 0 else 128 * max(0, m - 4)

                        evS = lambda g: peS0 + g + 1
                        evL = lambda g: peL0 + g + 1
                        evX = lambda g: actX0 + g + 1
                        evT = lambda bh, j: dveT0 + bh * NJT + j + 1
                        evE = lambda bh: dveE0 + bh + 1
                        evQ = lambda bh: (sQI[bh % 2],
                                          sQI0[bh % 2] + 16 * (bh // 2 + 1))

                        def p2_sy(e):
                            entry(e)
                            for bh in range(NBH):
                                b, h = bh // NH, bh % NH
                                if bh >= 2:
                                    e.wait_ge(pe1.h, evS((bh - 1) * nU - 1))
                                t0 = b * 1024
                                evt = dma(e, sQI[bh % 2], qbuf[:, bh % 2, :],
                                          qt_scr[h * P:(h + 1) * P, t0:t0 + 1024])
                                assert evt == evQ(bh)
                        SY(p2_sy)

                        for bh in range(NBH):
                            b, h = bh // NH, bh % NH
                            g0 = bh * nU

                            def p2_t(e, bh=bh, b=b, h=h, g0=g0):
                                if bh == 0:
                                    entry(e)
                                qb = qbuf[:, bh % 2, :]

                                def scores(u):
                                    cx, j = units[u]
                                    c = cseq[u]
                                    l0 = loc0(cx, j)
                                    if u == 0:
                                        wt(e, evQ(bh))
                                    if c >= 2:
                                        e.wait_ge(act1.h,
                                                  evX(g0 + ctxu[cx][c - 2]))
                                    elif bh > 0:
                                        pu = ctxu[cx][len(ctxu[cx]) - 2 + c]
                                        e.wait_ge(act1.h, evX(g0 - nU + pu))
                                    k0 = b * S + j * P
                                    mm = e.matmul(ps_sc[cx][c % 2][:, l0:512],
                                                  kt_all[:, k0:k0 + P],
                                                  qb[:, 512 * cx + l0:
                                                     512 * cx + 512],
                                                  start=True, stop=True)
                                    assert pe1.inc(mm) == evS(g0 + u)

                                def pvl(u):
                                    cx, j = units[u]
                                    c = cseq[u]
                                    l0 = loc0(cx, j)
                                    gq0 = 512 * cx + l0
                                    if owner(j) == cx:
                                        e.wait_ge(dve1.h, evT(bh, j))
                                    else:
                                        e.wait_ge(act1.h, evX(g0 + u))
                                    if j == 0 and bh > 0:
                                        e.wait_ge(dveE.h, evE(bh - 1))
                                    pb = pbuf[:, u % 4, :]
                                    stp = (j == 31) if cx == 0 else (j == NJT - 1)
                                    e.matmul(ps_o[:, gq0:512 * cx + 512],
                                             v_all[:, b * NJT + j, 0:P],
                                             pb[:, l0:512], start=(j == 0),
                                             stop=stp)
                                    mm = e.matmul(ps_l[:, gq0:512 * cx + 512],
                                                  ones_red[:], pb[:, l0:512],
                                                  start=(j == 0), stop=stp)
                                    assert pe2.inc(mm) == evL(g0 + u)

                                scores(0)
                                scores(1)
                                scores(2)
                                for u in range(nU):
                                    pvl(u)
                                    if u + 3 < nU:
                                        scores(u + 3)

                            def p2_a(e, bh=bh, g0=g0):
                                if bh == 0:
                                    entry(e)
                                for u in range(nU):
                                    cx, j = units[u]
                                    c = cseq[u]
                                    l0 = loc0(cx, j)
                                    e.wait_ge(pe1.h, evS(g0 + u))
                                    if g0 + u >= 4:
                                        e.wait_ge(pe2.h, evL(g0 + u - 4))
                                    x = e.activation(pbuf[:, u % 4, l0:512],
                                                     ps_sc[cx][c % 2][:, l0:512],
                                                     AF.Exp, bias=0.0, scale=SCALE)
                                    assert act1.inc(x) == evX(g0 + u)

                            def p2_v(e, bh=bh, b=b, h=h, g0=g0):
                                if bh == 0:
                                    entry(e)
                                for j in range(NJT):
                                    cx = owner(j)
                                    u = ctxu[cx][cseq_of_owner[j]]
                                    l0 = loc0(cx, j)
                                    e.wait_ge(act1.h, evX(g0 + u))
                                    tt = e.tensor_mul(
                                        pbuf[:, u % 4, l0:l0 + P],
                                        pbuf[:, u % 4, l0:l0 + P],
                                        tmask[:, j % 8, :])
                                    assert dve1.inc(tt) == evT(bh, j)
                                e.wait_ge(pe2.h, evL(g0 + nU - 1))
                                dve2.inc(e.reciprocal(rl_sb[:], ps_l[:]))
                                e.wait_ge(dve2.h, dve2.n)
                                t0 = b * 1024
                                e.tensor_mul(o_all[:, h, t0:t0 + 512],
                                             ps_o[:, 0:512], rl_sb[:, 0:512])
                                tt = e.tensor_mul(o_all[:, h, t0 + 512:t0 + 1024],
                                                  ps_o[:, 512:1024],
                                                  rl_sb[:, 512:1024])
                                assert dveE.inc(tt) == evE(bh)

                            cseq_of_owner = {}
                            for j in range(NJT):
                                cx = owner(j)
                                for uu in ctxu[cx]:
                                    if units[uu][1] == j:
                                        cseq_of_owner[j] = cseq[uu]
                                        break

                            T(p2_t)
                            A(p2_a)
                            V(p2_v)

                        if debug:
                            def p2_dbg(e):
                                for c in allct:
                                    e.wait_ge(c.h, c.n)
                                dma(e, sDBG, dbg["oall"],
                                    o_all[:].rearrange("p a b -> p a b"))
                            SY(p2_dbg)
                        barrier()
            if nphase >= 4:
                with ExitStack() as _es:
                    z_ob = _es.enter_context(nc.sbuf_tensor("z_ob", [P, 2, NCH, TBW], BF16, side="right"))
                    bprow_sb = _es.enter_context(nc.sbuf_tensor("bprow_sb", [1, D], BF16, side="right"))
                    zsq = _es.enter_context(nc.sbuf_tensor("zsq", [P, NCH, TBW], BF16, side="right"))
                    xof = _es.enter_context(nc.sbuf_tensor("xof", [P, 2, NCH, TBW], BF16, side="right"))
                    x1st = _es.enter_context(nc.sbuf_tensor("x1st", [P, 2, TBW], F32, side="right"))
                    wpc0 = _es.enter_context(nc.sbuf_tensor("wpc0", [P, NCH, P], BF16, side="right"))
                    wpc1 = _es.enter_context(nc.sbuf_tensor("wpc1", [P, NCH, P], BF16, side="right"))
                    rr3 = _es.enter_context(nc.sbuf_tensor("rr3", [P, 2, TBW], BF16, side="right"))
                    rm3 = _es.enter_context(nc.sbuf_tensor("rm3", [P, 2, TBW], BF16, side="right"))
                    r3row = _es.enter_context(nc.sbuf_tensor("r3row", [1, 2, TBW], BF16, side="right"))
                    m3row = _es.enter_context(nc.sbuf_tensor("m3row", [1, 2, TBW], BF16, side="right"))
                    ttmp = _es.enter_context(nc.sbuf_tensor("ttmp", [P, TBW], F32, side="right"))
                    ps_z = _es.enter_context(nc.psum_tensor("ps_z", [P, 2 * TBW], F32))
                    ps_st3 = _es.enter_context(nc.psum_tensor("ps_st3", [1, 2 * TBW], F32))
                    ps_u3 = _es.enter_context(nc.psum_tensor("ps_u3", [P, TBW], F32))
                    wpc = [wpc0, wpc1]

                    def p3_head(ob):
                        par = ob % 2

                        def p3_sy1(e, ob=ob, par=par):
                            pass

                        def p3_g(e, ob=ob, par=par):
                            if ob == 0:
                                entry(e)
                                ev["bprow"] = dma(e, sC, bprow_sb[:], bprow_d)
                            if ob >= 2:
                                e.wait_ge(dve2.h, ev[("res3", ob - 2)])
                            t0 = ob * TBW
                            ev[("xof", ob)] = dma(
                                e, sXOF[par], xof[:, par, :, :],
                                xt_own[0:D, t0:t0 + TBW].rearrange(
                                    "(c p) t -> p c t", p=P))
                            pass

                        def p3_syw(e, ob=ob):
                            for oc in range(NCH):
                                pc = (ob, oc - 2) if oc >= 2 else (ob - 1, oc + 14)
                                if ("zmm",) + pc in ev:
                                    e.wait_ge(pe1.h, ev[("zmm",) + pc])
                                o0 = oc * P
                                ev[("wp", ob, oc)] = dma(
                                    e, sWP[oc % 2], wpc[oc % 2][:],
                                    wp_scr[0:D, o0:o0 + P].rearrange(
                                        "(c p) o -> p c o", p=P))

                        def p3_t1(e, ob=ob, par=par):
                            if ob == 0:
                                entry(e)
                                wt(e, ev["bprow"])
                            for oc in range(NCH):
                                wt(e, ev[("wp", ob, oc)])
                                pc = (ob, oc - 2) if oc >= 2 else (ob - 1, oc + 14)
                                if ("zev",) + pc in ev:
                                    e.wait_ge(act1.h, ev[("zev",) + pc])
                                lo = (oc % 2) * TBW
                                for c in range(NCH):
                                    e.matmul(ps_z[:, lo:lo + TBW],
                                             wpc[oc % 2][:, c, :],
                                             o_all[:, c, ob * TBW:(ob + 1) * TBW],
                                             start=(c == 0), stop=False)
                                o0 = oc * P
                                mm = e.matmul(ps_z[:, lo:lo + TBW],
                                              bprow_sb[0:1, o0:o0 + P], onesrow[:],
                                              start=False, stop=True)
                                mm.then_inc(pe1.h, 1)

                        def p3_a1(e, ob=ob, par=par):
                            if ob == 0:
                                entry(e)
                            for oc in range(NCH):
                                e.wait_ge(pe1.h, ev[("zmm", ob, oc)])
                                if ob >= 2:
                                    e.wait_ge(dve2.h, ev[("res3c", ob - 2, oc)])
                                if oc == 0 and ob >= 2:
                                    e.wait_ge(pe1.h, ev[("st3", ob - 2)])
                                lo = (oc % 2) * TBW
                                cp = e.activation(z_ob[:, par, oc, :],
                                                  ps_z[:, lo:lo + TBW], AF.Copy,
                                                  bias=0.0, scale=1.0)
                                cp.then_inc(act1.h, 1)

                        for oc in range(NCH):
                            ev[("zmm", ob, oc)] = pe1.pre()
                            ev[("zev", ob, oc)] = act1.pre()
                        for c in range(NCH):
                            ct = sX1O[c % 2]
                            ev[("x1dmac", ob, c)] = (ct, ct.pre(16))
                        G(p3_g)
                        SY(p3_syw)
                        T(p3_t1)
                        A(p3_a1)

                    def p3_tail_pre(ob):
                        par = ob % 2

                        def p3_v1(e, ob=ob, par=par):
                            e.wait_ge(act1.h, ev[("zev", ob, NCH - 1)])
                            if ob > 0:
                                e.wait_ge(pe1.h, ev[("st3", ob - 1)])
                            for c in range(NCH):
                                tt = e.tensor_mul(zsq[:, c, :],
                                                  z_ob[:, par, c, :],
                                                  z_ob[:, par, c, :])
                            ev[("zsq", ob)] = dve1.inc(tt)

                        def p3_t2(e, ob=ob, par=par):
                            e.wait_ge(act1.h, ev[("zev", ob, NCH - 1)])
                            if ob > 0:
                                e.wait_ge(act1.h, ev[("math3", ob - 1)]["mu"])
                            for c in range(NCH):
                                e.matmul(ps_st3[0:1, 0:TBW], ones_red[:, 0:1],
                                         z_ob[:, par, c, :], start=(c == 0),
                                         stop=(c == NCH - 1))
                            e.wait_ge(dve1.h, ev[("zsq", ob)])
                            for c in range(NCH):
                                mm = e.matmul(ps_st3[0:1, TBW:2 * TBW],
                                              ones_red[:, 0:1], zsq[:, c, :],
                                              start=(c == 0), stop=(c == NCH - 1))
                            ev[("st3", ob)] = pe1.inc(mm)
                        V(p3_v1)
                        T(p3_t2)
                        p3_tail_pre2(ob)

                    def p3_tail_pre2(ob):
                        par = ob % 2

                        ev[("math3", ob)] = {}
                        pm = ev[("math3", ob - 1)]["mur"] if ob >= 1 else None
                        stats_math(ps_st3, par, r3row[0:1, par, :],
                                   m3row[0:1, par, :], ev[("st3", ob)], pm,
                                   ev[("math3", ob)])
                        md = ev[("math3", ob)]

                        def p3_t3(e, ob=ob, par=par, md=md):
                            e.wait_ge(act1.h, md["r"])
                            if ob >= 2:
                                e.wait_ge(dve2.h, ev[("res3", ob - 2)])
                            mm = e.matmul(ps_u3[:, 0:TBW], oneB[0:1, :],
                                          r3row[0:1, par, :], start=True,
                                          stop=True)
                            ev[("rr3mm", ob)] = pe1.inc(mm)

                        def p3_a4(e, ob=ob, par=par):
                            e.wait_ge(pe1.h, ev[("rr3mm", ob)])
                            cp = e.activation(rr3[:, par, :], ps_u3[:, 0:TBW],
                                              AF.Copy, bias=0.0, scale=1.0)
                            ev[("rr3ev", ob)] = act2.inc(cp)

                        def p3_t4(e, ob=ob, par=par, md=md):
                            e.wait_ge(dve2.h, md["mur"])
                            e.wait_ge(act2.h, ev[("rr3ev", ob)])
                            mm = e.matmul(ps_u3[:, 0:TBW], oneB[0:1, :],
                                          m3row[0:1, par, :], start=True,
                                          stop=True)
                            ev[("rm3mm", ob)] = pe1.inc(mm)

                        def p3_a5(e, ob=ob, par=par):
                            e.wait_ge(pe1.h, ev[("rm3mm", ob)])
                            cp = e.activation(rm3[:, par, :], ps_u3[:, 0:TBW],
                                              AF.Copy, bias=0.0, scale=1.0)
                            ev[("rm3ev", ob)] = act2.inc(cp)

                        def p3_v4(e, ob=ob, par=par):
                            e.wait_ge(act2.h, ev[("rm3ev", ob)])
                            wt(e, ev[("xof", ob)])
                            for c in range(NCH):
                                pc = (ob, c - 2) if c >= 2 else (ob - 1, c + 14)
                                if ("x1dmac",) + pc in ev:
                                    wt(e, ev[("x1dmac",) + pc])
                                ed(e, dve2, e.tensor_mul(ttmp[:],
                                                         z_ob[:, par, c, :],
                                                         rr3[:, par, :]))
                                ed(e, dve2, e.scalar_tensor_tensor(
                                    ttmp[:], rm3[:, par, :], -1.0, ttmp[:],
                                    ALU.mult, ALU.add))
                                tt = e.tensor_add(x1st[:, c % 2, :], ttmp[:],
                                                  xof[:, par, c, :])
                                ev[("res3c", ob, c)] = dve2.inc(tt)
                                e.wait_ge(dve2.h, dve2.n)
                            ev[("res3", ob)] = ev[("res3c", ob, NCH - 1)]

                        def p3_sy2(e, ob=ob):
                            t0 = ob * TBW
                            for c in range(NCH):
                                e.wait_ge(dve2.h, ev[("res3c", ob, c)])
                                e.dma_start(
                                    x1_scr[:, c, t0:t0 + TBW],
                                    x1st[:, c % 2, :]).then_inc(sX1O[c % 2].h, 16)
                            ev[("x1dma", ob)] = ev[("x1dmac", ob, NCH - 1)]
                        T(p3_t3)
                        A(p3_a4)
                        T(p3_t4)
                        A(p3_a5)
                        p3_res_defs[ob] = (p3_v4, p3_sy2)

                    def p3_tail_res(ob):
                        v4, sy2 = p3_res_defs[ob]
                        V(v4)
                        G(sy2)

                    p3_res_defs = {}
                    for ob in range(NOB):
                        if ob >= 2:
                            p3_tail_res(ob - 2)
                        p3_head(ob)
                        if ob >= 1:
                            p3_tail_pre(ob - 1)
                    p3_tail_res(NOB - 2)
                    p3_tail_pre(NOB - 1)
                    p3_tail_res(NOB - 1)

                    if debug:
                        def p3_dbg(e):
                            for c in allct:
                                e.wait_ge(c.h, c.n)
                            dma(e, sDBG, dbg["x1"], x1_scr)
                        SY(p3_dbg)
                    barrier()
            if nphase >= 3:
                es_o.close()

            if nphase >= 5:
                with ExitStack() as _es:
                    x1b = _es.enter_context(nc.sbuf_tensor("x1b", [P, 3, NCH, TBW], BF16, side="right"))
                    sq4 = _es.enter_context(nc.sbuf_tensor("sq4", [P, NCH, TBW], BF16, side="right"))
                    ln2 = _es.enter_context(nc.sbuf_tensor("ln2", [P, 2, NCH, TBW], BF16, side="right"))
                    z2 = _es.enter_context(nc.sbuf_tensor("z2", [P, 2, NCH, TBW], BF16, side="right"))
                    x2st = _es.enter_context(nc.sbuf_tensor("x2st", [P, 2, TBW], F32, side="right"))
                    wfc0 = _es.enter_context(nc.sbuf_tensor("wfc0", [P, NCH, P], BF16, side="right"))
                    wfc1 = _es.enter_context(nc.sbuf_tensor("wfc1", [P, NCH, P], BF16, side="right"))
                    wa_bf = _es.enter_context(nc.sbuf_tensor("wa_bf", [P, NCH, P], BF16, side="right"))
                    wb_bf = _es.enter_context(nc.sbuf_tensor("wb_bf", [P, D], BF16, side="right"))
                    wr_bf = _es.enter_context(nc.sbuf_tensor("wr_bf", [P, NCH, 8], BF16, side="right"))
                    e8_bf = _es.enter_context(nc.sbuf_tensor("e8_bf", [8, P], BF16, side="right"))
                    idn_f4 = _es.enter_context(nc.sbuf_tensor("idn_f4", [P, P], F32, side="right"))
                    ax_sb = _es.enter_context(nc.sbuf_tensor("ax_sb", [P, TBW], BF16, side="right"))
                    sc_sb = _es.enter_context(nc.sbuf_tensor("sc_sb", [P, TBW], BF16, side="right"))
                    we8 = _es.enter_context(nc.sbuf_tensor("we8", [P, 4 * P + 64], F32, side="right"))
                    w8t = _es.enter_context(nc.sbuf_tensor("w8t", [8, TBW], BF16, side="right"))
                    tcol = _es.enter_context(nc.sbuf_tensor("tcol", [P, 8], F32, side="right"))
                    lg_sb = _es.enter_context(nc.sbuf_tensor("lg_sb", [P, 32], F32, side="right"))
                    rr4 = _es.enter_context(nc.sbuf_tensor("rr4", [P, 2, TBW], BF16, side="right"))
                    rm4 = _es.enter_context(nc.sbuf_tensor("rm4", [P, 2, TBW], BF16, side="right"))
                    r4row = _es.enter_context(nc.sbuf_tensor("r4row", [1, 2, TBW], BF16, side="right"))
                    m4row = _es.enter_context(nc.sbuf_tensor("m4row", [1, 2, TBW], BF16, side="right"))
                    tt4 = _es.enter_context(nc.sbuf_tensor("tt4", [P, TBW], F32, side="right"))
                    ps_z4 = _es.enter_context(nc.psum_tensor("ps_z4", [P, 2 * TBW], F32))
                    ps_u4 = _es.enter_context(nc.psum_tensor("ps_u4", [P, 2 * TBW], F32))
                    ps_lg = _es.enter_context(nc.psum_tensor("ps_lg", [P, 32], F32))
                    ps_w4 = _es.enter_context(nc.psum_tensor("ps_w4", [P, TBW], F32))
                    ps_st4 = _es.enter_context(nc.psum_tensor("ps_st4", [1, 2 * TBW], F32))
                    wfc = [wfc0, wfc1]

                    def p4_g0(e):
                        entry(e)
                        dma(e, sC, wa_bf[:], wa_d.rearrange("(c p) o -> p c o", p=P))
                        dma(e, sC, wb_bf[:], wb_d)
                        dma(e, sC, wr_bf[:], wr_d.rearrange("(c p) o -> p c o", p=P))
                        dma(e, sC, e8_bf[:], e8_d)
                        for k in ("wa", "wb", "wr", "e8"):
                            ev[k] = (sC, sC.n)

                    def p4_sy0(e):
                        entry(e)
                        ev["idn4"] = dma(e, sCs, idn_f4[:], idn_d)

                    def p4_v_init(e):
                        entry(e)
                        dve2.inc(e.memset(we8[:], 0.0))
                    G(p4_g0)
                    SY(p4_sy0)
                    V(p4_v_init)

                    def p4_g1(ob):
                        def g1(e, ob=ob):
                            if ob >= 3:
                                e.wait_ge(dve2.h, ev[("res4", ob - 3)])
                            t0 = ob * TBW
                            ev[("x1l", ob)] = dma(e, sX1I[ob % 3],
                                                  x1b[:, ob % 3, :, :],
                                                  x1_scr[:, :, t0:t0 + TBW])
                            if ob == 0:
                                wt(e, ev["wr"])
                        G(g1)

                    def p4_part_a(ob):
                        par = ob % 2

                        def p4_v0(e, ob=ob, par=par):
                            if ob == 0:
                                entry(e)
                            wt(e, ev[("x1l", ob)])
                            if ob > 0:
                                e.wait_ge(pe1.h, ev[("st4", ob - 1)])
                            for c in range(NCH):
                                tt = e.tensor_mul(sq4[:, c, :],
                                                  x1b[:, ob % 3, c, :],
                                                  x1b[:, ob % 3, c, :])
                            ev[("x1sq", ob)] = dve1.inc(tt)

                        def p4_t0(e, ob=ob, par=par):
                            if ob == 0:
                                entry(e)
                            wt(e, ev[("x1l", ob)])
                            if ob == 1:
                                e.wait_ge(act1.h, ev[("math4", 0)]["mu"])
                            elif ob >= 2:
                                e.wait_ge(act1.h, ev[("math4b", ob - 2)]["mu"])
                            for c in range(NCH):
                                e.matmul(ps_st4[0:1, 0:TBW], ones_red[:, 0:1],
                                         x1b[:, ob % 3, c, :], start=(c == 0),
                                         stop=(c == NCH - 1))
                            e.wait_ge(dve1.h, ev[("x1sq", ob)])
                            for c in range(NCH):
                                mm = e.matmul(ps_st4[0:1, TBW:2 * TBW],
                                              ones_red[:, 0:1], sq4[:, c, :],
                                              start=(c == 0), stop=(c == NCH - 1))
                            ev[("st4", ob)] = pe1.inc(mm)

                        V(p4_v0)
                        T(p4_t0)

                        ev[("math4", ob)] = {}
                        pm = (ev[("math4b", ob - 2)]["mur"] if ob >= 2
                              else (ev[("math4", 0)]["mur"] if ob == 1 else None))
                        stats_math(ps_st4, par, r4row[0:1, par, :],
                                   m4row[0:1, par, :], ev[("st4", ob)], pm,
                                   ev[("math4", ob)])
                        md4 = ev[("math4", ob)]

                        def p4_t1(e, ob=ob, par=par, md=md4):
                            e.wait_ge(act1.h, md["r"])
                            if ob > 0:
                                e.wait_ge(act1.h, ev[("w8tev", ob - 1, 3)])
                            if ob >= 2:
                                e.wait_ge(dve2.h, ev[("res4", ob - 2)])
                            mm = e.matmul(ps_w4[:, 0:TBW], oneB[0:1, :],
                                          r4row[0:1, par, :], start=True,
                                          stop=True)
                            ev[("rr4mm", ob)] = pe1.inc(mm)

                        def p4_a3(e, ob=ob, par=par):
                            e.wait_ge(pe1.h, ev[("rr4mm", ob)])
                            cp = e.activation(rr4[:, par, :], ps_w4[:, 0:TBW],
                                              AF.Copy, bias=0.0, scale=1.0)
                            ev[("rr4ev", ob)] = act2.inc(cp)

                        def p4_t2(e, ob=ob, par=par, md=md4):
                            e.wait_ge(dve2.h, md["mur"])
                            e.wait_ge(act2.h, ev[("rr4ev", ob)])
                            mm = e.matmul(ps_w4[:, 0:TBW], oneB[0:1, :],
                                          m4row[0:1, par, :], start=True,
                                          stop=True)
                            ev[("rm4mm", ob)] = pe1.inc(mm)

                        def p4_a4(e, ob=ob, par=par):
                            e.wait_ge(pe1.h, ev[("rm4mm", ob)])
                            cp = e.activation(rm4[:, par, :], ps_w4[:, 0:TBW],
                                              AF.Copy, bias=0.0, scale=1.0)
                            ev[("rm4ev", ob)] = act2.inc(cp)

                        def p4_v3(e, ob=ob, par=par):
                            e.wait_ge(act2.h, ev[("rm4ev", ob)])
                            if ob >= 2:
                                e.wait_ge(pe1.h, ev[("z4mm", ob - 2, NCH - 1)])
                            for c in range(NCH):
                                ed(e, dve2, e.tensor_mul(ln2[:, par, c, :],
                                                         x1b[:, ob % 3, c, :],
                                                         rr4[:, par, :]))
                                tt = e.scalar_tensor_tensor(
                                    ln2[:, par, c, :], rm4[:, par, :], -1.0,
                                    ln2[:, par, c, :], ALU.mult, ALU.add)
                                if c < NCH - 1:
                                    ed(e, dve2, tt)
                            ev[("ln2", ob)] = dve1.inc(tt)
                        T(p4_t1)
                        A(p4_a3)
                        T(p4_t2)
                        A(p4_a4)
                        V(p4_v3)

                    def p4_part_b(ob):
                        par = ob % 2
                        for si in range(4):
                            ev[("lgev", ob, si)] = act2.pre()

                        def p4_t3(e, ob=ob, par=par):
                            e.wait_ge(dve1.h, ev[("ln2", ob)])
                            if ob == 0:
                                wt(e, ev["wa"])
                            else:
                                e.wait_ge(dve1.h, ev[("sc", ob - 1)])
                                e.wait_ge(dve2.h, ev[("top2", ob - 1, 3)])
                            for c in range(NCH):
                                mm = e.matmul(ps_u4[:, 0:TBW], wa_bf[:, c, :],
                                              ln2[:, par, c, :], start=(c == 0),
                                              stop=(c == NCH - 1))
                            ev[("axmm", ob)] = pe1.inc(mm)
                            for si in range(4):
                                if si > 0:
                                    e.wait_ge(act2.h, ev[("lgev", ob, si - 1)])
                                elif ob > 0:
                                    e.wait_ge(act2.h, ev[("lgev", ob - 1, 3)])
                                for c in range(NCH):
                                    mm = e.matmul(ps_lg[:, si * 8:(si + 1) * 8],
                                                  ln2[:, par, c,
                                                      si * P:(si + 1) * P],
                                                  wr_bf[:, c, :], start=(c == 0),
                                                  stop=(c == NCH - 1))
                                ev[("lg", ob, si)] = pe1.inc(mm)

                        def p4_a5b(e, ob=ob):
                            for si in range(4):
                                e.wait_ge(pe1.h, ev[("lg", ob, si)])
                                if si == 0 and ob > 0:
                                    e.wait_ge(dve2.h, ev[("lgdone", ob - 1, 3)])
                                cp = e.activation(lg_sb[:, si * 8:(si + 1) * 8],
                                                  ps_lg[:, si * 8:(si + 1) * 8],
                                                  AF.Copy, bias=0.0, scale=1.0)
                                cp.then_inc(act2.h, 1)

                        def p4_a5(e, ob=ob):
                            e.wait_ge(pe1.h, ev[("axmm", ob)])
                            if ob > 0:
                                e.wait_ge(dve1.h, ev[("sc", ob - 1)])
                            cp = e.activation(ax_sb[:], ps_u4[:, 0:TBW], AF.Copy,
                                              bias=0.0, scale=1.0)
                            ev[("axev", ob)] = act1.inc(cp)
                        T(p4_t3)
                        A(p4_a5b)
                        A(p4_a5)

                        for si in range(4):
                            ev[("eexp", ob, si)] = act2.pre()

                        def p4_v4(e, ob=ob):
                            for si in range(4):
                                e.wait_ge(act2.h, ev[("lgev", ob, si)])
                                lg = lg_sb[:, si * 8:(si + 1) * 8]
                                m1 = tcol[:, 0:1]
                                m2 = tcol[:, 1:2]
                                dm = tcol[:, 2:3]
                                w1 = tcol[:, 3:4]
                                w2 = tcol[:, 4:5]
                                eecol = tcol[:, 5:6]
                                eq1 = we8[:, 512:520]
                                lgm = we8[:, 520:528]
                                eq2 = we8[:, 528:536]
                                ed(e, dve2, e.reduce_max(m1, lg, axis=AX))
                                ed(e, dve2, e.tensor_scalar(eq1, lg, m1, None,
                                                            ALU.is_equal))
                                ed(e, dve2, e.scalar_tensor_tensor(
                                    lgm, eq1, -1e30, lg, ALU.mult, ALU.add))
                                ed(e, dve2, e.reduce_max(m2, lgm, axis=AX))
                                ed(e, dve2, e.tensor_scalar(eq2, lgm, m2, None,
                                                            ALU.is_equal))
                                tt = e.tensor_sub(dm, m2, m1)
                                ev[("dm", ob, si)] = dve2.inc(tt)
                                ev[("lgdone", ob, si)] = ev[("dm", ob, si)]
                                e.wait_ge(act2.h, ev[("eexp", ob, si)])
                                ed(e, dve2, e.tensor_scalar(w1, eecol, 1.0, None,
                                                            ALU.add))
                                ed(e, dve2, e.reciprocal(w1, w1))
                                ed(e, dve2, e.tensor_scalar(w1, w1, 2.0, None,
                                                            ALU.mult))
                                ed(e, dve2, e.tensor_scalar(w2, w1, -1.0, 2.0,
                                                            ALU.mult, ALU.add))
                                ed(e, dve2, e.tensor_scalar(
                                    we8[:, si * P:si * P + 8], eq1, w1, None,
                                    ALU.mult))
                                tt = e.scalar_tensor_tensor(
                                    we8[:, si * P:si * P + 8], eq2, w2,
                                    we8[:, si * P:si * P + 8], ALU.mult, ALU.add)
                                ev[("top2", ob, si)] = dve2.inc(tt)

                        def p4_a6(e, ob=ob):
                            for si in range(4):
                                e.wait_ge(dve2.h, ev[("dm", ob, si)])
                                if si > 0:
                                    e.wait_ge(dve2.h, ev[("top2", ob, si - 1)])
                                elif ob > 0:
                                    e.wait_ge(dve2.h, ev[("top2", ob - 1, 3)])
                                x = e.activation(tcol[:, 5:6], tcol[:, 2:3],
                                                 AF.Exp, bias=0.0, scale=1.0)
                                x.then_inc(act2.h, 1)
                        V(p4_v4)
                        A(p4_a6)

                        for si in range(4):
                            ev[("w8tev", ob, si)] = act1.pre()

                        def p4_t4(e, ob=ob):
                            for si in range(4):
                                e.wait_ge(dve2.h, ev[("top2", ob, si)])
                                if si == 0:
                                    e.wait_ge(act2.h, ev[("rm4ev", ob)])
                                    wt(e, ev["idn4"])
                                    if ob > 0:
                                        e.wait_ge(act1.h,
                                                  ev[("w8tev", ob - 1, 3)])
                                else:
                                    e.wait_ge(act1.h, ev[("w8tev", ob, si - 1)])
                                tr = e.transpose(ps_w4[:, 0:P],
                                                 we8[:, si * P:(si + 1) * P],
                                                 idn_f4[:])
                                ev[("w8tr", ob, si)] = pe2.inc(tr)

                        def p4_a7(e, ob=ob):
                            for si in range(4):
                                e.wait_ge(pe2.h, ev[("w8tr", ob, si)])
                                cp = e.activation(w8t[0:8, si * P:(si + 1) * P],
                                                  ps_w4[0:8, 0:P], AF.Copy,
                                                  bias=0.0, scale=1.0)
                                cp.then_inc(act1.h, 1)

                        def p4_t5(e, ob=ob):
                            e.wait_ge(act1.h, ev[("w8tev", ob, 3)])
                            if ob == 0:
                                wt(e, ev["e8"])
                            mm = e.matmul(ps_u4[:, TBW:2 * TBW], e8_bf[:], w8t[:],
                                          start=True, stop=True)
                            ev[("wxmm", ob)] = pe2.inc(mm)

                        def p4_v5(e, ob=ob):
                            e.wait_ge(pe2.h, ev[("wxmm", ob)])
                            e.wait_ge(act1.h, ev[("axev", ob)])
                            tt = e.tensor_mul(sc_sb[:], ax_sb[:],
                                              ps_u4[:, TBW:2 * TBW])
                            ev[("sc", ob)] = dve1.inc(tt)
                        T(p4_t4)
                        A(p4_a7)
                        T(p4_t5)
                        V(p4_v5)

                    def p4_part_base(ob):
                        par = ob % 2
                        for oc in range(NCH):
                            ev[("z4mm", ob, oc)] = pe1.pre()
                            ev[("z4ev", ob, oc)] = act1.pre()

                        def p4_g1b(e, ob=ob):
                            for oc in range(NCH):
                                pc = (ob, oc - 2) if oc >= 2 else (ob - 1, oc + 14)
                                if ("z4mm",) + pc in ev:
                                    e.wait_ge(pe1.h, ev[("z4mm",) + pc])
                                o0 = oc * P
                                ev[("wf", ob, oc)] = dma(
                                    e, sWF[oc % 2], wfc[oc % 2][:],
                                    wf_scr[0:D, o0:o0 + P].rearrange(
                                        "(c p) o -> p c o", p=P))
                        SY(p4_g1b)

                        def p4_t6(e, ob=ob, par=par):
                            e.wait_ge(dve1.h, ev[("sc", ob)])
                            if ob == 0:
                                wt(e, ev["wb"])
                            for oc in range(NCH):
                                wt(e, ev[("wf", ob, oc)])
                                pc = (ob, oc - 2) if oc >= 2 else (ob - 1, oc + 14)
                                if ("z4ev",) + pc in ev:
                                    e.wait_ge(act1.h, ev[("z4ev",) + pc])
                                lo = (oc % 2) * TBW
                                for c in range(NCH):
                                    e.matmul(ps_z4[:, lo:lo + TBW],
                                             wfc[oc % 2][:, c, :],
                                             ln2[:, par, c, :],
                                             start=(c == 0), stop=False)
                                o0 = oc * P
                                mm = e.matmul(ps_z4[:, lo:lo + TBW],
                                              wb_bf[:, o0:o0 + P], sc_sb[:],
                                              start=False, stop=True)
                                mm.then_inc(pe1.h, 1)

                        def p4_a8(e, ob=ob, par=par):
                            for oc in range(NCH):
                                e.wait_ge(pe1.h, ev[("z4mm", ob, oc)])
                                if oc == 0 and ob >= 2:
                                    e.wait_ge(dve2.h, ev[("res4", ob - 2)])
                                    e.wait_ge(pe1.h, ev[("st4b", ob - 2)])
                                lo = (oc % 2) * TBW
                                cp = e.activation(z2[:, par, oc, :],
                                                  ps_z4[:, lo:lo + TBW], AF.Copy,
                                                  bias=0.0, scale=1.0)
                                cp.then_inc(act1.h, 1)
                        T(p4_t6)
                        A(p4_a8)

                    def p4_tail(ob):
                        par = ob % 2

                        def p4_v6(e, ob=ob, par=par):
                            e.wait_ge(act1.h, ev[("z4ev", ob, NCH - 1)])
                            if ob > 0:
                                e.wait_ge(pe1.h, ev[("st4b", ob - 1)])
                            for c in range(NCH):
                                tt = e.tensor_mul(sq4[:, c, :], z2[:, par, c, :],
                                                  z2[:, par, c, :])
                            ev[("z4sq", ob)] = dve1.inc(tt)

                        def p4_t7(e, ob=ob, par=par):
                            e.wait_ge(act1.h, ev[("z4ev", ob, NCH - 1)])
                            if ob + 1 < NOB:
                                e.wait_ge(act1.h, ev[("math4", ob + 1)]["mu"])
                            elif ob > 0:
                                e.wait_ge(act1.h, ev[("math4b", ob - 1)]["mu"])
                            for c in range(NCH):
                                e.matmul(ps_st4[0:1, 0:TBW], ones_red[:, 0:1],
                                         z2[:, par, c, :], start=(c == 0),
                                         stop=(c == NCH - 1))
                            e.wait_ge(dve1.h, ev[("z4sq", ob)])
                            for c in range(NCH):
                                mm = e.matmul(ps_st4[0:1, TBW:2 * TBW],
                                              ones_red[:, 0:1], sq4[:, c, :],
                                              start=(c == 0), stop=(c == NCH - 1))
                            ev[("st4b", ob)] = pe1.inc(mm)
                        V(p4_v6)
                        T(p4_t7)

                        ev[("math4b", ob)] = {}
                        pmb = (ev[("math4b", ob - 1)]["mur"] if ob >= 1
                               else ev[("math4", 0)]["mur"])
                        stats_math(ps_st4, par, r4row[0:1, par, :],
                                   m4row[0:1, par, :], ev[("st4b", ob)], pmb,
                                   ev[("math4b", ob)])
                        md4b = ev[("math4b", ob)]

                        def p4_t8(e, ob=ob, par=par, md=md4b):
                            e.wait_ge(act1.h, md["r"])
                            e.wait_ge(act1.h, ev[("w8tev", ob, 3)])
                            mm = e.matmul(ps_w4[:, 0:TBW], oneB[0:1, :],
                                          r4row[0:1, par, :], start=True,
                                          stop=True)
                            ev[("rr4bmm", ob)] = pe1.inc(mm)

                        def p4_a11(e, ob=ob, par=par):
                            e.wait_ge(pe1.h, ev[("rr4bmm", ob)])
                            cp = e.activation(rr4[:, par, :], ps_w4[:, 0:TBW],
                                              AF.Copy, bias=0.0, scale=1.0)
                            ev[("rr4bev", ob)] = act2.inc(cp)

                        def p4_t9(e, ob=ob, par=par, md=md4b):
                            e.wait_ge(dve2.h, md["mur"])
                            e.wait_ge(act2.h, ev[("rr4bev", ob)])
                            mm = e.matmul(ps_w4[:, 0:TBW], oneB[0:1, :],
                                          m4row[0:1, par, :], start=True,
                                          stop=True)
                            ev[("rm4bmm", ob)] = pe1.inc(mm)

                        def p4_a12(e, ob=ob, par=par):
                            e.wait_ge(pe1.h, ev[("rm4bmm", ob)])
                            cp = e.activation(rm4[:, par, :], ps_w4[:, 0:TBW],
                                              AF.Copy, bias=0.0, scale=1.0)
                            ev[("rm4bev", ob)] = act2.inc(cp)

                        def p4_v9(e, ob=ob, par=par):
                            e.wait_ge(act2.h, ev[("rm4bev", ob)])
                            for c in range(NCH):
                                pc = (ob, c - 2) if c >= 2 else (ob - 1, c + 14)
                                if ("x2dmac",) + pc in ev:
                                    wt(e, ev[("x2dmac",) + pc])
                                ed(e, dve2, e.tensor_mul(tt4[:], z2[:, par, c, :],
                                                         rr4[:, par, :]))
                                ed(e, dve2, e.scalar_tensor_tensor(
                                    tt4[:], rm4[:, par, :], -1.0, tt4[:],
                                    ALU.mult, ALU.add))
                                tt = e.tensor_add(x2st[:, c % 2, :], tt4[:],
                                                  x1b[:, ob % 3, c, :])
                                ev[("res4c", ob, c)] = dve2.inc(tt)
                                e.wait_ge(dve2.h, dve2.n)
                            ev[("res4", ob)] = ev[("res4c", ob, NCH - 1)]

                        def p4_sy2(e, ob=ob):
                            t0 = ob * TBW
                            for c in range(NCH):
                                e.wait_ge(dve2.h, ev[("res4c", ob, c)])
                                e.dma_start(
                                    out_d[:, c, t0:t0 + TBW],
                                    x2st[:, c % 2, :]).then_inc(sX2O[c % 2].h, 16)
                        T(p4_t8)
                        A(p4_a11)
                        T(p4_t9)
                        A(p4_a12)
                        p4_res_defs[ob] = (p4_v9, p4_sy2)

                    def p4_tail_res(ob):
                        v9, sy2 = p4_res_defs[ob]
                        V(v9)
                        G(sy2)

                    p4_res_defs = {}
                    p4_g1(0)
                    for ob in range(NOB):
                        for c in range(NCH):
                            ct = sX2O[c % 2]
                            ev[("x2dmac", ob, c)] = (ct, ct.pre(16))
                        p4_part_a(ob)
                        if ob >= 1:
                            p4_part_base(ob - 1)
                        p4_part_b(ob)
                        if ob + 1 < NOB:
                            p4_g1(ob + 1)
                        if ob >= 1:
                            p4_tail(ob - 1)
                            p4_tail_res(ob - 1)
                    p4_part_base(NOB - 1)
                    p4_tail(NOB - 1)
                    p4_tail_res(NOB - 1)

        def fin(e):
            for c in allct:
                e.wait_ge(c.h, c.n)
        SY(fin)
    return nc


def own_cols(c):
    cols = []
    for b in range(B):
        for t in range(8):
            a = b * S + (c + 8 * t) * P
            cols.extend(range(a, a + P))
    return np.array(cols)


def host_prep_shared(inputs):
    f = np.float32
    x = np.asarray(inputs["L0_attn_x"], f)
    wqkv = np.asarray(inputs["L0_attn_W_qkv"], f)
    sh = {}
    sh["xt_full"] = np.ascontiguousarray(x.reshape(TOK, D).T)
    sh["wq"] = np.ascontiguousarray(wqkv[:, 0:D])
    sh["wk"] = np.ascontiguousarray(wqkv[:, D:D + HD])
    sh["wv"] = np.ascontiguousarray(wqkv[:, D + HD:D + 2 * HD])
    sh["brow"] = np.asarray(inputs["L0_attn_b_qkv"], f).reshape(1, -1)
    sh["wproj"] = np.ascontiguousarray(np.asarray(inputs["L0_attn_W_proj"], f))
    sh["bprow"] = np.asarray(inputs["L0_attn_b_proj"], f).reshape(1, -1)
    sh["wffn"] = np.ascontiguousarray(np.asarray(inputs["L0_ffn_weight"], f).T)
    sh["wa"] = np.ascontiguousarray(
        np.asarray(inputs["L0_ffn_lora_A"], f).reshape(128, D).T)
    sh["wb"] = np.ascontiguousarray(
        np.asarray(inputs["L0_ffn_lora_B"], f).transpose(0, 2, 1).reshape(128, D))
    sh["wr"] = np.ascontiguousarray(np.asarray(inputs["L0_ffn_router_w"], f).T)
    e8m = np.zeros((8, 128), f)
    for e in range(8):
        e8m[e, e * 16:(e + 1) * 16] = 1.0
    sh["e8"] = e8m
    sh["idn"] = np.eye(P, dtype=f)
    return sh


def host_prep_core(sh, c):
    m = {k: v for k, v in sh.items() if k != "xt_full"}
    cols = own_cols(c)
    m["xt_own"] = np.ascontiguousarray(sh["xt_full"][:, cols])
    dltv = 128.0 * (c - np.arange(8, dtype=np.float32))
    m["dlt"] = np.ascontiguousarray(
        np.broadcast_to(dltv[None, :], (P, 8)).astype(np.float32))
    return m


def host_gather(results):
    full = np.empty((TOK, D), np.float32)
    for c, r in enumerate(results):
        o = np.asarray(r["out"])
        xt = o.transpose(1, 0, 2).reshape(D, OWN)
        full[own_cols(c), :] = xt.T
    return full.reshape(B, S, D)


def kernel(**inputs):
    from concourse.bass_utils import run_bass_kernel_spmd
    sh = host_prep_shared(inputs)
    in_maps = [host_prep_core(sh, c) for c in range(8)]
    nc = build(nphase=5, debug=False)
    res = run_bass_kernel_spmd(nc, in_maps, core_ids=list(range(8)))
    return host_gather(res.results)
